# version 01
# baseline (speedup 1.0000x reference)
import contextlib
import os
import numpy as np
import ml_dtypes
import concourse.bass as bass
import concourse.mybir as mybir
from concourse.bass_utils import run_bass_kernel_spmd

F32 = mybir.dt.float32
BF16 = mybir.dt.bfloat16
AF = mybir.ActivationFunctionType
ALU = mybir.AluOpType
AX = mybir.AxisListType.X
BF16NP = ml_dtypes.bfloat16

NCORES = 8
D = 1024
H = 16
HD = 64
OC = 8
T_OWN = 2048
HALO = 384
T_EXT = T_OWN + HALO
B = 2
TSEQ = 8192
EPS_A = 1e-6
DEPTH = 6
DI = 2816
DH = 2 * DI
DI_REAL = 2730
NEG_BIG = -1e30
RG = [[0, 1, 2, 3], [4, 5, 6, 7]]



def _perm_qk():
    p = np.zeros(D, np.int64)
    for c in range(OC):
        ha, hb = 2 * c, 2 * c + 1
        b0 = c * 128
        p[b0 + 0:b0 + 64] = ha * HD + np.arange(64)
        p[b0 + 64:b0 + 128] = hb * HD + np.arange(64)
    return p


PERM_QK = _perm_qk()


def _core_tables(core):
    half = np.arange(0, HD, 2, dtype=np.float32)
    inv_freq = 1.0 / 10000.0 ** (half / HD)
    pos0 = (core % 4) * T_OWN
    idx = np.clip(pos0 - HALO + np.arange(T_EXT), 0, TSEQ - 1).astype(np.float32)
    f = idx[:, None] * inv_freq[None, :]
    base_scale = (half + 0.4 * HD) / (1.4 * HD)
    power = (idx - TSEQ // 2) / 128.0
    sc = base_scale[None, :] ** power[:, None]
    cos, sin = np.cos(f), np.sin(f)
    inv = 1.0 / sc

    def build(ct, st):
        o = np.empty((128, T_EXT), np.float32)
        o[0:32] = ct.T; o[32:64] = ct.T; o[64:96] = ct.T; o[96:128] = ct.T
        s = np.empty((128, T_EXT), np.float32)
        s[0:32] = -st.T; s[32:64] = st.T; s[64:96] = -st.T; s[96:128] = st.T
        return o, s

    cq, sq = build(cos * sc, sin * sc)
    ck, sk = build(cos * inv, sin * inv)
    return cq, sq, ck, sk


def _fmcols(v):
    return np.ascontiguousarray(v.reshape(OC, 128).T)


def prep_inputs(inputs):
    x = np.ascontiguousarray(inputs["L0_attn_x"].astype(np.float32).reshape(B * TSEQ, D))
    shared = {
        "wqT": np.ascontiguousarray(inputs["L0_attn_Wq"].T),
        "wkT": np.ascontiguousarray(inputs["L0_attn_Wk"].T),
        "wvT": np.ascontiguousarray(inputs["L0_attn_Wv"].T),
        "woT": np.ascontiguousarray(inputs["L0_attn_Wo"].T),
        "bq128": _fmcols(inputs["L0_attn_bq"]),
        "bk128": _fmcols(inputs["L0_attn_bk"]),
        "bo128": _fmcols(inputs["L0_attn_bo"]),
        "bkrow": inputs["L0_attn_bk"].reshape(1, D),
        "bvrow": inputs["L0_attn_bv"].reshape(1, D),
    }
    wqkv = inputs["L0_ffn_Wqkv"]
    wq_b = wqkv[:, :, 0:D][:, :, PERM_QK]
    wk_b = wqkv[:, :, D:2 * D][:, :, PERM_QK]
    wv_b = wqkv[:, :, 2 * D:3 * D]
    shared["wqkv"] = np.ascontiguousarray(np.concatenate([wq_b, wk_b, wv_b], axis=2))
    shared["wo_b"] = np.ascontiguousarray(inputs["L0_ffn_Wo"])
    w1 = inputs["L0_ffn_W1"]; b1 = inputs["L0_ffn_b1"]
    w1p = np.zeros((DEPTH, D, DH), np.float32)
    w1p[:, :, 0:DI_REAL] = w1[:, :, 0:DI_REAL]
    w1p[:, :, DI:DI + DI_REAL] = w1[:, :, DI_REAL:2 * DI_REAL]
    shared["w1"] = np.ascontiguousarray(w1p)
    b1p = np.zeros((DEPTH, DH), np.float32)
    b1p[:, 0:DI_REAL] = b1[:, 0:DI_REAL]
    b1p[:, DI:DI + DI_REAL] = b1[:, DI_REAL:2 * DI_REAL]
    shared["b1c"] = np.ascontiguousarray(b1p.reshape(DEPTH, 44, 128).transpose(0, 2, 1))
    w2p = np.zeros((DEPTH, DI, D), np.float32)
    w2p[:, 0:DI_REAL, :] = inputs["L0_ffn_W2"]
    shared["w2"] = w2p
    shared["ln_g"] = np.stack([_fmcols(inputs["L0_ffn_ln_g"][i]) for i in range(DEPTH)])
    shared["ln_b"] = np.stack([_fmcols(inputs["L0_ffn_ln_b"][i]) for i in range(DEPTH)])
    shared["b2c"] = np.stack([_fmcols(inputs["L0_ffn_b2"][i]) for i in range(DEPTH)])
    src_rows = np.concatenate([np.arange(64), np.arange(64)])
    ha = np.zeros(128, np.float32); ha[0:64] = 1
    qsp = np.zeros((DEPTH, 2, 128), np.float32)
    ksp = np.zeros((DEPTH, 2, 128), np.float32)
    for i in range(DEPTH):
        qs = inputs["L0_ffn_q_scale"][i]; ks = inputs["L0_ffn_k_scale"][i]
        qsp[i, 0] = 8.0 * qs[src_rows] * ha; qsp[i, 1] = 8.0 * qs[src_rows] * (1 - ha)
        ksp[i, 0] = ks[src_rows] * ha; ksp[i, 1] = ks[src_rows] * (1 - ha)
    shared["qsp"] = qsp
    shared["ksp"] = ksp
    ii = np.arange(128)[:, None]; ss = np.arange(384)[None, :]
    mstd = np.where((ss >= ii) & (ss <= ii + 256), 0.0, NEG_BIG).astype(np.float32)
    shared["maskstd"] = mstd

    per_core = []
    for core in range(NCORES):
        s = core * T_OWN
        cq, sq, ck, sk = _core_tables(core)
        sel = np.zeros((128, 4), np.float32)
        sel[:, 0:core % 4] = 1.0
        selh = np.zeros((128, 4), np.float32)
        if core % 4 > 0:
            selh[:, core % 4 - 1] = 1.0
        inv = np.zeros((1, T_EXT), np.float32)
        if core % 4 == 0:
            inv[0, 0:HALO] = 1.0
        mfirst = np.stack([shared["maskstd"]] * 3)
        if core % 4 == 0:
            ii = np.arange(128)[:, None]; ss = np.arange(384)[None, :]
            for j in range(3):
                kb = 128 * j - 384 + ss
                mfirst[j] = np.where(kb < 0, NEG_BIG, mfirst[j])
        d = dict(shared)
        d["x_fm"] = np.ascontiguousarray(x[s:s + T_OWN].T)
        d["tab_cq"] = cq; d["tab_sq"] = sq; d["tab_ck"] = ck; d["tab_sk"] = sk
        d["sel"] = sel
        d["selh"] = selh
        d["invrow"] = inv
        d["maskfirst"] = np.ascontiguousarray(mfirst)
        per_core.append(d)
    return per_core



class Sem:
    __slots__ = ("h", "n")

    def __init__(self, h):
        self.h = h
        self.n = 0

    def inc(self, ins, k=1):
        if k == 1:
            ins.then_inc(self.h)
        else:
            ins.then_inc(self.h, k)
        self.n += k
        return self.n


class P:
    pass


def fm(ap_dram, c0, w, nchunk=OC):
    return ap_dram.rearrange("(c p) t -> p c t", p=128)[:, :, c0:c0 + w]


def sb3(ap_sb, w):
    return ap_sb.rearrange("p (c t) -> p c t", t=w)


def wview(w_dram):
    return w_dram.rearrange("(c p) o -> p c o", p=128)



def vec_(p, f):
    n = p.sems["ev"].n
    h = p.sems["ev"].h

    def g(ve):
        ve.wait_ge(h, n)
        f(ve)
    p.block.vector(g)


def scl_(p, f):
    n = p.sems["ac"].n
    h = p.sems["ac"].h

    def g(se):
        se.wait_ge(h, n)
        f(se)
    p.block.scalar(g)


def build(mode="A"):
    ctx = contextlib.ExitStack()
    import os
    drc = os.environ.get("MK_DETECT_RACES", "0") == "1"
    nc = bass.Bass("TRN2", target_bir_lowering=False, debug=False,
                   detect_race_conditions=drc)
    p = P()
    p.nc, p.ctx, p.mode = nc, ctx, mode

    din = lambda n, s, dt=F32: nc.dram_tensor(n, s, dt, kind="ExternalInput").ap()
    _dbg = os.environ.get("MK_DBG_DRAM", "0") == "1"
    dint = lambda n, s, dt=F32: nc.dram_tensor(
        n, s, dt, kind="ExternalOutput" if _dbg and not n.startswith("cc_") else "Internal").ap()

    p.x_fm = din("x_fm", [D, T_OWN])
    p.wqT = din("wqT", [D, D]); p.wkT = din("wkT", [D, D])
    p.wvT = din("wvT", [D, D]); p.woT = din("woT", [D, D])
    p.bq128 = din("bq128", [128, OC]); p.bk128 = din("bk128", [128, OC])
    p.bo128 = din("bo128", [128, OC])
    p.bkrow = din("bkrow", [1, D]); p.bvrow = din("bvrow", [1, D])
    p.sel = din("sel", [128, 4])
    p.invrow = din("invrow", [1, T_EXT])
    if mode != "A":
        p.wqkv = din("wqkv", [DEPTH, D, 3 * D])
        p.wo_b = din("wo_b", [DEPTH, D, D])
        p.w1 = din("w1", [DEPTH, D, DH])
        p.w2 = din("w2", [DEPTH, DI, D])
        p.ln_g = din("ln_g", [DEPTH, 128, OC]); p.ln_b = din("ln_b", [DEPTH, 128, OC])
        p.b2c = din("b2c", [DEPTH, 128, OC]); p.b1c = din("b1c", [DEPTH, 128, 44])
        p.qsp = din("qsp", [DEPTH, 2, 128]); p.ksp = din("ksp", [DEPTH, 2, 128])
        p.maskstd = din("maskstd", [128, 384])
        p.maskfirst = din("maskfirst", [3, 128, 384])
        p.selh = din("selh", [128, 4])
        p.tab_cq = din("tab_cq", [128, T_EXT]); p.tab_sq = din("tab_sq", [128, T_EXT])
        p.tab_ck = din("tab_ck", [128, T_EXT]); p.tab_sk = din("tab_sk", [128, T_EXT])

    p.out_fm = nc.dram_tensor("out_fm", [D, T_OWN], F32, kind="ExternalOutput").ap()

    p.Xs = dint("Xs", [D, T_EXT])
    p.x1s = dint("x1s", [D, T_OWN])
    p.cc_kv_in = dint("cc_kv_in", [1040, 64])
    p.cc_kv_out = dint("cc_kv_out", [4160, 64])
    if mode != "A":
        p.X2s = dint("X2s", [D, T_EXT])
        p.ms = dint("ms", [64, 128])
        p.gh = dint("gh", [DI, T_OWN], BF16)
        p.cc_x_in = dint("cc_x_in", [D, HALO])
        p.cc_x_out = dint("cc_x_out", [4 * D, HALO])

    p.c_ident = nc.inline_tensor(np.eye(128, dtype=BF16NP), "c_ident").ap()
    p.c_maskA = nc.inline_tensor(np.triu(np.ones((128, 128), np.float32)).astype(BF16NP), "c_maskA").ap()
    if mode != "A":
        p.c_maskB0 = nc.inline_tensor(np.tril(np.ones((128, 128), np.float32)).astype(BF16NP), "c_maskB0").ap()
        p.c_maskB2 = nc.inline_tensor(np.triu(np.ones((128, 128), np.float32)).astype(BF16NP), "c_maskB2").ap()

    p.actx = contextlib.ExitStack()
    sb = lambda n, s, dt=F32: ctx.enter_context(nc.sbuf_tensor(n, s, dt))
    sba = lambda n, s, dt=F32: p.actx.enter_context(nc.sbuf_tensor(n, s, dt))
    psa = lambda n, s, dt=F32: ctx.enter_context(nc.psum_tensor(n, s, dt))

    p.ps0 = psa("ps0", [128, 512]); p.ps1 = psa("ps1", [128, 512])
    p.ps2 = psa("ps2", [128, 512]); p.ps3 = psa("ps3", [128, 512])
    p.ps4 = psa("ps4", [128, 512]); p.ps5 = psa("ps5", [128, 512])
    p.ps6 = psa("ps6", [128, 512])
    p.ps7 = psa("ps7", [128, 512])

    p.wbuf = sb("wbuf", [128, OC * D], BF16)
    p.xb = sb("xb", [128, OC * 512], BF16)
    p.scr = sb("scr", [128, 3 * 512])
    p.brow = sb("brow", [1, 2 * D], BF16)
    p.bcolq = sb("bcolq", [128, OC]); p.bcolk = sb("bcolk", [128, OC])
    p.bcolo = sb("bcolo", [128, OC])
    p.onesb = sb("onesb", [128, 1], BF16)
    p.onesf = sb("onesf", [128, 1])
    p.onesrow_b = sb("onesrow_b", [1, 128], BF16)
    p.onesrow_f = sb("onesrow_f", [1, 128])
    p.selb = sb("selb", [128, 4])
    p.afm = sb("afm", [128, OC * 512], BF16)
    p.xres = sb("xres", [128, OC * 512])
    p.xsq = sb("xsq", [128, 512])
    p.mident = sb("mident", [128, 128], BF16)
    p.cvals = sb("cvals", [128, 2])
    nc.const_aps.register(F32, 1e-5, p.cvals[:, 0:1])
    nc.const_aps.register(F32, -0.5, p.cvals[:, 1:2])
    p.kvpre = p.scr[:, 0:1024]

    p.qA = sba("qA", [128, OC * T_OWN], BF16)
    p.kA = sba("kA", [128, OC * T_OWN], BF16)
    p.ktokA = sba("ktokA", [128, 16 * D], BF16)
    p.vtokA = sba("vtokA", [128, 16 * D], BF16)
    p.kvseg = sba("kvseg", [64, 512])
    p.kvst = sba("kvst", [128, 8 * 64])
    p.kvstb = sba("kvstb", [128, 8 * 64], BF16)
    p.ksst = sba("ksst", [128, 8]); p.ksstb = sba("ksstb", [128, 8], BF16)
    p.ksst2 = sba("ksst2", [128, 8])
    p.kspre = sba("kspre", [128, 32])
    p.smt = sba("smt", [128, 2 * 128], BF16)
    p.atok = sba("atok", [128, 4 * D], BF16)
    p.rden = sba("rden", [128, 16])
    p.mmaskA = sba("mmaskA", [128, 128], BF16)

    p.sems = {}
    for name in ["ld", "wx", "mm", "ev", "pl", "ac", "cc", "st", "dp", "xr", "fn", "tb", "lv", "w2"]:
        p.sems[name] = Sem(ctx.enter_context(nc.semaphore("s_" + name)))

    p.block = ctx.enter_context(nc.Block())
    return p



def emit_stage_a(p):
    blk = p.block
    ld, wx, mm, ev, pl, ac, cc, st, dp = (p.sems[k] for k in
        ["ld", "wx", "mm", "ev", "pl", "ac", "cc", "st", "dp"])

    def g_ld(gp):
        gp.dma_start(out=p.mident[:, :], in_=p.c_ident).then_inc(ld.h, 16); ld.n += 16
        gp.dma_start(out=p.mmaskA[:, :], in_=p.c_maskA).then_inc(ld.h, 16); ld.n += 16
        gp.dma_start(out=p.bcolq[:, :], in_=p.bq128).then_inc(ld.h, 16); ld.n += 16
        gp.dma_start(out=p.bcolk[:, :], in_=p.bk128).then_inc(ld.h, 16); ld.n += 16
        gp.dma_start(out=p.bcolo[:, :], in_=p.bo128).then_inc(ld.h, 16); ld.n += 16
        gp.dma_start(out=p.brow[:, 0:D], in_=p.bkrow).then_inc(ld.h, 16); ld.n += 16
        gp.dma_start(out=p.brow[:, D:2 * D], in_=p.bvrow).then_inc(ld.h, 16); ld.n += 16
        gp.dma_start(out=p.selb[:, :], in_=p.sel).then_inc(ld.h, 16); ld.n += 16
        gp.dma_start(out=sb3(p.wbuf[:, :], D), in_=wview(p.wqT)).then_inc(ld.h, 16); ld.n += 16
    blk.gpsimd(g_ld)
    ld_init = ld.n

    def v_ones(ve):
        ve.memset(p.cvals[:, 0:1], 1e-5)
        ve.memset(p.cvals[:, 1:2], -0.5)
        ve.memset(p.onesb[:, :], 1.0)
        ve.memset(p.onesf[:, :], 1.0)
        ve.memset(p.onesrow_b[:, :], 1.0)
        ev.inc(ve.memset(p.onesrow_f[:, :], 1.0))
    vec_(p, v_ones)
    ones_rdy = ev.n

    st8 = {"evlast": None, "aclast": None, "xfree": 0, "wrdy": ld_init}

    def emit_fm_pass(dst, bcol, has_phi=True):
        for ti in range(4):
            def s_x(se, ti=ti):
                se.wait_ge(mm.h, st8["xfree"])
                se.dma_start(out=sb3(p.xb[:, :], 512), in_=fm(p.x_fm, ti * 512, 512)).then_inc(wx.h, 16)
                wx.n += 16
            blk.gpsimd(s_x)
            xr = wx.n
            for oc in range(OC):
                def t_p(te, oc=oc, ti=ti, xr=xr, wrdy=st8["wrdy"], g=st8["evlast"]):
                    te.wait_ge(wx.h, xr)
                    if ti == 0 and oc == 0:
                        te.wait_ge(ld.h, wrdy)
                        te.wait_ge(ev.h, ones_rdy)
                    if g is not None:
                        te.wait_ge(ev.h, g)
                    for dc in range(OC):
                        ins = te.matmul(
                            p.ps0[:, 0:512],
                            p.wbuf[:, dc * D + oc * 128: dc * D + oc * 128 + 128],
                            p.xb[:, dc * 512:(dc + 1) * 512],
                            start=(dc == 0), stop=(dc == OC - 1))
                    ins.then_inc(mm.h); mm.n += 1
                blk.tensor(t_p)
                mmv = mm.n

                def v_p(ve, oc=oc, bcol=bcol, mmv=mmv, g=st8["aclast"]):
                    ve.wait_ge(mm.h, mmv)
                    if g is not None:
                        ve.wait_ge(ac.h, g)
                    ve.tensor_scalar(p.scr[:, 0:512], p.ps0[:, 0:512],
                                     bcol[:, oc:oc + 1], 0.0, ALU.add, ALU.max)
                    ev.inc(ve.tensor_scalar(p.scr[:, 512:1024], p.ps0[:, 0:512],
                                            bcol[:, oc:oc + 1], 0.0, ALU.add, ALU.min))
                vec_(p, v_p)
                st8["evlast"] = ev.n
                evv = ev.n

                def a_p(se, evv=evv):
                    se.wait_ge(ev.h, evv)
                    ac.inc(se.activation(p.scr[:, 1024:1536], p.scr[:, 512:1024], AF.Exp))
                scl_(p, a_p)
                st8["aclast"] = ac.n
                acv = ac.n

                def v_p2(ve, oc=oc, ti=ti, acv=acv, dst=dst):
                    ve.wait_ge(ac.h, acv)
                    ev.inc(ve.tensor_add(
                        dst[:, oc * T_OWN + ti * 512: oc * T_OWN + ti * 512 + 512],
                        p.scr[:, 0:512], p.scr[:, 1024:1536]))
                vec_(p, v_p2)
            st8["xfree"] = mm.n

    def emit_tok_pass(dst, brow_off, has_phi):
        for ti in range(4):
            def s_x(se, ti=ti):
                se.wait_ge(mm.h, st8["xfree"])
                se.dma_start(out=sb3(p.xb[:, :], 512), in_=fm(p.x_fm, ti * 512, 512)).then_inc(wx.h, 16)
                wx.n += 16
            blk.gpsimd(s_x)
            xr = wx.n
            for j in range(4):
                tt = ti * 4 + j
                for half in range(2):
                    def t_t(te, j=j, half=half, ti=ti, xr=xr, wrdy=st8["wrdy"], g=st8["evlast"]):
                        te.wait_ge(wx.h, xr)
                        if ti == 0 and j == 0 and half == 0:
                            te.wait_ge(ld.h, wrdy)
                        if g is not None:
                            te.wait_ge(ev.h, g)
                        for dc in range(OC):
                            te.matmul(
                                p.ps0[:, 0:512],
                                p.xb[:, dc * 512 + j * 128: dc * 512 + j * 128 + 128],
                                p.wbuf[:, dc * D + half * 512: dc * D + half * 512 + 512],
                                start=(dc == 0), stop=False,
                            )
                        te.matmul(
                            p.ps0[:, 0:512],
                            p.onesrow_b[0:1, :],
                            p.brow[0:1, brow_off + half * 512: brow_off + half * 512 + 512],
                            start=False, stop=True,
                        ).then_inc(mm.h); mm.n += 1
                    blk.tensor(t_t)
                    mmv = mm.n

                    if has_phi:
                        def v_t(ve, mmv=mmv, g=st8["aclast"]):
                            ve.wait_ge(mm.h, mmv)
                            if g is not None:
                                ve.wait_ge(ac.h, g)
                            ve.tensor_scalar(p.scr[:, 0:512], p.ps0[:, 0:512], 0.0, None, ALU.max)
                            ev.inc(ve.tensor_scalar(p.scr[:, 512:1024], p.ps0[:, 0:512], 0.0, None, ALU.min))
                        vec_(p, v_t)
                        st8["evlast"] = ev.n
                        evv = ev.n

                        def a_t(se, evv=evv):
                            se.wait_ge(ev.h, evv)
                            ac.inc(se.activation(p.scr[:, 1024:1536], p.scr[:, 512:1024], AF.Exp))
                        scl_(p, a_t)
                        st8["aclast"] = ac.n
                        acv = ac.n

                        def v_t2(ve, tt=tt, half=half, acv=acv, dst=dst):
                            ve.wait_ge(ac.h, acv)
                            ev.inc(ve.tensor_add(
                                dst[:, tt * D + half * 512: tt * D + half * 512 + 512],
                                p.scr[:, 0:512], p.scr[:, 1024:1536]))
                        vec_(p, v_t2)
                    else:
                        def v_t(ve, tt=tt, half=half, mmv=mmv, dst=dst):
                            ve.wait_ge(mm.h, mmv)
                            ev.inc(ve.tensor_copy(
                                dst[:, tt * D + half * 512: tt * D + half * 512 + 512],
                                p.ps0[:, 0:512]))
                        vec_(p, v_t)
                        st8["evlast"] = ev.n
            st8["xfree"] = mm.n

    def load_w(src):
        mm_before = mm.n

        def g_w(gp, mm_before=mm_before):
            gp.wait_ge(mm.h, mm_before)
            gp.dma_start(out=sb3(p.wbuf[:, :], D), in_=wview(src)).then_inc(ld.h, 16)
            ld.n += 16
        blk.gpsimd(g_w)
        st8["wrdy"] = ld.n

    emit_fm_pass(p.qA, p.bcolq)
    load_w(p.wkT)
    emit_fm_pass(p.kA, p.bcolk)
    emit_tok_pass(p.ktokA, 0, True)
    load_w(p.wvT)
    emit_tok_pass(p.vtokA, D, False)
    a1_done_ev = ev.n
    a1_done_mm = mm.n

    def t_seg(te):
        te.wait_ge(ev.h, a1_done_ev)
        for h in range(H):
            bank = p.ps4 if h < 8 else p.ps5
            col = (h % 8) * 64
            for j in range(16):
                ins = te.matmul(
                    bank[0:64, col:col + 64],
                    p.ktokA[:, j * D + h * HD: j * D + h * HD + 64],
                    p.vtokA[:, j * D + h * HD: j * D + h * HD + 64],
                    start=(j == 0), stop=(j == 15),
                )
        ins.then_inc(mm.h); mm.n += 1
    blk.tensor(t_seg)
    seg_mm = mm.n

    def v_seg(ve):
        ve.wait_ge(ev.h, a1_done_ev)
        for oc in range(OC):
            ve.reduce_sum(p.ksst[:, oc:oc + 1], p.kA[:, oc * T_OWN:(oc + 1) * T_OWN], axis=AX)
        ve.wait_ge(mm.h, seg_mm)
        ev.inc(ve.tensor_copy(p.kvseg[0:64, 0:512], p.ps4[0:64, 0:512]))
    vec_(p, v_seg)
    seg_ev1 = ev.n

    import os
    if os.environ.get("MK_CUT_CC", "0") == "1":
        def v_zero(ve):
            ve.memset(p.kvst[:, :], 0.0)
            ve.memset(p.ksst[:, :], 0.0)
            ve.memset(p.kvstb[:, :], 0.0)
            ev.inc(ve.memset(p.ksstb[:, :], 0.0))
        vec_(p, v_zero)
        state_rdy = ev.n
        if os.environ.get("MK_CUT_A2", "0") != "1":
            emit_a2(p, state_rdy, a1_done_mm)
        return

    def g_cc1(gp):
        gp.wait_ge(ev.h, seg_ev1)
        gp.dma_start(
            out=p.cc_kv_in[0:512, :].rearrange("(hh d) e -> d hh e", hh=8),
            in_=sb3(p.kvseg[0:64, :], 64)).then_inc(dp.h, 16); dp.n += 16
        gp.dma_start(
            out=p.cc_kv_in[1024:1040, :].rearrange("r (pp c) -> (r pp) c", c=8),
            in_=p.ksst[:, :]).then_inc(dp.h, 16); dp.n += 16
    blk.gpsimd(g_cc1)
    dma1 = dp.n

    def v_seg2(ve):
        ve.wait_ge(dp.h, dma1)
        ev.inc(ve.tensor_copy(p.kvseg[0:64, 0:512], p.ps5[0:64, 0:512]))
    vec_(p, v_seg2)
    seg_ev2 = ev.n

    def g_cc(gp):
        gp.wait_ge(ev.h, seg_ev2)
        gp.dma_start(
            out=p.cc_kv_in[512:1024, :].rearrange("(hh d) e -> d hh e", hh=8),
            in_=sb3(p.kvseg[0:64, :], 64)).then_inc(dp.h, 16); dp.n += 16
        gp.wait_ge(dp.h, dp.n)
        cc.inc(gp.collective_compute(
            "AllGather", ALU.bypass, replica_groups=RG,
            ins=[p.cc_kv_in.opt()], outs=[p.cc_kv_out.opt()]))
        gp.wait_ge(cc.h, cc.n)
        for r in range(4):
            b0 = r * 1040
            src2 = p.cc_kv_out[b0:b0 + 1024, :].rearrange("(j t d) e -> t d j e", j=8, t=2)
            if r >= 2:
                gp.wait_ge(ev.h, p._cc_stt[r - 2])
            sl = (r % 2) * 512
            gp.dma_start(out=sb3(p.kvpre[0:64, sl:sl + 512], 64), in_=src2[0]).then_inc(dp.h, 16); dp.n += 16
            gp.dma_start(out=sb3(p.kvpre[64:128, sl:sl + 512], 64), in_=src2[1]).then_inc(dp.h, 16); dp.n += 16
            gp.dma_start(out=p.kspre[:, r * 8:(r + 1) * 8],
                         in_=p.cc_kv_out[b0 + 1024:b0 + 1040, :].rearrange(
                             "r (pp c) -> (r pp) c", c=8)).then_inc(dp.h, 16); dp.n += 16
            gp.wait_ge(dp.h, dp.n)
            p._cc_dp[r] = dp.n
    p._cc_dp = {}
    p._cc_stt = {r: ev.n + 2 + 2 * (r + 1) for r in range(4)}
    blk.gpsimd(g_cc)

    def v_cc(ve):
        ev.inc(ve.memset(p.kvst[:, :], 0.0))
        ev.inc(ve.memset(p.ksst2[:, :], 0.0))
        for r in range(4):
            sl = (r % 2) * 512
            ve.wait_ge(dp.h, p._cc_dp[r])
            ve.wait_ge(ev.h, ev.n)
            ev.inc(ve.scalar_tensor_tensor(p.kvst[:, :], p.kvpre[:, sl:sl + 512], p.selb[:, r:r + 1],
                                           p.kvst[:, :], ALU.mult, ALU.add))
            ve.wait_ge(ev.h, ev.n)
            ev.inc(ve.scalar_tensor_tensor(p.ksst2[:, :], p.kspre[:, r * 8:(r + 1) * 8],
                                           p.selb[:, r:r + 1], p.ksst2[:, :], ALU.mult, ALU.add))
            assert p._cc_stt[r] == ev.n
        ve.wait_ge(ev.h, ev.n)
        ev.inc(ve.tensor_copy(p.ksst[:, :], p.ksst2[:, :]))
        ve.wait_ge(ev.h, ev.n)
        ve.tensor_copy(p.kvstb[:, :], p.kvst[:, :])
        ev.inc(ve.tensor_copy(p.ksstb[:, :], p.ksst[:, :]))
    vec_(p, v_cc)
    state_rdy = ev.n
    import os as _os
    if _os.environ.get("MK_CUT_A2", "0") == "1":
        return


    emit_a2(p, state_rdy, a1_done_mm)

    a_end = {k: p.sems[k].n for k in ("mm", "ev", "ac", "pl")}

    fn = p.sems["fn"]

    def g_fence(gp, a_end=a_end):
        for k in ("mm", "ev", "ac", "pl"):
            gp.wait_ge(p.sems[k].h, a_end[k])
        gp.sem_inc(fn.h, 1)
    blk.gpsimd(g_fence)
    fn.n += 1
    p._fence_ld = fn.n


def emit_a2(p, state_rdy, a1_done_mm):
    import os
    NCH = int(os.environ.get("MK_A2_NCH", "16"))
    DO_WOLN = os.environ.get("MK_A2_WOLN", "1") == "1"
    blk = p.block
    ld, wx, mm, ev, pl, ac, cc, st, dp = (p.sems[k] for k in
        ["ld", "wx", "mm", "ev", "pl", "ac", "cc", "st", "dp"])

    def g_wo(gp):
        gp.wait_ge(mm.h, a1_done_mm)
        gp.dma_start(out=sb3(p.wbuf[:, :], D), in_=wview(p.woT)).then_inc(ld.h, 16)
        ld.n += 16
    blk.gpsimd(g_wo)
    wo_rdy = ld.n

    sev = {}
    smt_free = {}
    vdiv_ev = {}
    vu_ev = {}

    for c in range(NCH):
        for h in range(H):
            par = h % 2
            sbank = p.ps0 if h % 2 == 0 else p.ps1
            sslot = h % 2

            def t_s(te, c=c, h=h, par=par, sbank=sbank):
                if c == 0 and h == 0:
                    te.wait_ge(ev.h, state_rdy)
                if h == 0:
                    g = vdiv_ev.get(c - 1)
                    if g is not None:
                        te.wait_ge(ev.h, g)
                    gu = vu_ev.get(c - 1)
                    if gu is not None:
                        te.wait_ge(ev.h, gu)
                    if c % 4 == 0 and c > 0 and getattr(p, "_a2_x1_done", None) is not None:
                        te.wait_ge(ev.h, p._a2_x1_done)
                g = sev.get((c, h - 2)) or (sev.get((c - 1, h + 14)) if h < 2 else None)
                if g is not None:
                    te.wait_ge(ev.h, g)
                te.matmul(
                    sbank[:, 0:128],
                    p.kA[par * 64:(par + 1) * 64, (h // 2) * T_OWN + c * 128:(h // 2) * T_OWN + c * 128 + 128],
                    p.qA[par * 64:(par + 1) * 64, (h // 2) * T_OWN + c * 128:(h // 2) * T_OWN + c * 128 + 128],
                    start=True, stop=True,
                ).then_inc(mm.h); mm.n += 1
            blk.tensor(t_s)
            s_mm = mm.n

            def v_m(ve, s_mm=s_mm, sbank=sbank, sslot=sslot, c=c, h=h):
                ve.wait_ge(mm.h, s_mm)
                g = smt_free.get((c, h - 2)) or (smt_free.get((c - 1, h + 14)) if h < 2 else None)
                if g is not None:
                    ve.wait_ge(mm.h, g)
                ev.inc(ve.tensor_tensor(
                    p.smt[:, sslot * 128:(sslot + 1) * 128],
                    sbank[:, 0:128], p.mmaskA[:, :], ALU.mult))
            vec_(p, v_m)
            sev[(c, h)] = ev.n
            smt_rdy = ev.n

            numbank = p.ps2 if h < 8 else p.ps3
            ncol = (h % 8) * 64

            def t_n(te, c=c, h=h, par=par, sslot=sslot, numbank=numbank, ncol=ncol,
                    smt_rdy=smt_rdy):
                te.wait_ge(ev.h, smt_rdy)
                te.matmul(numbank[:, ncol:ncol + 64],
                          p.smt[:, sslot * 128:(sslot + 1) * 128],
                          p.vtokA[:, c * D + h * HD: c * D + h * HD + 64],
                          start=True, stop=False)
                te.matmul(numbank[:, ncol:ncol + 64],
                          p.qA[par * 64:(par + 1) * 64, (h // 2) * T_OWN + c * 128:(h // 2) * T_OWN + c * 128 + 128],
                          p.kvstb[par * 64:(par + 1) * 64, (h // 2) * 64:(h // 2) * 64 + 64],
                          start=False, stop=True)
                te.matmul(p.ps6[:, h:h + 1],
                          p.smt[:, sslot * 128:(sslot + 1) * 128],
                          p.onesb[:, 0:1], start=True, stop=False)
                te.matmul(p.ps6[:, h:h + 1],
                          p.qA[par * 64:(par + 1) * 64, (h // 2) * T_OWN + c * 128:(h // 2) * T_OWN + c * 128 + 128],
                          p.ksstb[par * 64:(par + 1) * 64, h // 2:h // 2 + 1],
                          start=False, stop=True).then_inc(mm.h)
                mm.n += 1
            blk.tensor(t_n)
            smt_free[(c, h)] = mm.n
        ph1_mm = mm.n

        def v_dd(ve, c=c, ph1_mm=ph1_mm):
            ve.wait_ge(mm.h, ph1_mm)
            ev.inc(ve.tensor_scalar(p.rden[:, 0:16], p.ps6[:, 0:16], EPS_A, None, ALU.add))
            ve.wait_ge(ev.h, ev.n)
            ev.inc(ve.reciprocal(p.rden[:, 0:16], p.rden[:, 0:16]))
            ve.wait_ge(ev.h, ev.n)
            for h in range(H):
                numbank = p.ps2 if h < 8 else p.ps3
                ncol = (h % 8) * 64
                ins = ve.tensor_scalar(
                    p.atok[:, (c % 4) * D + h * HD:(c % 4) * D + h * HD + 64],
                    numbank[:, ncol:ncol + 64],
                    p.rden[:, h:h + 1], None, ALU.mult)
            ev.inc(ins)
        vec_(p, v_dd)
        vdiv_ev[c] = ev.n
        atok_rdy = ev.n

        def t_u(te, c=c, ph1_mm=ph1_mm):
            g = vu_ev.get(c - 1)
            if g is not None:
                te.wait_ge(ev.h, g)
            if c % 4 == 0 and c > 0 and getattr(p, "_a2_ln_var", None) is not None:
                te.wait_ge(ev.h, p._a2_ln_var)
            for h in range(H):
                par = h % 2
                te.matmul(p.ps4[par * 64:(par + 1) * 64, (h // 2) * 64:(h // 2) * 64 + 64],
                          p.ktokA[:, c * D + h * HD: c * D + h * HD + 64],
                          p.vtokA[:, c * D + h * HD: c * D + h * HD + 64],
                          start=True, stop=True, tile_position=(0, par * 64))
                ins = te.matmul(p.ps5[par * 64:(par + 1) * 64, h // 2:h // 2 + 1],
                                p.ktokA[:, c * D + h * HD: c * D + h * HD + 64],
                                p.onesb[:, 0:1],
                                start=True, stop=True, tile_position=(0, par * 64))
            ins.then_inc(mm.h); mm.n += 1
        blk.tensor(t_u)
        u_mm = mm.n

        def v_u(ve, c=c, u_mm=u_mm):
            ve.wait_ge(mm.h, u_mm)
            ve.tensor_add(p.kvst[:, :], p.kvst[:, :], p.ps4[:, 0:512])
            ev.inc(ve.tensor_add(p.ksst[:, 0:8], p.ksst[:, 0:8], p.ps5[:, 0:8]))
            ve.wait_ge(ev.h, ev.n)
            ve.tensor_copy(p.kvstb[:, :], p.kvst[:, :])
            ev.inc(ve.tensor_copy(p.ksstb[:, :], p.ksst[:, :]))
        vec_(p, v_u)
        vu_ev[c] = ev.n

        if c % 4 == 3 and DO_WOLN:
            emit_a2_wo_ln(p, c // 4, atok_rdy, wo_rdy, vu_ev[c])


def emit_a2_wo_ln(p, gidx, atok_rdy, wo_rdy, vu_done):
    import os
    blk = p.block
    ld, wx, mm, ev, pl, ac, cc, st, dp = (p.sems[k] for k in
        ["ld", "wx", "mm", "ev", "pl", "ac", "cc", "st", "dp"])
    xq = p.sems["xr"]

    def s_xr(se, gidx=gidx):
        if gidx > 0:
            se.wait_ge(ev.h, p._a2_x1_done)
            if getattr(p, "_a2_x1_store", None) is not None:
                se.wait_ge(st.h, p._a2_x1_store)
        se.dma_start(out=sb3(p.xres[:, :], 512), in_=fm(p.x_fm, gidx * 512, 512)).then_inc(xq.h, 16)
        xq.n += 16
    blk.sync(s_xr)
    xr_rdy = xq.n

    tr_ev0 = ev.n
    tr_mm0 = mm.n

    def t_tr(te, atok_rdy=atok_rdy, gidx=gidx):
        te.wait_ge(ev.h, atok_rdy)
        if gidx > 0:
            te.wait_ge(ev.h, p._a2_afm_free)
        for idx in range(32):
            j, oc = divmod(idx, OC)
            bank = p.ps7 if idx % 2 == 0 else p.ps6
            if idx >= 2:
                te.wait_ge(ev.h, tr_ev0 + idx - 1)
            te.matmul(bank[:, 0:128],
                      p.atok[:, j * D + oc * 128: j * D + oc * 128 + 128],
                      p.mident[:, :], start=True, stop=True).then_inc(mm.h)
            mm.n += 1
    blk.tensor(t_tr)

    def v_tr(ve, tr_mm0=tr_mm0):
        for idx in range(32):
            j, oc = divmod(idx, OC)
            bank = p.ps7 if idx % 2 == 0 else p.ps6
            ve.wait_ge(mm.h, tr_mm0 + idx + 1)
            ev.inc(ve.tensor_copy(
                p.afm[:, oc * 512 + j * 128: oc * 512 + j * 128 + 128],
                bank[:, 0:128]))
    vec_(p, v_tr)
    afm_rdy = ev.n

    wo_ev = {}
    for oco in range(OC):
        bank = p.ps2 if oco % 2 == 0 else p.ps3

        def t_w(te, oco=oco, bank=bank, afm_rdy=afm_rdy):
            te.wait_ge(ev.h, afm_rdy)
            if oco == 0:
                te.wait_ge(ld.h, wo_rdy)
            g = wo_ev.get(oco - 2)
            if g is not None:
                te.wait_ge(ev.h, g)
            for ic in range(OC):
                ins = te.matmul(
                    bank[:, 0:512],
                    p.wbuf[:, ic * D + oco * 128: ic * D + oco * 128 + 128],
                    p.afm[:, ic * 512:(ic + 1) * 512],
                    start=(ic == 0), stop=(ic == OC - 1))
            ins.then_inc(mm.h); mm.n += 1
        blk.tensor(t_w)
        w_mm = mm.n

        def v_w(ve, oco=oco, bank=bank, w_mm=w_mm, xr_rdy=xr_rdy):
            ve.wait_ge(mm.h, w_mm)
            ve.wait_ge(p.sems["xr"].h, xr_rdy)
            ev.inc(ve.scalar_tensor_tensor(
                p.xres[:, oco * 512:(oco + 1) * 512],
                bank[:, 0:512], p.bcolo[:, oco:oco + 1],
                p.xres[:, oco * 512:(oco + 1) * 512],
                ALU.add, ALU.add))
        vec_(p, v_w)
        wo_ev[oco] = ev.n
    p._a2_afm_free = ev.n
    xsum_rdy = ev.n

    emit_ln_fm(p, p.xres, xsum_rdy, x1_store_gidx=gidx, vu_done=vu_done)


def emit_ln_fm(p, src, src_rdy, x1_store_gidx=None, gcol=None, bcol=None,
               out_sb=None, out_cb=None, lvl=4, vu_done=None, waits=()):
    blk = p.block
    mm, ev, ac, st, wx = (p.sems[k] for k in ["mm", "ev", "ac", "st", "wx"])

    ac0 = ac.n
    mm0 = mm.n

    def a_sq(se, src_rdy=src_rdy, mm0=mm0):
        if src_rdy is not None:
            se.wait_ge(ev.h, src_rdy)
        for hh, vv in waits:
            se.wait_ge(hh, vv)
        for oc in range(OC):
            if oc > 0:
                se.wait_ge(mm.h, mm0 + oc)
            ac.inc(se.activation(p.xsq[:, 0:512], src[:, oc * 512:(oc + 1) * 512], AF.Square))
    scl_(p, a_sq)

    def t_sums(te, src_rdy=src_rdy, ac0=ac0, vu_done=vu_done):
        if src_rdy is not None:
            te.wait_ge(ev.h, src_rdy)
        for hh, vv in waits:
            te.wait_ge(hh, vv)
        if vu_done is not None:
            te.wait_ge(ev.h, vu_done)
        for oc in range(OC):
            te.matmul(p.ps4[0:1, 0:512], p.onesf[:, 0:1], src[:, oc * 512:(oc + 1) * 512],
                      start=(oc == 0), stop=(oc == OC - 1))
        for oc in range(OC):
            te.wait_ge(ac.h, ac0 + oc + 1)
            te.matmul(p.ps5[0:1, 0:512], p.onesf[:, 0:1], p.xsq[:, 0:512],
                      start=(oc == 0), stop=(oc == OC - 1)).then_inc(mm.h)
            mm.n += 1
    blk.tensor(t_sums)
    sums_mm = mm.n

    R = p.scr
    def v_r(ve, sums_mm=sums_mm):
        ve.wait_ge(mm.h, sums_mm)
        ev.inc(ve.tensor_scalar(R[0:1, 0:512], p.ps4[0:1, 0:512], 1.0 / D, None, ALU.mult))
        ve.wait_ge(ev.h, ev.n)
        ev.inc(ve.tensor_tensor(R[0:1, 512:1024], R[0:1, 0:512], R[0:1, 0:512], ALU.mult))
        ve.wait_ge(ev.h, ev.n)
        ev.inc(ve.scalar_tensor_tensor(
            R[0:1, 1024:1536], p.ps5[0:1, 0:512], 1.0 / D, R[0:1, 512:1024],
            ALU.mult, ALU.subtract))
    vec_(p, v_r)
    var_ev = ev.n
    p._a2_ln_var = var_ev

    def a_r(se, var_ev=var_ev):
        se.wait_ge(ev.h, var_ev)
        ac.inc(se.activation(R[0:1, 512:1024], R[0:1, 1024:1536], AF.Ln, bias=1e-5))
        se.wait_ge(ac.h, ac.n)
        ac.inc(se.activation(R[0:1, 1024:1536], R[0:1, 512:1024], AF.Exp, scale=-0.5))
    scl_(p, a_r)
    rstd_ac = ac.n

    def v_r2(ve, rstd_ac=rstd_ac):
        ve.wait_ge(ac.h, rstd_ac)
        ev.inc(ve.tensor_tensor(R[0:1, 512:1024], R[0:1, 0:512], R[0:1, 1024:1536], ALU.mult))
    vec_(p, v_r2)
    mur_ev = ev.n
    if lvl < 4:
        p._a2_x1_done = mur_ev
        return mur_ev

    def t_bc(te, mur_ev=mur_ev):
        te.wait_ge(ev.h, mur_ev)
        te.matmul(p.ps0[:, 0:512], p.onesrow_f[0:1, :], R[0:1, 1024:1536], start=True, stop=True)
        te.matmul(p.ps1[:, 0:512], p.onesrow_f[0:1, :], R[0:1, 512:1024],
                  start=True, stop=True).then_inc(mm.h)
        mm.n += 1
    blk.tensor(t_bc)
    bc_mm = mm.n

    dst = out_sb if out_sb is not None else src
    def v_a(ve, bc_mm=bc_mm):
        ve.wait_ge(mm.h, bc_mm)
        for oc in range(OC):
            ev.inc(ve.tensor_tensor(p.xsq[:, 0:512], src[:, oc * 512:(oc + 1) * 512],
                                    p.ps0[:, 0:512], ALU.mult))
            ve.wait_ge(ev.h, ev.n)
            if gcol is None:
                ev.inc(ve.tensor_tensor(dst[:, oc * 512:(oc + 1) * 512],
                                        p.xsq[:, 0:512], p.ps1[:, 0:512], ALU.subtract))
            else:
                ev.inc(ve.tensor_tensor(p.xsq[:, 0:512], p.xsq[:, 0:512], p.ps1[:, 0:512], ALU.subtract))
                ve.wait_ge(ev.h, ev.n)
                ev.inc(ve.tensor_scalar(dst[:, oc * 512:(oc + 1) * 512], p.xsq[:, 0:512],
                                        gcol[:, oc:oc + 1], bcol[:, oc:oc + 1],
                                        ALU.mult, ALU.add))
            ve.wait_ge(ev.h, ev.n)
    vec_(p, v_a)
    app_ev = ev.n
    p._a2_x1_done = app_ev

    if x1_store_gidx is not None:
        gidx = x1_store_gidx
        def s_o(se, gidx=gidx, app_ev=app_ev):
            se.wait_ge(ev.h, app_ev)
            se.dma_start(out=fm(p.x1s, gidx * 512, 512), in_=sb3(dst[:, :], 512)).then_inc(st.h, 16)
            st.n += 16
            if p.mode != "A":
                se.dma_start(out=fm(p.Xs, HALO + gidx * 512, 512), in_=sb3(dst[:, :], 512)).then_inc(st.h, 16)
                st.n += 16
        blk.sync(s_o)
        p._a2_x1_store = st.n
    return app_ev



def _run(p, per_core, outs=("out_fm",)):
    p.ctx.close()
    in_names = {a.memorylocations[0].name if hasattr(a, 'memorylocations') else None
                for a in []}
    res = run_bass_kernel_spmd(p.nc, per_core, core_ids=list(range(NCORES)))
    return res


def kernel_stage_a_test(**inputs):
    per_core = prep_inputs(inputs)
    p = build(mode="A")
    emit_stage_a(p)

    st, dp = p.sems["st"], p.sems["dp"]
    st_all = st.n

    import os
    def g_out(gp, st_all=st_all):
        gp.wait_ge(st.h, st_all)
        if os.environ.get("MK_DBG_ST", "0") == "1":
            gp.dma_start(out=p.out_fm.rearrange("a b -> (a b)")[0:128 * 512].rearrange("(a b) -> a b", b=512),
                         in_=p.kvst[:, :]).then_inc(dp.h, 16); dp.n += 16
            gp.dma_start(out=p.out_fm.rearrange("a b -> (a b)")[70000:70000 + 128 * 8].rearrange("(a b) -> a b", b=8),
                         in_=p.ksst[:, :]).then_inc(dp.h, 16); dp.n += 16
        elif os.environ.get("MK_DBG_CC", "0") == "1":
            gp.dma_start(out=p.out_fm.rearrange("a b -> (a b)")[0:1040 * 64].rearrange("(a b) -> a b", b=64),
                         in_=p.cc_kv_in).then_inc(dp.h, 16); dp.n += 16
            gp.dma_start(out=p.out_fm.rearrange("a b -> (a b)")[70000:70000 + 4160 * 64].rearrange("(a b) -> a b", b=64),
                         in_=p.cc_kv_out).then_inc(dp.h, 16); dp.n += 16
        else:
            gp.dma_start(out=p.out_fm, in_=p.x1s).then_inc(dp.h, 16); dp.n += 16
        gp.wait_ge(dp.h, dp.n)
    p.block.gpsimd(g_out)

    keys = ["x_fm", "wqT", "wkT", "wvT", "woT", "bq128", "bk128", "bo128",
            "bkrow", "bvrow", "sel", "invrow"]
    in_maps = [{k: pc[k] for k in keys} for pc in per_core]
    res = _run(p, in_maps)
    outs = [res.results[c]["out_fm"] for c in range(NCORES)]
    return np.concatenate([o.T for o in outs], axis=0)



def pol_(p, f):
    n = p.sems["pl"].n
    h = p.sems["pl"].h

    def g(po):
        po.wait_ge(h, n)
        f(po)
    p.block.gpsimd(g)


def alloc_stage_b(p):
    nc = p.nc
    sb = lambda n, s, dt=F32: p.ctx.enter_context(nc.sbuf_tensor(n, s, dt))
    p.WSLOT = sb("WSLOT", [128, 8 * 3072], BF16)
    p.kbuf = sb("kbuf", [128, 8 * 1024], BF16)
    p.vbuf = sb("vbuf", [128, 8 * 1024], BF16)
    p.qbuf = sb("qbuf", [128, 8 * 512], BF16)
    p.xnb = sb("xnb", [128, 8 * 2048], BF16)
    p.w1slot = sb("w1slot", [128, 2 * 2048], BF16)
    p.pbuf = sb("pbuf", [128, 2 * 384], BF16)
    p.ptb = sb("ptb", [128, 2 * 384], BF16)
    p.mrow = sb("mrow", [128, 8])
    p.tabsb = sb("tabsb", [128, 2 * 512], BF16)
    p.msk = sb("msk", [128, 4 * 384])
    p.mb2 = sb("mb2", [128, 2048], BF16)
    p.qspb = sb("qspb", [2, 256])
    p.selhb = sb("selhb", [128, 4])
    p.blockpat = sb("blockpat", [128, 2])
    p.ubuf = sb("ubuf", [128, 512], BF16)
    p.rbuf = sb("rbuf", [128, 512], BF16)
    p.lncol = sb("lncol", [128, 3 * OC])
    p.b1col = sb("b1col", [128, 44])
    p.afm2 = p.xnb[:, 0:8 * 512]
    p.nrm0 = p.xsq[64:66, 0:512]
    p.nrm1 = p.xsq[32:34, 0:512]
    p.nrm2 = p.xsq[0:2, 0:512]

    ld = p.sems["ld"]

    lv = p.sems["lv"]

    def g0(gp):
        gp.wait_ge(p.sems["fn"].h, p._fence_ld)
        gp.dma_start(out=p.msk[:, 0:384], in_=p.maskstd).then_inc(lv.h, 16); lv.n += 16
        gp.dma_start(out=sb3(p.msk[:, 384:4 * 384], 384),
                     in_=p.maskfirst.rearrange("j p t -> p j t")).then_inc(lv.h, 16); lv.n += 16
        gp.dma_start(out=p.selhb[:, :], in_=p.selh).then_inc(lv.h, 16); lv.n += 16
    p.block.gpsimd(g0)
    p._b_const_ld = lv.n

    def v0(ve):
        ev = p.sems["ev"]
        ve.wait_ge(p.sems["fn"].h, p._fence_ld)
        ev.inc(ve.memset(p.blockpat[:, 0:2], 0.0))
        ve.wait_ge(ev.h, ev.n)
        ve.memset(p.blockpat[0:64, 0:1], 1.0)
        ev.inc(ve.memset(p.blockpat[64:128, 1:2], 1.0))
    vec_(p, v0)
    p._b_const_ev = p.sems["ev"].n


ST_B = [(0, 512), (512, 512), (1024, 512), (1536, 512), (2048, 384)]
QT_GROUPS = [[0, 1], [2, 3, 4, 5], [6, 7, 8, 9], [10, 11, 12, 13], [14, 15, 16]]


def emit_cc_x(p, i, ready_st):
    blk = p.block
    ev, cc, st, dp, wx = (p.sems[k] for k in ["ev", "cc", "st", "dp", "wx"])
    if not hasattr(p, "_ccx_dp"):
        p._ccx_dp = {}
        p._ccx_stt = {}

    ev0 = p.sems["ev"].n
    k = ev0 + 1
    for r in range(4):
        for half in range(2):
            k += 1
            p._ccx_stt[(i, r, half)] = k
    final_ev = k + 1

    def g(gp, i=i, ready_st=ready_st):
        gp.wait_ge(st.h, ready_st)
        gp.dma_start(out=p.cc_x_in, in_=p.Xs[:, T_EXT - HALO:T_EXT]).then_inc(dp.h, 16)
        dp.n += 16
        gp.wait_ge(dp.h, dp.n)
        cc.inc(gp.collective_compute(
            "AllGather", ALU.bypass, replica_groups=RG,
            ins=[p.cc_x_in.opt()], outs=[p.cc_x_out.opt()]))
        gp.wait_ge(cc.h, cc.n)
        for r in range(4):
            for half in range(2):
                if not (r == 0 and half == 0):
                    gp.wait_ge(ev.h, p._ccx_stt[(i, r, half)] - 1)
                gp.dma_start(
                    out=sb3(p.scr[:, 0:4 * 384], 384),
                    in_=p.cc_x_out[r * 1024 + half * 512: r * 1024 + half * 512 + 512, :]
                        .rearrange("(c p) t -> p c t", p=128),
                ).then_inc(dp.h, 16); dp.n += 16
                p._ccx_dp[(i, r, half)] = dp.n
                gp.wait_ge(dp.h, dp.n)
        gp.wait_ge(ev.h, final_ev)
        gp.dma_start(out=fm(p.Xs, 0, HALO), in_=sb3(p.xres[:, 0:8 * 384], 384)).then_inc(dp.h, 16)
        dp.n += 16
        gp.wait_ge(dp.h, dp.n)
    blk.gpsimd(g)

    def v(ve, i=i, ready_st=ready_st):
        ve.wait_ge(st.h, ready_st)
        ve.wait_ge(p.sems["lv"].h, p._b_const_ld)
        ev.inc(ve.memset(p.xres[:, 0:8 * 384], 0.0))
        for r in range(4):
            for half in range(2):
                ve.wait_ge(dp.h, p._ccx_dp[(i, r, half)])
                ve.wait_ge(ev.h, ev.n)
                ev.inc(ve.scalar_tensor_tensor(
                    p.xres[:, half * 4 * 384:(half + 1) * 4 * 384],
                    p.scr[:, 0:4 * 384], p.selhb[:, r:r + 1],
                    p.xres[:, half * 4 * 384:(half + 1) * 4 * 384],
                    ALU.mult, ALU.add))
                assert p._ccx_stt[(i, r, half)] == ev.n
        ve.wait_ge(ev.h, ev.n)
        ev.inc(ve.tensor_copy(p.xres[0:1, 0:1], p.xres[0:1, 0:1]))
        assert final_ev == ev.n
    vec_(p, v)
    return dp.n


def emit_p1(p, i, halo_dp):
    p._halo_dp = halo_dp
    blk = p.block
    ld, wx, mm, ev, pl, ac, cc, st, dp, xq = (p.sems[k] for k in
        ["ld", "wx", "mm", "ev", "pl", "ac", "cc", "st", "dp", "xr"])

    def g_w(gp, i=i):
        if i == 0:
            gp.wait_ge(p.sems["fn"].h, p._fence_ld)
        gp.wait_ge(mm.h, p._wslot_free)
        gp.dma_start(out=sb3(p.WSLOT[:, :], 3 * D),
                     in_=p.wqkv[i].rearrange("(c p) o -> p c o", p=128)).then_inc(ld.h, 16)
        ld.n += 16
        gp.wait_ge(mm.h, p._wbuf_free)
        gp.dma_start(out=sb3(p.wbuf[:, :], D),
                     in_=p.wo_b[i].rearrange("(c p) o -> p c o", p=128)).then_inc(ld.h, 16)
        ld.n += 16
        gp.dma_start(out=p.qspb[0:2, 0:128], in_=p.qsp[i]).then_inc(ld.h, 16); ld.n += 16
        gp.dma_start(out=p.qspb[0:2, 128:256], in_=p.ksp[i]).then_inc(ld.h, 16); ld.n += 16
    blk.gpsimd(g_w)
    w_rdy = ld.n

    M = {}
    M["afm_free"] = {0: None, 1: None}
    M["x2st"] = {}
    qdone = {}

    for sti, (t0, wid) in enumerate(ST_B):
        nt = wid // 128

        def g_x(gp, sti=sti, t0=t0, wid=wid, i=i, halo_dp=halo_dp):
            gp.wait_ge(mm.h, M.get(("xbfree", sti), 0))
            if sti == 0:
                gp.wait_ge(dp.h, halo_dp)
                if i == 0 and p.mode != "A":
                    gp.wait_ge(st.h, p._a2_x1_store)
            gp.dma_start(out=sb3(p.xb[:, 0:4 * 512], 512)[:, :, 0:wid],
                         in_=fm(p.Xs, t0, wid)[:, 0:4, :]).then_inc(wx.h, 16)
            wx.n += 16
            if t0 > 0:
                gp.dma_start(out=sb3(p.xb[:, 4 * 512:8 * 512], 512)[:, :, 0:wid],
                             in_=fm(p.Xs, t0 - 1, wid)[:, 4:8, :]).then_inc(wx.h, 16)
                wx.n += 16
            else:
                gp.dma_start(out=sb3(p.xb[:, 4 * 512:8 * 512], 512)[:, :, 1:wid],
                             in_=fm(p.Xs, 0, wid - 1)[:, 4:8, :]).then_inc(wx.h, 16)
                wx.n += 16
                gp.dma_start(out=sb3(p.xb[:, 4 * 512:8 * 512], 512)[:, :, 0:1],
                             in_=fm(p.Xs, 0, 1)[:, 4:8, :],
                             allow_slow_non_contiguous=True).then_inc(wx.h, 16)
                wx.n += 16
        blk.gpsimd(g_x)
        x_rdy = wx.n

        for qk in range(2):
            tabs_free = M.get(("tabfree",))

            def g_tab(gp, qk=qk, t0=t0, wid=wid, tabs_free=tabs_free, i=i):
                if tabs_free is not None:
                    gp.wait_ge(ev.h, tabs_free)
                elif i == 0:
                    gp.wait_ge(p.sems["fn"].h, p._fence_ld)
                srcs = [p.tab_cq, p.tab_sq] if qk == 0 else [p.tab_ck, p.tab_sk]
                fnm = p.sems["tb"]
                for tbl, src_t in enumerate(srcs):
                    gp.dma_start(out=p.tabsb[:, tbl * 512: tbl * 512 + wid],
                                 in_=src_t[:, t0:t0 + wid]).then_inc(fnm.h, 16)
                    fnm.n += 16
            blk.gpsimd(g_tab)
            tab_rdy = p.sems["tb"].n
            for oc in range(OC):
                pp = (qk * OC + oc) % 2
                bank = p.ps0 if pp == 0 else p.ps1
                fbank = p.ps2 if pp == 0 else p.ps3

                def t_p(te, qk=qk, oc=oc, sti=sti, wid=wid, bank=bank, x_rdy=x_rdy, pp=pp,
                        evbar=(p.sems["ev"].n if (qk == 0 and oc == 0) else None)):
                    te.wait_ge(wx.h, x_rdy)
                    if evbar is not None:
                        te.wait_ge(ev.h, evbar)
                    if sti == 0 and qk == 0 and oc == 0:
                        te.wait_ge(ld.h, w_rdy)
                        te.wait_ge(ev.h, p._b_const_ev)
                        te.wait_ge(p.sems["lv"].h, p._b_const_ld)
                        if getattr(p, "_p1_prev_done", None) is not None:
                            te.wait_ge(ev.h, p._p1_prev_done)
                    g = M.get(("projfree", pp))
                    if g is not None:
                        te.wait_ge(ac.h, g)
                    for dc in range(OC):
                        ins = te.matmul(
                            bank[:, 0:wid],
                            p.WSLOT[:, dc * 3 * D + qk * D + oc * 128: dc * 3 * D + qk * D + oc * 128 + 128],
                            p.xb[:, dc * 512: dc * 512 + wid],
                            start=(dc == 0), stop=(dc == OC - 1))
                    ins.then_inc(mm.h); mm.n += 1
                blk.tensor(t_p)
                mmv = mm.n

                def a_e(se, bank=bank, wid=wid, mmv=mmv):
                    se.wait_ge(mm.h, mmv)
                    g = M.get(("scrfree",))
                    if g is not None:
                        se.wait_ge(ev.h, g)
                    ac.inc(se.copy(p.scr[:, 0:wid], bank[:, 0:wid]))
                    se.wait_ge(ac.h, ac.n)
                    ac.inc(se.activation(p.scr[:, 512:512 + wid], p.scr[:, 0:wid], AF.Square))
                scl_(p, a_e)
                M[("projfree", pp)] = ac.n
                acv = ac.n

                def t_f(te, acv=acv, wid=wid, fbank=fbank, pp=pp):
                    te.wait_ge(ac.h, acv)
                    g = M.get(("facfree", pp))
                    if g is not None:
                        te.wait_ge(ev.h, g)
                    g2 = M.get(("nrmfree",))
                    if g2 is not None:
                        te.wait_ge(ev.h, g2)
                    te.matmul(p.ps6[0:2, 0:wid], p.blockpat[:, 0:2],
                              p.scr[:, 512:512 + wid],
                              start=True, stop=True).then_inc(mm.h)
                    mm.n += 1
                blk.tensor(t_f)
                nmv = mm.n

                def v_f(ve, nmv=nmv, wid=wid):
                    ve.wait_ge(mm.h, nmv)
                    ev.inc(ve.tensor_scalar(p.nrm0[0:2, 0:wid], p.ps6[0:2, 0:wid],
                                            1e-24, None, ALU.max))
                vec_(p, v_f)
                evf = ev.n
                M[("nrmfree",)] = ev.n

                def a_f(se, evf=evf, wid=wid):
                    se.wait_ge(ev.h, evf)
                    ac.inc(se.activation(p.nrm1[0:2, 0:wid],
                                         p.nrm0[0:2, 0:wid], AF.Ln))
                    se.wait_ge(ac.h, ac.n)
                    ac.inc(se.activation(p.nrm2[0:2, 0:wid],
                                         p.nrm1[0:2, 0:wid], AF.Exp, scale=-0.5))
                scl_(p, a_f)
                acf = ac.n

                def t_f2(te, qk=qk, acf=acf, wid=wid, fbank=fbank):
                    te.wait_ge(ac.h, acf)
                    te.matmul(fbank[:, 0:wid], p.qspb[0:2, qk * 128:qk * 128 + 128],
                              p.nrm2[0:2, 0:wid],
                              start=True, stop=True).then_inc(mm.h)
                    mm.n += 1
                blk.tensor(t_f2)
                fmv = mm.n

                def v_rot(ve, qk=qk, oc=oc, t0=t0, wid=wid, fmv=fmv, fbank=fbank, tab_rdy=tab_rdy):
                    ve.wait_ge(mm.h, fmv)
                    ve.wait_ge(p.sems["tb"].h, tab_rdy)
                    ev.inc(ve.tensor_tensor(p.ubuf[:, 0:wid], p.scr[:, 0:wid],
                                            fbank[:, 0:wid], ALU.mult))
                    ve.wait_ge(ev.h, ev.n)
                    ve.tensor_copy(p.rbuf[0:32, 0:wid], p.ubuf[32:64, 0:wid])
                    ve.tensor_copy(p.rbuf[32:64, 0:wid], p.ubuf[0:32, 0:wid])
                    ve.tensor_copy(p.rbuf[64:96, 0:wid], p.ubuf[96:128, 0:wid])
                    ev.inc(ve.tensor_copy(p.rbuf[96:128, 0:wid], p.ubuf[64:96, 0:wid]))
                    ve.wait_ge(ev.h, ev.n)
                    ctab = p.tabsb[:, 0:wid]
                    stab = p.tabsb[:, 512:512 + wid]
                    ev.inc(ve.tensor_tensor(p.ubuf[:, 0:wid], p.ubuf[:, 0:wid], ctab, ALU.mult))
                    ev.inc(ve.tensor_tensor(p.rbuf[:, 0:wid], p.rbuf[:, 0:wid], stab, ALU.mult))
                    ve.wait_ge(ev.h, ev.n)
                    if qk == 0:
                        dst = p.qbuf[:, oc * 512: oc * 512 + wid]
                    else:
                        ring0 = (t0 // 128) % 8
                        dst = p.kbuf[:, oc * 1024 + ring0 * 128: oc * 1024 + ring0 * 128 + wid]
                    ev.inc(ve.tensor_add(dst, p.ubuf[:, 0:wid], p.rbuf[:, 0:wid]))
                vec_(p, v_rot)
                M[("facfree", pp)] = ev.n
                M[("scrfree",)] = ev.n
                M[("tabfree",)] = ev.n

        for j in range(nt):
            for half in range(2):
                pp = (j * 2 + half) % 2
                bank = p.ps0 if pp == 0 else p.ps1

                def t_v(te, j=j, half=half, bank=bank, x_rdy=x_rdy, pp=pp):
                    te.wait_ge(wx.h, x_rdy)
                    g = M.get(("vfree", pp))
                    if g is not None:
                        te.wait_ge(ev.h, g)
                    g2 = M.get(("projfree", pp))
                    if g2 is not None:
                        te.wait_ge(ac.h, g2)
                    for dc in range(OC):
                        ins = te.matmul(
                            bank[:, 0:512],
                            p.xb[:, dc * 512 + j * 128: dc * 512 + j * 128 + 128],
                            p.WSLOT[:, dc * 3 * D + 2 * D + half * 512: dc * 3 * D + 2 * D + half * 512 + 512],
                            start=(dc == 0), stop=(dc == OC - 1))
                    ins.then_inc(mm.h); mm.n += 1
                blk.tensor(t_v)
                mmv = mm.n

                def v_v(ve, j=j, half=half, t0=t0, mmv=mmv, bank=bank):
                    ve.wait_ge(mm.h, mmv)
                    ring = ((t0 // 128) + j) % 8
                    ev.inc(ve.tensor_copy(
                        p.vbuf[:, ring * 1024 + half * 512: ring * 1024 + half * 512 + 512],
                        bank[:, 0:512]))
                vec_(p, v_v)
                M[("vfree", pp)] = ev.n
        proj_ev = ev.n
        M[("xbfree", sti + 1)] = mm.n

        ready = [j for j in range(17) if t0 <= (j + 2) * 128 < t0 + wid]
        for j in ready:
            qcol = (j + 2) * 128 - t0
            group = next(gi for gi, g_ in enumerate(QT_GROUPS) if j in g_)
            gslot = group % 2
            afm_t = p.afm if gslot == 0 else p.afm2
            jj = j - QT_GROUPS[group][0]
            for h in range(H):
                par = h % 2
                oc = h // 2
                slot = h % 2
                sbank = p.ps2 if h % 2 == 0 else p.ps3
                tbank = p.ps4 if h % 2 == 0 else p.ps5
                pvbank = p.ps6 if h % 2 == 0 else p.ps7

                def t_s(te, j=j, h=h, par=par, oc=oc, qcol=qcol, sbank=sbank,
                        proj_ev=proj_ev, first=(j == ready[0] and h == 0), group=group, gslot=gslot):
                    if first:
                        te.wait_ge(ev.h, proj_ev)
                        g0_ = M["afm_free"][gslot]
                        if g0_ is not None:
                            te.wait_ge(ev.h, g0_)
                    g = M.get(("sfree", id(sbank)))
                    if g is not None:
                        te.wait_ge(ac.h, g)
                    for kt in range(3):
                        ring = (j + kt) % 8
                        ins = te.matmul(
                            sbank[:, kt * 128: kt * 128 + 128],
                            p.qbuf[par * 64:(par + 1) * 64, oc * 512 + qcol: oc * 512 + qcol + 128],
                            p.kbuf[par * 64:(par + 1) * 64, oc * 1024 + ring * 128: oc * 1024 + ring * 128 + 128],
                            start=True, stop=True)
                    ins.then_inc(mm.h); mm.n += 1
                blk.tensor(t_s)
                smm = mm.n

                def a_cp(se, smm=smm, sbank=sbank, slot=slot):
                    se.wait_ge(mm.h, smm)
                    g = M.get(("scrAfree", slot))
                    if g is not None:
                        se.wait_ge(ac.h, g)
                    ac.inc(se.copy(p.scr[:, slot * 512: slot * 512 + 384], sbank[:, 0:384]))
                scl_(p, a_cp)
                M[("sfree", id(sbank))] = ac.n
                accp = ac.n

                def p_m(po, j=j, slot=slot, accp=accp):
                    po.wait_ge(ac.h, accp)
                    mk = p.msk[:, (j + 1) * 384:(j + 2) * 384] if j < 3 else p.msk[:, 0:384]
                    pl.inc(po.tensor_add(p.scr[:, slot * 512: slot * 512 + 384],
                                         p.scr[:, slot * 512: slot * 512 + 384], mk))
                pol_(p, p_m)
                plm = pl.n

                def v_mx(ve, slot=slot, plm=plm):
                    ve.wait_ge(pl.h, plm)
                    g = M.get(("mrowfree", slot))
                    if g is not None:
                        ve.wait_ge(ac.h, g)
                    ev.inc(ve.reduce_max(p.mrow[:, slot * 4: slot * 4 + 1],
                                         p.scr[:, slot * 512: slot * 512 + 384],
                                         axis=AX, negate=True))
                vec_(p, v_mx)
                evm = ev.n

                def a_ex(se, slot=slot, evm=evm):
                    se.wait_ge(ev.h, evm)
                    g = M.get(("pbfree", slot))
                    if g is not None:
                        se.wait_ge(ev.h, g)
                    ac.inc(se.activation(p.pbuf[:, slot * 384:(slot + 1) * 384],
                                         p.scr[:, slot * 512: slot * 512 + 384],
                                         AF.Exp, bias=p.mrow[:, slot * 4: slot * 4 + 1],
                                         accum_out=p.mrow[:, slot * 4 + 1: slot * 4 + 2]))
                scl_(p, a_ex)
                M[("scrAfree", slot)] = ac.n
                M[("mrowfree", slot)] = ac.n
                acex = ac.n

                def v_sc(ve, slot=slot, acex=acex):
                    ve.wait_ge(ac.h, acex)
                    ev.inc(ve.reciprocal(p.mrow[:, slot * 4 + 2: slot * 4 + 3],
                                         p.mrow[:, slot * 4 + 1: slot * 4 + 2]))
                    ve.wait_ge(ev.h, ev.n)
                    ev.inc(ve.tensor_scalar(p.pbuf[:, slot * 384:(slot + 1) * 384],
                                            p.pbuf[:, slot * 384:(slot + 1) * 384],
                                            p.mrow[:, slot * 4 + 2: slot * 4 + 3],
                                            None, ALU.mult))
                vec_(p, v_sc)
                evp = ev.n

                def t_tr(te, slot=slot, evp=evp, tbank=tbank):
                    te.wait_ge(ev.h, evp)
                    g = M.get(("trfree", id(tbank)))
                    if g is not None:
                        te.wait_ge(ev.h, g)
                    for kt in range(3):
                        ins = te.matmul(
                            tbank[:, kt * 128: kt * 128 + 128],
                            p.pbuf[:, slot * 384 + kt * 128: slot * 384 + kt * 128 + 128],
                            p.mident[:, :], start=True, stop=True)
                    ins.then_inc(mm.h); mm.n += 1
                blk.tensor(t_tr)
                trmm = mm.n

                def v_tr(ve, slot=slot, trmm=trmm, tbank=tbank):
                    ve.wait_ge(mm.h, trmm)
                    g = M.get(("ptfree", slot))
                    if g is not None:
                        ve.wait_ge(mm.h, g)
                    ev.inc(ve.tensor_copy(p.ptb[:, slot * 384:(slot + 1) * 384],
                                          tbank[:, 0:384]))
                vec_(p, v_tr)
                M[("trfree", id(tbank))] = ev.n
                M[("pbfree", slot)] = ev.n
                evt = ev.n

                def t_pv(te, j=j, h=h, slot=slot, evt=evt, pvbank=pvbank):
                    te.wait_ge(ev.h, evt)
                    g = M.get(("pvfree", id(pvbank)))
                    if g is not None:
                        te.wait_ge(ev.h, g)
                    for kt in range(3):
                        ring = (j + kt) % 8
                        ins = te.matmul(
                            pvbank[0:64, 0:128],
                            p.vbuf[:, ring * 1024 + h * HD: ring * 1024 + h * HD + 64],
                            p.ptb[:, slot * 384 + kt * 128: slot * 384 + kt * 128 + 128],
                            start=(kt == 0), stop=(kt == 2))
                    ins.then_inc(mm.h); mm.n += 1
                blk.tensor(t_pv)
                M[("ptfree", slot)] = mm.n
                pvmm = mm.n

                def v_pv(ve, h=h, par=par, oc=oc, jj=jj, pvmm=pvmm, pvbank=pvbank, afm_t=afm_t):
                    ve.wait_ge(mm.h, pvmm)
                    ev.inc(ve.tensor_copy(
                        afm_t[par * 64:(par + 1) * 64, oc * 512 + jj * 128: oc * 512 + jj * 128 + 128],
                        pvbank[0:64, 0:128]))
                vec_(p, v_pv)
                M[("pvfree", id(pvbank))] = ev.n

            if j == QT_GROUPS[group][-1]:
                emit_p1_wo(p, i, group, M)
        qdone[sti] = ev.n

    p._p1_prev_done = p.sems["ev"].n
    p._p1_last_mm = mm.n
    return M


def emit_p1_wo(p, i, group, M):
    blk = p.block
    ld, wx, mm, ev, pl, ac, st, xq, dp = (p.sems[k] for k in
        ["ld", "wx", "mm", "ev", "pl", "ac", "st", "xr", "dp"])
    js = QT_GROUPS[group]
    glen = len(js) * 128
    e0 = 256 + js[0] * 128
    gslot = group % 2
    afm_t = p.afm if gslot == 0 else p.afm2
    afm_rdy = p.sems["ev"].n

    def s_x(se, group=group, e0=e0, glen=glen):
        g = M.get(("xresfree",))
        if g is not None:
            se.wait_ge(ev.h, g)
        g2 = M["x2st"].get(group - 1)
        if g2 is not None:
            se.wait_ge(st.h, g2)
        if group == 0:
            se.wait_ge(dp.h, p._halo_dp)
        se.dma_start(out=sb3(p.xres[:, 0:8 * 512], 512)[:, :, 0:glen],
                     in_=fm(p.Xs, e0, glen)).then_inc(xq.h, 16)
        xq.n += 16
    blk.sync(s_x)
    xr_rdy = xq.n

    woev = {}
    for oco in range(OC):
        bank = p.ps0 if oco % 2 == 0 else p.ps1

        def t_w(te, oco=oco, bank=bank, glen=glen, afm_rdy=afm_rdy, group=group, afm_t=afm_t):
            te.wait_ge(ev.h, afm_rdy)
            g = woev.get(oco - 2)
            if g is not None:
                te.wait_ge(ev.h, g)
            g2 = M.get(("wo_projpend", oco % 2))
            if g2 is not None:
                te.wait_ge(ac.h, g2)
            for ic in range(OC):
                ins = te.matmul(
                    bank[:, 0:glen],
                    p.wbuf[:, ic * D + oco * 128: ic * D + oco * 128 + 128],
                    afm_t[:, ic * 512: ic * 512 + glen],
                    start=(ic == 0), stop=(ic == OC - 1))
            ins.then_inc(mm.h); mm.n += 1
        blk.tensor(t_w)
        wmm = mm.n

        def v_w(ve, oco=oco, bank=bank, wmm=wmm, xr_rdy=xr_rdy, glen=glen):
            ve.wait_ge(mm.h, wmm)
            ve.wait_ge(xq.h, xr_rdy)
            ev.inc(ve.tensor_add(
                p.xres[:, oco * 512: oco * 512 + glen],
                bank[:, 0:glen],
                p.xres[:, oco * 512: oco * 512 + glen]))
        vec_(p, v_w)
        woev[oco] = ev.n
    M["afm_free"][gslot] = ev.n
    M[("xresfree",)] = ev.n
    x2_ev = ev.n

    def s_o(se, group=group, e0=e0, glen=glen, x2_ev=x2_ev):
        se.wait_ge(ev.h, x2_ev)
        se.dma_start(out=fm(p.X2s, e0, glen),
                     in_=sb3(p.xres[:, 0:8 * 512], 512)[:, :, 0:glen]).then_inc(st.h, 16)
        st.n += 16
    blk.sync(s_o)
    M["x2st"][group] = st.n


def emit_p2(p, i):
    blk = p.block
    ld, wx, mm, ev, pl, ac, st, dp, xq = (p.sems[k] for k in
        ["ld", "wx", "mm", "ev", "pl", "ac", "st", "dp", "xr"])

    lv = p.sems["lv"]

    def g_lnv(gp, i=i):
        gp.wait_ge(st.h, p._p1_x2st_all)
        gp.dma_start(out=p.lncol[:, 0:OC], in_=p.ln_g[i]).then_inc(lv.h, 16); lv.n += 16
        gp.dma_start(out=p.lncol[:, OC:2 * OC], in_=p.ln_b[i]).then_inc(lv.h, 16); lv.n += 16
        gp.dma_start(out=p.lncol[:, 2 * OC:3 * OC], in_=p.b2c[i]).then_inc(lv.h, 16); lv.n += 16
        gp.dma_start(out=p.b1col[:, :], in_=p.b1c[i]).then_inc(lv.h, 16); lv.n += 16
    blk.gpsimd(g_lnv)
    lnv_rdy = lv.n

    last_store = None
    for tt in range(4):
        e0 = HALO + tt * 512

        def s_x2(se, tt=tt, e0=e0, i=i):
            g = getattr(p, "_p2_xres_free", None)
            if g is not None:
                se.wait_ge(ev.h, g)
            se.wait_ge(st.h, p._p1_x2st_all)
            se.dma_start(out=sb3(p.xres[:, 0:8 * 512], 512)[:, 0:4, :],
                         in_=fm(p.X2s, e0, 512)[:, 0:4, :]).then_inc(xq.h, 16)
            xq.n += 16
            se.dma_start(out=sb3(p.xres[:, 0:8 * 512], 512)[:, 4:8, :],
                         in_=fm(p.X2s, e0 - 1, 512)[:, 4:8, :]).then_inc(xq.h, 16)
            xq.n += 16
        blk.sync(s_x2)
        x2_rdy = xq.n

        emit_ln_fm(p, p.xres, None, gcol=p.lncol[:, 0:OC], bcol=p.lncol[:, OC:2 * OC],
                   out_sb=p.xnb[:, tt * 4096:(tt + 1) * 4096],
                   waits=[(xq.h, x2_rdy), (lv.h, lnv_rdy)])
        p._p2_xres_free = p._a2_x1_done

    xn_rdy = p._a2_x1_done

    w2s = p.sems["w2"]

    def g_w2(gp, i=i):
        gp.wait_ge(mm.h, p._p1_last_mm)
        gp.dma_start(out=sb3(p.WSLOT[:, 0:22 * 1024], 1024),
                     in_=p.w2[i].rearrange("(j p) o -> p j o", p=128)).then_inc(w2s.h, 16)
        w2s.n += 16
    blk.gpsimd(g_w2)
    w2_rdy = w2s.n

    ghst = {}
    for jp in range(22):
        wslot2 = (jp % 2) * 2048

        def g_w1(gp, jp=jp, i=i, wslot2=wslot2):
            g = getattr(p, "_p2_w1_free", {}).get(jp - 2)
            if g is not None:
                gp.wait_ge(mm.h, g)
            if jp < 2:
                gp.wait_ge(mm.h, p._p1_last_mm)
            g2 = M2get(p, ("w1cons", jp - 1 if jp >= 1 else 21))
            if g2 is not None:
                gp.wait_ge(mm.h, g2)
            gp.dma_start(out=sb3(p.w1slot[:, wslot2:wslot2 + 1024], 128),
                         in_=p.w1[i].rearrange("(c p) o -> p c o", p=128)[:, :, jp * 128:(jp + 1) * 128]
                         ).then_inc(ld.h, 16)
            ld.n += 16
            gp.dma_start(out=sb3(p.w1slot[:, wslot2 + 1024:wslot2 + 2048], 128),
                         in_=p.w1[i].rearrange("(c p) o -> p c o", p=128)[:, :, (jp + 22) * 128:(jp + 23) * 128]
                         ).then_inc(ld.h, 16)
            ld.n += 16
        blk.gpsimd(g_w1)
        w1_rdy = ld.n

        mb_ev = {}
        for tt in range(4):
            pa = p.ps0 if tt % 2 == 0 else p.ps2
            pg = p.ps1 if tt % 2 == 0 else p.ps3

            def t_h(te, jp=jp, tt=tt, pa=pa, pg=pg, w1_rdy=w1_rdy, xn_rdy=xn_rdy, wslot2=wslot2):
                if tt == 0:
                    te.wait_ge(ld.h, w1_rdy)
                    if jp == 0:
                        te.wait_ge(ev.h, xn_rdy)
                g = M2get(p, ("hfree", tt % 2))
                if g is not None:
                    te.wait_ge(ev.h, g)
                for dc in range(OC):
                    te.matmul(pa[:, 0:512],
                              p.w1slot[:, wslot2 + dc * 128: wslot2 + dc * 128 + 128],
                              p.xnb[:, tt * 4096 + dc * 512: tt * 4096 + (dc + 1) * 512],
                              start=(dc == 0), stop=(dc == OC - 1))
                for dc in range(OC):
                    ins = te.matmul(pg[:, 0:512],
                                    p.w1slot[:, wslot2 + 1024 + dc * 128: wslot2 + 1024 + dc * 128 + 128],
                                    p.xnb[:, tt * 4096 + dc * 512: tt * 4096 + (dc + 1) * 512],
                                    start=(dc == 0), stop=(dc == OC - 1))
                ins.then_inc(mm.h); mm.n += 1
            blk.tensor(t_h)
            hmm = mm.n
            if tt == 0:
                M2set(p, ("w1cons", jp), hmm)
            if not hasattr(p, "_p2_w1_free"):
                p._p2_w1_free = {}
            p._p2_w1_free[jp] = hmm

            def a_g(se, jp=jp, tt=tt, pg=pg, hmm=hmm):
                se.wait_ge(mm.h, hmm)
                g = M2get(p, ("gactfree", tt % 2))
                if g is not None:
                    se.wait_ge(ev.h, g)
                g3 = M2get(p, ("xsqfree",))
                if g3 is not None:
                    se.wait_ge(ev.h, g3)
                ac.inc(se.activation(p.xsq[:, 0:512], pg[:, 0:512], AF.Gelu,
                                     bias=p.b1col[:, jp + 22: jp + 23]))
            scl_(p, a_g)
            agm = ac.n

            def v_m(ve, jp=jp, tt=tt, pa=pa, agm=agm):
                ve.wait_ge(ac.h, agm)
                g = M2get(p, ("mbfree",))
                if g is not None:
                    ve.wait_ge(dp.h, g)
                ev.inc(ve.scalar_tensor_tensor(
                    p.mb2[:, tt * 512:(tt + 1) * 512],
                    pa[:, 0:512], p.b1col[:, jp:jp + 1],
                    p.xsq[:, 0:512], ALU.add, ALU.mult))
            vec_(p, v_m)
            M2set(p, ("hfree", tt % 2), ev.n)
            M2set(p, ("gactfree", tt % 2), ev.n)
            M2set(p, ("xsqfree",), ev.n)
            mb_ev[tt] = ev.n

        def g_mb(gp, jp=jp, mb_ev=dict(mb_ev)):
            gp.wait_ge(ev.h, mb_ev[3])
            gp.dma_start(out=p.gh[jp * 128:(jp + 1) * 128, :], in_=p.mb2[:, :]).then_inc(dp.h, 16)
            dp.n += 16
        blk.gpsimd(g_mb)
        ghst[jp] = dp.n
        M2set(p, ("mbfree",), dp.n)
    gh_all = dp.n

    x3st = 0
    for tt in range(8):
        e0 = HALO + tt * 256

        def g_m(gp, tt=tt, gh_all=gh_all):
            gp.wait_ge(dp.h, gh_all)
            g = M2get(p, ("xnfree", tt % 2))
            if g is not None:
                gp.wait_ge(mm.h, g)
            g2 = M2get(p, ("mcons", tt - 1 if tt >= 1 else 7))
            if g2 is not None:
                gp.wait_ge(mm.h, g2)
            gp.dma_start(out=sb3(p.xnb[:, (tt % 2) * 8192:(tt % 2) * 8192 + 22 * 256], 256),
                         in_=p.gh.rearrange("(j p) t -> p j t", p=128)[:, :, tt * 256:(tt + 1) * 256]
                         ).then_inc(wx.h, 16)
            wx.n += 16
        blk.gpsimd(g_m)
        m_rdy = wx.n

        def s_x2b(se, tt=tt, e0=e0, xn_rdy=xn_rdy):
            g = M2get(p, ("x2bfree",))
            if g is not None:
                se.wait_ge(ev.h, g)
            if tt == 0:
                se.wait_ge(ev.h, xn_rdy)
            g2 = M2get(p, ("x3st",))
            if g2 is not None:
                se.wait_ge(st.h, g2)
            se.dma_start(out=sb3(p.xres[:, 0:8 * 512], 512)[:, :, 0:256],
                         in_=fm(p.X2s, e0, 256)).then_inc(xq.h, 16)
            xq.n += 16
        blk.sync(s_x2b)
        x2b_rdy = xq.n

        for oco in range(OC):
            bank = p.ps4 if oco % 2 == 0 else p.ps5

            def t_o(te, tt=tt, oco=oco, bank=bank, m_rdy=m_rdy):
                te.wait_ge(wx.h, m_rdy)
                if tt == 0 and oco == 0:
                    te.wait_ge(w2s.h, w2_rdy)
                g = M2get(p, ("obfree", oco % 2))
                if g is not None:
                    te.wait_ge(ev.h, g)
                for jc in range(22):
                    ins = te.matmul(
                        bank[:, 0:256],
                        p.WSLOT[:, jc * 1024 + oco * 128: jc * 1024 + oco * 128 + 128],
                        p.xnb[:, (tt % 2) * 8192 + jc * 256:(tt % 2) * 8192 + (jc + 1) * 256],
                        start=(jc == 0), stop=(jc == 21))
                ins.then_inc(mm.h); mm.n += 1
            blk.tensor(t_o)
            omm = mm.n
            if oco == 0:
                M2set(p, ("mcons", tt), omm)

            def v_o(ve, tt=tt, oco=oco, bank=bank, omm=omm, x2b_rdy=x2b_rdy):
                ve.wait_ge(mm.h, omm)
                ve.wait_ge(xq.h, x2b_rdy)
                ev.inc(ve.scalar_tensor_tensor(
                    p.xres[:, oco * 512:oco * 512 + 256],
                    bank[:, 0:256], p.lncol[:, 2 * OC + oco: 2 * OC + oco + 1],
                    p.xres[:, oco * 512:oco * 512 + 256],
                    ALU.add, ALU.add))
            vec_(p, v_o)
            M2set(p, ("obfree", oco % 2), ev.n)
        M2set(p, ("xnfree", tt % 2), mm.n)
        M2set(p, ("x2bfree",), ev.n)
        x3_ev = ev.n

        def s_x3(se, tt=tt, e0=e0, x3_ev=x3_ev):
            se.wait_ge(ev.h, x3_ev)
            se.dma_start(out=fm(p.Xs, e0, 256),
                         in_=sb3(p.xres[:, 0:8 * 512], 512)[:, :, 0:256]).then_inc(st.h, 16)
            st.n += 16
        blk.sync(s_x3)
        x3st = st.n
        M2set(p, ("x3st",), st.n)
    p._wslot_free = mm.n
    return x3st


def M2get(p, k):
    return getattr(p, "_M2", {}).get(k)


def M2set(p, k, v):
    if not hasattr(p, "_M2"):
        p._M2 = {}
    p._M2[k] = v


def emit_final(p):
    blk = p.block
    ev, st, xq, dp = (p.sems[k] for k in ["ev", "st", "xr", "dp"])
    for gidx in range(4):
        def s_l(se, gidx=gidx):
            g = getattr(p, "_fin_free", None)
            if g is not None:
                se.wait_ge(ev.h, g)
            g2 = getattr(p, "_fin_stp", None)
            if g2 is not None:
                se.wait_ge(st.h, g2)
            se.wait_ge(st.h, p._x6_st)
            se.dma_start(out=sb3(p.xres[:, 0:8 * 512], 512),
                         in_=fm(p.Xs, HALO + gidx * 512, 512)).then_inc(xq.h, 16)
            xq.n += 16
            se.dma_start(out=sb3(p.xnb.bitcast(F32)[:, 0:4096], 512),
                         in_=fm(p.x1s, gidx * 512, 512)).then_inc(xq.h, 16)
            xq.n += 16
        blk.sync(s_l)
        l_rdy = xq.n

        def v_add(ve, l_rdy=l_rdy):
            ve.wait_ge(xq.h, l_rdy)
            ev.inc(ve.tensor_add(p.xres[:, 0:4096], p.xres[:, 0:4096],
                                 p.xnb.bitcast(F32)[:, 0:4096]))
        vec_(p, v_add)
        emit_ln_fm(p, p.xres, None, waits=[(ev.h, ev.n)])
        app = p._a2_x1_done
        p._fin_free = app

        def s_out(se, gidx=gidx, app=app):
            se.wait_ge(ev.h, app)
            se.dma_start(out=fm(p.out_fm, gidx * 512, 512),
                         in_=sb3(p.xres[:, 0:8 * 512], 512)).then_inc(st.h, 16)
            st.n += 16
        blk.sync(s_out)
        p._fin_stp = st.n
    fin_st = st.n

    def g_end(gp, fin_st=fin_st):
        gp.wait_ge(st.h, fin_st)
    blk.gpsimd(g_end)


def kernel(**inputs):
    per_core = prep_inputs(inputs)
    p = build(mode="full")
    emit_stage_a(p)
    p.actx.close()
    alloc_stage_b(p)
    p._wslot_free = 0
    p._wbuf_free = p.sems["mm"].n
    p._p1_x2st_all = 0
    ready_st = p._a2_x1_store
    for i in range(DEPTH):
        halo_dp = emit_cc_x(p, i, ready_st)
        M = emit_p1(p, i, halo_dp)
        p._p1_x2st_all = p.sems["st"].n
        ready_st = emit_p2(p, i)
        p._wbuf_free = p.sems["mm"].n
    p._x6_st = ready_st
    emit_final(p)
    p.ctx.close()

    keys = ["x_fm", "wqT", "wkT", "wvT", "woT", "bq128", "bk128", "bo128",
            "bkrow", "bvrow", "sel", "invrow",
            "wqkv", "wo_b", "w1", "w2", "ln_g", "ln_b", "b2c", "b1c",
            "qsp", "ksp", "tab_cq", "tab_sq", "tab_ck", "tab_sk",
            "maskstd", "maskfirst", "selh"]
    in_maps = [{k: pc[k] for k in keys} for pc in per_core]
    res = run_bass_kernel_spmd(p.nc, in_maps, core_ids=list(range(NCORES)))
    outs = [res.results[c]["out_fm"] for c in range(NCORES)]
    full = np.concatenate([o.T for o in outs], axis=0)
    return np.ascontiguousarray(full.reshape(B, TSEQ, D))



# version 02
# speedup vs baseline: 1.1090x; 1.1090x over previous
import contextlib
import os
import numpy as np
import ml_dtypes
import concourse.bass as bass
import concourse.mybir as mybir
from concourse.bass_utils import run_bass_kernel_spmd

F32 = mybir.dt.float32
BF16 = mybir.dt.bfloat16
AF = mybir.ActivationFunctionType
ALU = mybir.AluOpType
AX = mybir.AxisListType.X
BF16NP = ml_dtypes.bfloat16

NCORES = 8
D = 1024
H = 16
HD = 64
OC = 8
T_OWN = 2048
HALO = 384
T_EXT = T_OWN + HALO
B = 2
TSEQ = 8192
EPS_A = 1e-6
DEPTH = 6
DI = 2816
DH = 2 * DI
DI_REAL = 2730
NEG_BIG = -1e30
RG = [[0, 1, 2, 3], [4, 5, 6, 7]]



def _perm_qk():
    p = np.zeros(D, np.int64)
    for c in range(OC):
        ha, hb = 2 * c, 2 * c + 1
        b0 = c * 128
        p[b0 + 0:b0 + 64] = ha * HD + np.arange(64)
        p[b0 + 64:b0 + 128] = hb * HD + np.arange(64)
    return p


PERM_QK = _perm_qk()


def _core_tables(core):
    half = np.arange(0, HD, 2, dtype=np.float32)
    inv_freq = 1.0 / 10000.0 ** (half / HD)
    pos0 = (core % 4) * T_OWN
    idx = np.clip(pos0 - HALO + np.arange(T_EXT), 0, TSEQ - 1).astype(np.float32)
    f = idx[:, None] * inv_freq[None, :]
    base_scale = (half + 0.4 * HD) / (1.4 * HD)
    power = (idx - TSEQ // 2) / 128.0
    sc = base_scale[None, :] ** power[:, None]
    cos, sin = np.cos(f), np.sin(f)
    inv = 1.0 / sc

    def build(ct, st):
        o = np.empty((128, T_EXT), np.float32)
        o[0:32] = ct.T; o[32:64] = ct.T; o[64:96] = ct.T; o[96:128] = ct.T
        s = np.empty((128, T_EXT), np.float32)
        s[0:32] = -st.T; s[32:64] = st.T; s[64:96] = -st.T; s[96:128] = st.T
        return o, s

    cq, sq = build(cos * sc, sin * sc)
    ck, sk = build(cos * inv, sin * inv)
    return cq, sq, ck, sk


def _fmcols(v):
    return np.ascontiguousarray(v.reshape(OC, 128).T)


def prep_inputs(inputs):
    x = np.ascontiguousarray(inputs["L0_attn_x"].astype(np.float32).reshape(B * TSEQ, D))
    shared = {
        "wqT": np.ascontiguousarray(inputs["L0_attn_Wq"].T),
        "wkT": np.ascontiguousarray(inputs["L0_attn_Wk"].T),
        "wvT": np.ascontiguousarray(inputs["L0_attn_Wv"].T),
        "woT": np.ascontiguousarray(inputs["L0_attn_Wo"].T),
        "bq128": _fmcols(inputs["L0_attn_bq"]),
        "bk128": _fmcols(inputs["L0_attn_bk"]),
        "bo128": _fmcols(inputs["L0_attn_bo"]),
        "bkrow": inputs["L0_attn_bk"].reshape(1, D),
        "bvrow": inputs["L0_attn_bv"].reshape(1, D),
    }
    wqkv = inputs["L0_ffn_Wqkv"]
    wq_b = wqkv[:, :, 0:D][:, :, PERM_QK]
    wk_b = wqkv[:, :, D:2 * D][:, :, PERM_QK]
    wv_b = wqkv[:, :, 2 * D:3 * D]
    shared["wqkv"] = np.ascontiguousarray(np.concatenate([wq_b, wk_b, wv_b], axis=2))
    shared["wo_b"] = np.ascontiguousarray(inputs["L0_ffn_Wo"])
    w1 = inputs["L0_ffn_W1"]; b1 = inputs["L0_ffn_b1"]
    w1p = np.zeros((DEPTH, D, DH), np.float32)
    w1p[:, :, 0:DI_REAL] = w1[:, :, 0:DI_REAL]
    w1p[:, :, DI:DI + DI_REAL] = w1[:, :, DI_REAL:2 * DI_REAL]
    shared["w1"] = np.ascontiguousarray(w1p)
    b1p = np.zeros((DEPTH, DH), np.float32)
    b1p[:, 0:DI_REAL] = b1[:, 0:DI_REAL]
    b1p[:, DI:DI + DI_REAL] = b1[:, DI_REAL:2 * DI_REAL]
    shared["b1c"] = np.ascontiguousarray(b1p.reshape(DEPTH, 44, 128).transpose(0, 2, 1))
    w2p = np.zeros((DEPTH, DI, D), np.float32)
    w2p[:, 0:DI_REAL, :] = inputs["L0_ffn_W2"]
    shared["w2"] = w2p
    shared["ln_g"] = np.stack([_fmcols(inputs["L0_ffn_ln_g"][i]) for i in range(DEPTH)])
    shared["ln_b"] = np.stack([_fmcols(inputs["L0_ffn_ln_b"][i]) for i in range(DEPTH)])
    shared["b2c"] = np.stack([_fmcols(inputs["L0_ffn_b2"][i]) for i in range(DEPTH)])
    src_rows = np.concatenate([np.arange(64), np.arange(64)])
    ha = np.zeros(128, np.float32); ha[0:64] = 1
    qsp = np.zeros((DEPTH, 2, 128), np.float32)
    ksp = np.zeros((DEPTH, 2, 128), np.float32)
    for i in range(DEPTH):
        qs = inputs["L0_ffn_q_scale"][i]; ks = inputs["L0_ffn_k_scale"][i]
        qsp[i, 0] = 8.0 * qs[src_rows] * ha; qsp[i, 1] = 8.0 * qs[src_rows] * (1 - ha)
        ksp[i, 0] = ks[src_rows] * ha; ksp[i, 1] = ks[src_rows] * (1 - ha)
    shared["qsp"] = qsp
    shared["ksp"] = ksp
    ii = np.arange(128)[:, None]; ss = np.arange(384)[None, :]
    mstd = np.where((ss >= ii) & (ss <= ii + 256), 0.0, NEG_BIG).astype(np.float32)
    shared["maskstd"] = mstd

    per_core = []
    for core in range(NCORES):
        s = core * T_OWN
        cq, sq, ck, sk = _core_tables(core)
        sel = np.zeros((128, 4), np.float32)
        sel[:, 0:core % 4] = 1.0
        selh = np.zeros((128, 4), np.float32)
        if core % 4 > 0:
            selh[:, core % 4 - 1] = 1.0
        inv = np.zeros((1, T_EXT), np.float32)
        if core % 4 == 0:
            inv[0, 0:HALO] = 1.0
        mfirst = np.stack([shared["maskstd"]] * 3)
        if core % 4 == 0:
            ii = np.arange(128)[:, None]; ss = np.arange(384)[None, :]
            for j in range(3):
                kb = 128 * j - 384 + ss
                mfirst[j] = np.where(kb < 0, NEG_BIG, mfirst[j])
        d = dict(shared)
        d["x_fm"] = np.ascontiguousarray(x[s:s + T_OWN].T)
        d["tab_cq"] = cq; d["tab_sq"] = sq; d["tab_ck"] = ck; d["tab_sk"] = sk
        d["sel"] = sel
        d["selh"] = selh
        d["invrow"] = inv
        d["maskfirst"] = np.ascontiguousarray(mfirst)
        per_core.append(d)
    return per_core



class Sem:
    __slots__ = ("h", "n")

    def __init__(self, h):
        self.h = h
        self.n = 0

    def inc(self, ins, k=1):
        if k == 1:
            ins.then_inc(self.h)
        else:
            ins.then_inc(self.h, k)
        self.n += k
        return self.n


class P:
    pass


def fm(ap_dram, c0, w, nchunk=OC):
    return ap_dram.rearrange("(c p) t -> p c t", p=128)[:, :, c0:c0 + w]


def sb3(ap_sb, w):
    return ap_sb.rearrange("p (c t) -> p c t", t=w)


def wview(w_dram):
    return w_dram.rearrange("(c p) o -> p c o", p=128)



def vec_(p, f):
    n = p.sems["ev"].n
    h = p.sems["ev"].h

    def g(ve):
        ve.wait_ge(h, n)
        f(ve)
    p.block.vector(g)


def scl_(p, f):
    n = p.sems["ac"].n
    h = p.sems["ac"].h

    def g(se):
        se.wait_ge(h, n)
        f(se)
    p.block.scalar(g)


def build(mode="A"):
    ctx = contextlib.ExitStack()
    import os
    drc = os.environ.get("MK_DETECT_RACES", "0") == "1"
    nc = bass.Bass("TRN2", target_bir_lowering=False, debug=False,
                   detect_race_conditions=drc)
    p = P()
    p.nc, p.ctx, p.mode = nc, ctx, mode

    din = lambda n, s, dt=F32: nc.dram_tensor(n, s, dt, kind="ExternalInput").ap()
    _dbg = os.environ.get("MK_DBG_DRAM", "0") == "1"
    dint = lambda n, s, dt=F32: nc.dram_tensor(
        n, s, dt, kind="ExternalOutput" if _dbg and not n.startswith("cc_") else "Internal").ap()

    p.x_fm = din("x_fm", [D, T_OWN])
    p.wqT = din("wqT", [D, D]); p.wkT = din("wkT", [D, D])
    p.wvT = din("wvT", [D, D]); p.woT = din("woT", [D, D])
    p.bq128 = din("bq128", [128, OC]); p.bk128 = din("bk128", [128, OC])
    p.bo128 = din("bo128", [128, OC])
    p.bkrow = din("bkrow", [1, D]); p.bvrow = din("bvrow", [1, D])
    p.sel = din("sel", [128, 4])
    p.invrow = din("invrow", [1, T_EXT])
    if mode != "A":
        p.wqkv = din("wqkv", [DEPTH, D, 3 * D])
        p.wo_b = din("wo_b", [DEPTH, D, D])
        p.w1 = din("w1", [DEPTH, D, DH])
        p.w2 = din("w2", [DEPTH, DI, D])
        p.ln_g = din("ln_g", [DEPTH, 128, OC]); p.ln_b = din("ln_b", [DEPTH, 128, OC])
        p.b2c = din("b2c", [DEPTH, 128, OC]); p.b1c = din("b1c", [DEPTH, 128, 44])
        p.qsp = din("qsp", [DEPTH, 2, 128]); p.ksp = din("ksp", [DEPTH, 2, 128])
        p.maskstd = din("maskstd", [128, 384])
        p.maskfirst = din("maskfirst", [3, 128, 384])
        p.selh = din("selh", [128, 4])
        p.tab_cq = din("tab_cq", [128, T_EXT]); p.tab_sq = din("tab_sq", [128, T_EXT])
        p.tab_ck = din("tab_ck", [128, T_EXT]); p.tab_sk = din("tab_sk", [128, T_EXT])

    p.out_fm = nc.dram_tensor("out_fm", [D, T_OWN], F32, kind="ExternalOutput").ap()

    p.Xs = dint("Xs", [D, T_EXT])
    p.x1s = dint("x1s", [D, T_OWN])
    p.cc_kv_in = dint("cc_kv_in", [1040, 64])
    p.cc_kv_out = dint("cc_kv_out", [4160, 64])
    if mode != "A":
        p.X2s = dint("X2s", [D, T_EXT])
        p.ms = dint("ms", [64, 128])
        p.gh = dint("gh", [DI, T_OWN], BF16)
        p.cc_x_in = dint("cc_x_in", [D, HALO])
        p.cc_x_out = dint("cc_x_out", [4 * D, HALO])

    p.c_ident = nc.inline_tensor(np.eye(128, dtype=BF16NP), "c_ident").ap()
    p.c_maskA = nc.inline_tensor(np.triu(np.ones((128, 128), np.float32)).astype(BF16NP), "c_maskA").ap()
    if mode != "A":
        p.c_maskB0 = nc.inline_tensor(np.tril(np.ones((128, 128), np.float32)).astype(BF16NP), "c_maskB0").ap()
        p.c_maskB2 = nc.inline_tensor(np.triu(np.ones((128, 128), np.float32)).astype(BF16NP), "c_maskB2").ap()

    p.actx = contextlib.ExitStack()
    sb = lambda n, s, dt=F32: ctx.enter_context(nc.sbuf_tensor(n, s, dt))
    sba = lambda n, s, dt=F32: p.actx.enter_context(nc.sbuf_tensor(n, s, dt))
    psa = lambda n, s, dt=F32: ctx.enter_context(nc.psum_tensor(n, s, dt))

    p.ps0 = psa("ps0", [128, 512]); p.ps1 = psa("ps1", [128, 512])
    p.ps2 = psa("ps2", [128, 512]); p.ps3 = psa("ps3", [128, 512])
    p.ps4 = psa("ps4", [128, 512]); p.ps5 = psa("ps5", [128, 512])
    p.ps6 = psa("ps6", [128, 512])
    p.ps7 = psa("ps7", [128, 512])

    p.wbuf = sb("wbuf", [128, OC * D], BF16)
    p.xb = sb("xb", [128, OC * 512], BF16)
    p.scr = sb("scr", [128, 3 * 512])
    p.brow = sb("brow", [1, 2 * D], BF16)
    p.bcolq = sb("bcolq", [128, OC]); p.bcolk = sb("bcolk", [128, OC])
    p.bcolo = sb("bcolo", [128, OC])
    p.onesb = sb("onesb", [128, 1], BF16)
    p.onesf = sb("onesf", [128, 1])
    p.onesrow_b = sb("onesrow_b", [1, 128], BF16)
    p.onesrow_f = sb("onesrow_f", [1, 128])
    p.selb = sb("selb", [128, 4])
    p.afm = sb("afm", [128, OC * 512], BF16)
    p.xres = sb("xres", [128, OC * 512])
    p.xsq = sb("xsq", [128, 512])
    p.mident = sb("mident", [128, 128], BF16)
    p.cvals = sb("cvals", [128, 2])
    nc.const_aps.register(F32, 1e-5, p.cvals[:, 0:1])
    nc.const_aps.register(F32, -0.5, p.cvals[:, 1:2])
    p.kvpre = p.scr[:, 0:1024]

    p.qA = sba("qA", [128, OC * T_OWN], BF16)
    p.kA = sba("kA", [128, OC * T_OWN], BF16)
    p.ktokA = sba("ktokA", [128, 16 * D], BF16)
    p.vtokA = sba("vtokA", [128, 16 * D], BF16)
    p.kvseg = sba("kvseg", [64, 512])
    p.kvst = sba("kvst", [128, 8 * 64])
    p.kvstb = sba("kvstb", [128, 8 * 64], BF16)
    p.ksst = sba("ksst", [128, 8]); p.ksstb = sba("ksstb", [128, 8], BF16)
    p.ksst2 = sba("ksst2", [128, 8])
    p.kspre = sba("kspre", [128, 32])
    p.smt = sba("smt", [128, 2 * 128], BF16)
    p.atok = sba("atok", [128, 4 * D], BF16)
    p.rden = sba("rden", [128, 16])
    p.mmaskA = sba("mmaskA", [128, 128], BF16)

    p.sems = {}
    for name in ["ld", "wx", "mm", "ev", "pl", "ac", "cc", "st", "dp", "xr", "fn", "tb", "lv", "w2"]:
        p.sems[name] = Sem(ctx.enter_context(nc.semaphore("s_" + name)))

    p.block = ctx.enter_context(nc.Block())
    return p



def emit_stage_a(p):
    blk = p.block
    ld, wx, mm, ev, pl, ac, cc, st, dp = (p.sems[k] for k in
        ["ld", "wx", "mm", "ev", "pl", "ac", "cc", "st", "dp"])

    def g_ld(gp):
        gp.dma_start(out=p.mident[:, :], in_=p.c_ident).then_inc(ld.h, 16); ld.n += 16
        gp.dma_start(out=p.mmaskA[:, :], in_=p.c_maskA).then_inc(ld.h, 16); ld.n += 16
        gp.dma_start(out=p.bcolq[:, :], in_=p.bq128).then_inc(ld.h, 16); ld.n += 16
        gp.dma_start(out=p.bcolk[:, :], in_=p.bk128).then_inc(ld.h, 16); ld.n += 16
        gp.dma_start(out=p.bcolo[:, :], in_=p.bo128).then_inc(ld.h, 16); ld.n += 16
        gp.dma_start(out=p.brow[:, 0:D], in_=p.bkrow).then_inc(ld.h, 16); ld.n += 16
        gp.dma_start(out=p.brow[:, D:2 * D], in_=p.bvrow).then_inc(ld.h, 16); ld.n += 16
        gp.dma_start(out=p.selb[:, :], in_=p.sel).then_inc(ld.h, 16); ld.n += 16
        gp.dma_start(out=sb3(p.wbuf[:, :], D), in_=wview(p.wqT)).then_inc(ld.h, 16); ld.n += 16
    blk.gpsimd(g_ld)
    ld_init = ld.n

    def v_ones(ve):
        ve.memset(p.cvals[:, 0:1], 1e-5)
        ve.memset(p.cvals[:, 1:2], -0.5)
        ve.memset(p.onesb[:, :], 1.0)
        ve.memset(p.onesf[:, :], 1.0)
        ve.memset(p.onesrow_b[:, :], 1.0)
        ev.inc(ve.memset(p.onesrow_f[:, :], 1.0))
    vec_(p, v_ones)
    ones_rdy = ev.n

    st8 = {"evlast": None, "aclast": None, "xfree": 0, "wrdy": ld_init}

    def emit_fm_pass(dst, bcol, has_phi=True):
        for ti in range(4):
            def s_x(se, ti=ti):
                se.wait_ge(mm.h, st8["xfree"])
                se.dma_start(out=sb3(p.xb[:, :], 512), in_=fm(p.x_fm, ti * 512, 512)).then_inc(wx.h, 16)
                wx.n += 16
            blk.gpsimd(s_x)
            xr = wx.n
            for oc in range(OC):
                def t_p(te, oc=oc, ti=ti, xr=xr, wrdy=st8["wrdy"], g=st8["evlast"]):
                    te.wait_ge(wx.h, xr)
                    if ti == 0 and oc == 0:
                        te.wait_ge(ld.h, wrdy)
                        te.wait_ge(ev.h, ones_rdy)
                    if g is not None:
                        te.wait_ge(ev.h, g)
                    for dc in range(OC):
                        ins = te.matmul(
                            p.ps0[:, 0:512],
                            p.wbuf[:, dc * D + oc * 128: dc * D + oc * 128 + 128],
                            p.xb[:, dc * 512:(dc + 1) * 512],
                            start=(dc == 0), stop=(dc == OC - 1))
                    ins.then_inc(mm.h); mm.n += 1
                blk.tensor(t_p)
                mmv = mm.n

                def v_p(ve, oc=oc, bcol=bcol, mmv=mmv, g=st8["aclast"]):
                    ve.wait_ge(mm.h, mmv)
                    if g is not None:
                        ve.wait_ge(ac.h, g)
                    ve.tensor_scalar(p.scr[:, 0:512], p.ps0[:, 0:512],
                                     bcol[:, oc:oc + 1], 0.0, ALU.add, ALU.max)
                    ev.inc(ve.tensor_scalar(p.scr[:, 512:1024], p.ps0[:, 0:512],
                                            bcol[:, oc:oc + 1], 0.0, ALU.add, ALU.min))
                vec_(p, v_p)
                st8["evlast"] = ev.n
                evv = ev.n

                def a_p(se, evv=evv):
                    se.wait_ge(ev.h, evv)
                    ac.inc(se.activation(p.scr[:, 1024:1536], p.scr[:, 512:1024], AF.Exp))
                scl_(p, a_p)
                st8["aclast"] = ac.n
                acv = ac.n

                def v_p2(ve, oc=oc, ti=ti, acv=acv, dst=dst):
                    ve.wait_ge(ac.h, acv)
                    ev.inc(ve.tensor_add(
                        dst[:, oc * T_OWN + ti * 512: oc * T_OWN + ti * 512 + 512],
                        p.scr[:, 0:512], p.scr[:, 1024:1536]))
                vec_(p, v_p2)
            st8["xfree"] = mm.n

    def emit_tok_pass(dst, brow_off, has_phi):
        for ti in range(4):
            def s_x(se, ti=ti):
                se.wait_ge(mm.h, st8["xfree"])
                se.dma_start(out=sb3(p.xb[:, :], 512), in_=fm(p.x_fm, ti * 512, 512)).then_inc(wx.h, 16)
                wx.n += 16
            blk.gpsimd(s_x)
            xr = wx.n
            for j in range(4):
                tt = ti * 4 + j
                for half in range(2):
                    def t_t(te, j=j, half=half, ti=ti, xr=xr, wrdy=st8["wrdy"], g=st8["evlast"]):
                        te.wait_ge(wx.h, xr)
                        if ti == 0 and j == 0 and half == 0:
                            te.wait_ge(ld.h, wrdy)
                        if g is not None:
                            te.wait_ge(ev.h, g)
                        for dc in range(OC):
                            te.matmul(
                                p.ps0[:, 0:512],
                                p.xb[:, dc * 512 + j * 128: dc * 512 + j * 128 + 128],
                                p.wbuf[:, dc * D + half * 512: dc * D + half * 512 + 512],
                                start=(dc == 0), stop=False,
                            )
                        te.matmul(
                            p.ps0[:, 0:512],
                            p.onesrow_b[0:1, :],
                            p.brow[0:1, brow_off + half * 512: brow_off + half * 512 + 512],
                            start=False, stop=True,
                        ).then_inc(mm.h); mm.n += 1
                    blk.tensor(t_t)
                    mmv = mm.n

                    if has_phi:
                        def v_t(ve, mmv=mmv, g=st8["aclast"]):
                            ve.wait_ge(mm.h, mmv)
                            if g is not None:
                                ve.wait_ge(ac.h, g)
                            ve.tensor_scalar(p.scr[:, 0:512], p.ps0[:, 0:512], 0.0, None, ALU.max)
                            ev.inc(ve.tensor_scalar(p.scr[:, 512:1024], p.ps0[:, 0:512], 0.0, None, ALU.min))
                        vec_(p, v_t)
                        st8["evlast"] = ev.n
                        evv = ev.n

                        def a_t(se, evv=evv):
                            se.wait_ge(ev.h, evv)
                            ac.inc(se.activation(p.scr[:, 1024:1536], p.scr[:, 512:1024], AF.Exp))
                        scl_(p, a_t)
                        st8["aclast"] = ac.n
                        acv = ac.n

                        def v_t2(ve, tt=tt, half=half, acv=acv, dst=dst):
                            ve.wait_ge(ac.h, acv)
                            ev.inc(ve.tensor_add(
                                dst[:, tt * D + half * 512: tt * D + half * 512 + 512],
                                p.scr[:, 0:512], p.scr[:, 1024:1536]))
                        vec_(p, v_t2)
                    else:
                        def v_t(ve, tt=tt, half=half, mmv=mmv, dst=dst):
                            ve.wait_ge(mm.h, mmv)
                            ev.inc(ve.tensor_copy(
                                dst[:, tt * D + half * 512: tt * D + half * 512 + 512],
                                p.ps0[:, 0:512]))
                        vec_(p, v_t)
                        st8["evlast"] = ev.n
            st8["xfree"] = mm.n

    def load_w(src):
        mm_before = mm.n

        def g_w(gp, mm_before=mm_before):
            gp.wait_ge(mm.h, mm_before)
            gp.dma_start(out=sb3(p.wbuf[:, :], D), in_=wview(src)).then_inc(ld.h, 16)
            ld.n += 16
        blk.gpsimd(g_w)
        st8["wrdy"] = ld.n

    emit_fm_pass(p.qA, p.bcolq)
    load_w(p.wkT)
    emit_fm_pass(p.kA, p.bcolk)
    emit_tok_pass(p.ktokA, 0, True)
    load_w(p.wvT)
    emit_tok_pass(p.vtokA, D, False)
    a1_done_ev = ev.n
    a1_done_mm = mm.n

    def t_seg(te):
        te.wait_ge(ev.h, a1_done_ev)
        for h in range(H):
            bank = p.ps4 if h < 8 else p.ps5
            col = (h % 8) * 64
            for j in range(16):
                ins = te.matmul(
                    bank[0:64, col:col + 64],
                    p.ktokA[:, j * D + h * HD: j * D + h * HD + 64],
                    p.vtokA[:, j * D + h * HD: j * D + h * HD + 64],
                    start=(j == 0), stop=(j == 15),
                )
        ins.then_inc(mm.h); mm.n += 1
    blk.tensor(t_seg)
    seg_mm = mm.n

    def v_seg(ve):
        ve.wait_ge(ev.h, a1_done_ev)
        for oc in range(OC):
            ve.reduce_sum(p.ksst[:, oc:oc + 1], p.kA[:, oc * T_OWN:(oc + 1) * T_OWN], axis=AX)
        ve.wait_ge(mm.h, seg_mm)
        ev.inc(ve.tensor_copy(p.kvseg[0:64, 0:512], p.ps4[0:64, 0:512]))
    vec_(p, v_seg)
    seg_ev1 = ev.n

    import os
    if os.environ.get("MK_CUT_CC", "0") == "1":
        def v_zero(ve):
            ve.memset(p.kvst[:, :], 0.0)
            ve.memset(p.ksst[:, :], 0.0)
            ve.memset(p.kvstb[:, :], 0.0)
            ev.inc(ve.memset(p.ksstb[:, :], 0.0))
        vec_(p, v_zero)
        state_rdy = ev.n
        if os.environ.get("MK_CUT_A2", "0") != "1":
            emit_a2(p, state_rdy, a1_done_mm)
        return

    def g_cc1(gp):
        gp.wait_ge(ev.h, seg_ev1)
        gp.dma_start(
            out=p.cc_kv_in[0:512, :].rearrange("(hh d) e -> d hh e", hh=8),
            in_=sb3(p.kvseg[0:64, :], 64)).then_inc(dp.h, 16); dp.n += 16
        gp.dma_start(
            out=p.cc_kv_in[1024:1040, :].rearrange("r (pp c) -> (r pp) c", c=8),
            in_=p.ksst[:, :]).then_inc(dp.h, 16); dp.n += 16
    blk.gpsimd(g_cc1)
    dma1 = dp.n

    def v_seg2(ve):
        ve.wait_ge(dp.h, dma1)
        ev.inc(ve.tensor_copy(p.kvseg[0:64, 0:512], p.ps5[0:64, 0:512]))
    vec_(p, v_seg2)
    seg_ev2 = ev.n

    def g_cc(gp):
        gp.wait_ge(ev.h, seg_ev2)
        gp.dma_start(
            out=p.cc_kv_in[512:1024, :].rearrange("(hh d) e -> d hh e", hh=8),
            in_=sb3(p.kvseg[0:64, :], 64)).then_inc(dp.h, 16); dp.n += 16
        gp.wait_ge(dp.h, dp.n)
        cc.inc(gp.collective_compute(
            "AllGather", ALU.bypass, replica_groups=RG,
            ins=[p.cc_kv_in.opt()], outs=[p.cc_kv_out.opt()]))
        gp.wait_ge(cc.h, cc.n)
        for r in range(4):
            b0 = r * 1040
            src2 = p.cc_kv_out[b0:b0 + 1024, :].rearrange("(j t d) e -> t d j e", j=8, t=2)
            if r >= 2:
                gp.wait_ge(ev.h, p._cc_stt[r - 2])
            sl = (r % 2) * 512
            gp.dma_start(out=sb3(p.kvpre[0:64, sl:sl + 512], 64), in_=src2[0]).then_inc(dp.h, 16); dp.n += 16
            gp.dma_start(out=sb3(p.kvpre[64:128, sl:sl + 512], 64), in_=src2[1]).then_inc(dp.h, 16); dp.n += 16
            gp.dma_start(out=p.kspre[:, r * 8:(r + 1) * 8],
                         in_=p.cc_kv_out[b0 + 1024:b0 + 1040, :].rearrange(
                             "r (pp c) -> (r pp) c", c=8)).then_inc(dp.h, 16); dp.n += 16
            gp.wait_ge(dp.h, dp.n)
            p._cc_dp[r] = dp.n
    p._cc_dp = {}
    p._cc_stt = {r: ev.n + 2 + 2 * (r + 1) for r in range(4)}
    blk.gpsimd(g_cc)

    def v_cc(ve):
        ev.inc(ve.memset(p.kvst[:, :], 0.0))
        ev.inc(ve.memset(p.ksst2[:, :], 0.0))
        for r in range(4):
            sl = (r % 2) * 512
            ve.wait_ge(dp.h, p._cc_dp[r])
            ve.wait_ge(ev.h, ev.n)
            ev.inc(ve.scalar_tensor_tensor(p.kvst[:, :], p.kvpre[:, sl:sl + 512], p.selb[:, r:r + 1],
                                           p.kvst[:, :], ALU.mult, ALU.add))
            ve.wait_ge(ev.h, ev.n)
            ev.inc(ve.scalar_tensor_tensor(p.ksst2[:, :], p.kspre[:, r * 8:(r + 1) * 8],
                                           p.selb[:, r:r + 1], p.ksst2[:, :], ALU.mult, ALU.add))
            assert p._cc_stt[r] == ev.n
        ve.wait_ge(ev.h, ev.n)
        ev.inc(ve.tensor_copy(p.ksst[:, :], p.ksst2[:, :]))
        ve.wait_ge(ev.h, ev.n)
        ve.tensor_copy(p.kvstb[:, :], p.kvst[:, :])
        ev.inc(ve.tensor_copy(p.ksstb[:, :], p.ksst[:, :]))
    vec_(p, v_cc)
    state_rdy = ev.n
    import os as _os
    if _os.environ.get("MK_CUT_A2", "0") == "1":
        return


    emit_a2(p, state_rdy, a1_done_mm)

    a_end = {k: p.sems[k].n for k in ("mm", "ev", "ac", "pl")}

    fn = p.sems["fn"]

    def g_fence(gp, a_end=a_end):
        for k in ("mm", "ev", "ac", "pl"):
            gp.wait_ge(p.sems[k].h, a_end[k])
        gp.sem_inc(fn.h, 1)
    blk.gpsimd(g_fence)
    fn.n += 1
    p._fence_ld = fn.n


def emit_a2(p, state_rdy, a1_done_mm):
    import os
    NCH = int(os.environ.get("MK_A2_NCH", "16"))
    DO_WOLN = os.environ.get("MK_A2_WOLN", "1") == "1"
    blk = p.block
    ld, wx, mm, ev, pl, ac, cc, st, dp = (p.sems[k] for k in
        ["ld", "wx", "mm", "ev", "pl", "ac", "cc", "st", "dp"])

    def g_wo(gp):
        gp.wait_ge(mm.h, a1_done_mm)
        gp.dma_start(out=sb3(p.wbuf[:, :], D), in_=wview(p.woT)).then_inc(ld.h, 16)
        ld.n += 16
    blk.gpsimd(g_wo)
    wo_rdy = ld.n

    sev = {}
    smt_free = {}
    vdiv_ev = {}
    vu_ev = {}

    for c in range(NCH):
        for h in range(H):
            par = h % 2
            sbank = p.ps0 if h % 2 == 0 else p.ps1
            sslot = h % 2

            def t_s(te, c=c, h=h, par=par, sbank=sbank):
                if c == 0 and h == 0:
                    te.wait_ge(ev.h, state_rdy)
                if h == 0:
                    g = vdiv_ev.get(c - 1)
                    if g is not None:
                        te.wait_ge(ev.h, g)
                    gu = vu_ev.get(c - 1)
                    if gu is not None:
                        te.wait_ge(ev.h, gu)
                    if c % 4 == 0 and c > 0 and getattr(p, "_a2_x1_done", None) is not None:
                        te.wait_ge(ev.h, p._a2_x1_done)
                g = sev.get((c, h - 2)) or (sev.get((c - 1, h + 14)) if h < 2 else None)
                if g is not None:
                    te.wait_ge(ev.h, g)
                te.matmul(
                    sbank[:, 0:128],
                    p.kA[par * 64:(par + 1) * 64, (h // 2) * T_OWN + c * 128:(h // 2) * T_OWN + c * 128 + 128],
                    p.qA[par * 64:(par + 1) * 64, (h // 2) * T_OWN + c * 128:(h // 2) * T_OWN + c * 128 + 128],
                    start=True, stop=True,
                ).then_inc(mm.h); mm.n += 1
            blk.tensor(t_s)
            s_mm = mm.n

            def v_m(ve, s_mm=s_mm, sbank=sbank, sslot=sslot, c=c, h=h):
                ve.wait_ge(mm.h, s_mm)
                g = smt_free.get((c, h - 2)) or (smt_free.get((c - 1, h + 14)) if h < 2 else None)
                if g is not None:
                    ve.wait_ge(mm.h, g)
                ev.inc(ve.tensor_tensor(
                    p.smt[:, sslot * 128:(sslot + 1) * 128],
                    sbank[:, 0:128], p.mmaskA[:, :], ALU.mult))
            vec_(p, v_m)
            sev[(c, h)] = ev.n
            smt_rdy = ev.n

            numbank = p.ps2 if h < 8 else p.ps3
            ncol = (h % 8) * 64

            def t_n(te, c=c, h=h, par=par, sslot=sslot, numbank=numbank, ncol=ncol,
                    smt_rdy=smt_rdy):
                te.wait_ge(ev.h, smt_rdy)
                te.matmul(numbank[:, ncol:ncol + 64],
                          p.smt[:, sslot * 128:(sslot + 1) * 128],
                          p.vtokA[:, c * D + h * HD: c * D + h * HD + 64],
                          start=True, stop=False)
                te.matmul(numbank[:, ncol:ncol + 64],
                          p.qA[par * 64:(par + 1) * 64, (h // 2) * T_OWN + c * 128:(h // 2) * T_OWN + c * 128 + 128],
                          p.kvstb[par * 64:(par + 1) * 64, (h // 2) * 64:(h // 2) * 64 + 64],
                          start=False, stop=True)
                te.matmul(p.ps6[:, h:h + 1],
                          p.smt[:, sslot * 128:(sslot + 1) * 128],
                          p.onesb[:, 0:1], start=True, stop=False)
                te.matmul(p.ps6[:, h:h + 1],
                          p.qA[par * 64:(par + 1) * 64, (h // 2) * T_OWN + c * 128:(h // 2) * T_OWN + c * 128 + 128],
                          p.ksstb[par * 64:(par + 1) * 64, h // 2:h // 2 + 1],
                          start=False, stop=True).then_inc(mm.h)
                mm.n += 1
            blk.tensor(t_n)
            smt_free[(c, h)] = mm.n
        ph1_mm = mm.n

        def v_dd(ve, c=c, ph1_mm=ph1_mm):
            ve.wait_ge(mm.h, ph1_mm)
            ev.inc(ve.tensor_scalar(p.rden[:, 0:16], p.ps6[:, 0:16], EPS_A, None, ALU.add))
            ve.wait_ge(ev.h, ev.n)
            ev.inc(ve.reciprocal(p.rden[:, 0:16], p.rden[:, 0:16]))
            ve.wait_ge(ev.h, ev.n)
            for h in range(H):
                numbank = p.ps2 if h < 8 else p.ps3
                ncol = (h % 8) * 64
                ins = ve.tensor_scalar(
                    p.atok[:, (c % 4) * D + h * HD:(c % 4) * D + h * HD + 64],
                    numbank[:, ncol:ncol + 64],
                    p.rden[:, h:h + 1], None, ALU.mult)
            ev.inc(ins)
        vec_(p, v_dd)
        vdiv_ev[c] = ev.n
        atok_rdy = ev.n

        def t_u(te, c=c, ph1_mm=ph1_mm):
            g = vu_ev.get(c - 1)
            if g is not None:
                te.wait_ge(ev.h, g)
            if c % 4 == 0 and c > 0 and getattr(p, "_a2_ln_var", None) is not None:
                te.wait_ge(ev.h, p._a2_ln_var)
            for h in range(H):
                par = h % 2
                te.matmul(p.ps4[par * 64:(par + 1) * 64, (h // 2) * 64:(h // 2) * 64 + 64],
                          p.ktokA[:, c * D + h * HD: c * D + h * HD + 64],
                          p.vtokA[:, c * D + h * HD: c * D + h * HD + 64],
                          start=True, stop=True, tile_position=(0, par * 64))
                ins = te.matmul(p.ps5[par * 64:(par + 1) * 64, h // 2:h // 2 + 1],
                                p.ktokA[:, c * D + h * HD: c * D + h * HD + 64],
                                p.onesb[:, 0:1],
                                start=True, stop=True, tile_position=(0, par * 64))
            ins.then_inc(mm.h); mm.n += 1
        blk.tensor(t_u)
        u_mm = mm.n

        def v_u(ve, c=c, u_mm=u_mm):
            ve.wait_ge(mm.h, u_mm)
            ve.tensor_add(p.kvst[:, :], p.kvst[:, :], p.ps4[:, 0:512])
            ev.inc(ve.tensor_add(p.ksst[:, 0:8], p.ksst[:, 0:8], p.ps5[:, 0:8]))
            ve.wait_ge(ev.h, ev.n)
            ve.tensor_copy(p.kvstb[:, :], p.kvst[:, :])
            ev.inc(ve.tensor_copy(p.ksstb[:, :], p.ksst[:, :]))
        vec_(p, v_u)
        vu_ev[c] = ev.n

        if c % 4 == 3 and DO_WOLN:
            emit_a2_wo_ln(p, c // 4, atok_rdy, wo_rdy, vu_ev[c])


def emit_a2_wo_ln(p, gidx, atok_rdy, wo_rdy, vu_done):
    import os
    blk = p.block
    ld, wx, mm, ev, pl, ac, cc, st, dp = (p.sems[k] for k in
        ["ld", "wx", "mm", "ev", "pl", "ac", "cc", "st", "dp"])
    xq = p.sems["xr"]

    def s_xr(se, gidx=gidx):
        if gidx > 0:
            se.wait_ge(ev.h, p._a2_x1_done)
            if getattr(p, "_a2_x1_store", None) is not None:
                se.wait_ge(st.h, p._a2_x1_store)
        se.dma_start(out=sb3(p.xres[:, :], 512), in_=fm(p.x_fm, gidx * 512, 512)).then_inc(xq.h, 16)
        xq.n += 16
    blk.sync(s_xr)
    xr_rdy = xq.n

    tr_ev0 = ev.n
    tr_mm0 = mm.n

    def t_tr(te, atok_rdy=atok_rdy, gidx=gidx):
        te.wait_ge(ev.h, atok_rdy)
        if gidx > 0:
            te.wait_ge(ev.h, p._a2_afm_free)
        for idx in range(32):
            j, oc = divmod(idx, OC)
            bank = p.ps7 if idx % 2 == 0 else p.ps6
            if idx >= 2:
                te.wait_ge(ev.h, tr_ev0 + idx - 1)
            te.matmul(bank[:, 0:128],
                      p.atok[:, j * D + oc * 128: j * D + oc * 128 + 128],
                      p.mident[:, :], start=True, stop=True).then_inc(mm.h)
            mm.n += 1
    blk.tensor(t_tr)

    def v_tr(ve, tr_mm0=tr_mm0):
        for idx in range(32):
            j, oc = divmod(idx, OC)
            bank = p.ps7 if idx % 2 == 0 else p.ps6
            ve.wait_ge(mm.h, tr_mm0 + idx + 1)
            ev.inc(ve.tensor_copy(
                p.afm[:, oc * 512 + j * 128: oc * 512 + j * 128 + 128],
                bank[:, 0:128]))
    vec_(p, v_tr)
    afm_rdy = ev.n

    wo_ev = {}
    for oco in range(OC):
        bank = p.ps2 if oco % 2 == 0 else p.ps3

        def t_w(te, oco=oco, bank=bank, afm_rdy=afm_rdy):
            te.wait_ge(ev.h, afm_rdy)
            if oco == 0:
                te.wait_ge(ld.h, wo_rdy)
            g = wo_ev.get(oco - 2)
            if g is not None:
                te.wait_ge(ev.h, g)
            for ic in range(OC):
                ins = te.matmul(
                    bank[:, 0:512],
                    p.wbuf[:, ic * D + oco * 128: ic * D + oco * 128 + 128],
                    p.afm[:, ic * 512:(ic + 1) * 512],
                    start=(ic == 0), stop=(ic == OC - 1))
            ins.then_inc(mm.h); mm.n += 1
        blk.tensor(t_w)
        w_mm = mm.n

        def v_w(ve, oco=oco, bank=bank, w_mm=w_mm, xr_rdy=xr_rdy):
            ve.wait_ge(mm.h, w_mm)
            ve.wait_ge(p.sems["xr"].h, xr_rdy)
            ev.inc(ve.scalar_tensor_tensor(
                p.xres[:, oco * 512:(oco + 1) * 512],
                bank[:, 0:512], p.bcolo[:, oco:oco + 1],
                p.xres[:, oco * 512:(oco + 1) * 512],
                ALU.add, ALU.add))
        vec_(p, v_w)
        wo_ev[oco] = ev.n
    p._a2_afm_free = ev.n
    xsum_rdy = ev.n

    emit_ln_fm(p, p.xres, xsum_rdy, x1_store_gidx=gidx, vu_done=vu_done)


def emit_ln_fm(p, src, src_rdy, x1_store_gidx=None, gcol=None, bcol=None,
               out_sb=None, out_cb=None, lvl=4, vu_done=None, waits=()):
    blk = p.block
    mm, ev, ac, st, wx = (p.sems[k] for k in ["mm", "ev", "ac", "st", "wx"])

    pl = p.sems["pl"]
    ac0 = pl.n
    mm0 = mm.n

    def p_sq(po, src_rdy=src_rdy, mm0=mm0):
        if src_rdy is not None:
            po.wait_ge(ev.h, src_rdy)
        for hh, vv in waits:
            po.wait_ge(hh, vv)
        for oc in range(OC):
            if oc > 0:
                po.wait_ge(mm.h, mm0 + oc)
            pl.inc(po.tensor_tensor(p.xsq[:, 0:512], src[:, oc * 512:(oc + 1) * 512],
                                    src[:, oc * 512:(oc + 1) * 512], ALU.mult))
    pol_(p, p_sq)

    def t_sums(te, src_rdy=src_rdy, ac0=ac0, vu_done=vu_done):
        if src_rdy is not None:
            te.wait_ge(ev.h, src_rdy)
        for hh, vv in waits:
            te.wait_ge(hh, vv)
        if vu_done is not None:
            te.wait_ge(ev.h, vu_done)
        for oc in range(OC):
            te.matmul(p.ps4[0:1, 0:512], p.onesf[:, 0:1], src[:, oc * 512:(oc + 1) * 512],
                      start=(oc == 0), stop=(oc == OC - 1))
        for oc in range(OC):
            te.wait_ge(p.sems["pl"].h, ac0 + oc + 1)
            te.matmul(p.ps5[0:1, 0:512], p.onesf[:, 0:1], p.xsq[:, 0:512],
                      start=(oc == 0), stop=(oc == OC - 1)).then_inc(mm.h)
            mm.n += 1
    blk.tensor(t_sums)
    sums_mm = mm.n

    R = p.scr
    def v_r(ve, sums_mm=sums_mm):
        ve.wait_ge(mm.h, sums_mm)
        ev.inc(ve.tensor_scalar(R[0:1, 0:512], p.ps4[0:1, 0:512], 1.0 / D, None, ALU.mult))
        ve.wait_ge(ev.h, ev.n)
        ev.inc(ve.tensor_tensor(R[0:1, 512:1024], R[0:1, 0:512], R[0:1, 0:512], ALU.mult))
        ve.wait_ge(ev.h, ev.n)
        ev.inc(ve.scalar_tensor_tensor(
            R[0:1, 1024:1536], p.ps5[0:1, 0:512], 1.0 / D, R[0:1, 512:1024],
            ALU.mult, ALU.subtract))
    vec_(p, v_r)
    var_ev = ev.n
    p._a2_ln_var = var_ev

    def a_r(se, var_ev=var_ev):
        se.wait_ge(ev.h, var_ev)
        ac.inc(se.activation(R[0:1, 512:1024], R[0:1, 1024:1536], AF.Ln, bias=1e-5))
        se.wait_ge(ac.h, ac.n)
        ac.inc(se.activation(R[0:1, 1024:1536], R[0:1, 512:1024], AF.Exp, scale=-0.5))
    scl_(p, a_r)
    rstd_ac = ac.n

    def v_r2(ve, rstd_ac=rstd_ac):
        ve.wait_ge(ac.h, rstd_ac)
        ev.inc(ve.tensor_tensor(R[0:1, 512:1024], R[0:1, 0:512], R[0:1, 1024:1536], ALU.mult))
    vec_(p, v_r2)
    mur_ev = ev.n
    if lvl < 4:
        p._a2_x1_done = mur_ev
        return mur_ev

    def t_bc(te, mur_ev=mur_ev):
        te.wait_ge(ev.h, mur_ev)
        te.matmul(p.ps0[:, 0:512], p.onesrow_f[0:1, :], R[0:1, 1024:1536], start=True, stop=True)
        te.matmul(p.ps1[:, 0:512], p.onesrow_f[0:1, :], R[0:1, 512:1024],
                  start=True, stop=True).then_inc(mm.h)
        mm.n += 1
    blk.tensor(t_bc)
    bc_mm = mm.n

    dst = out_sb if out_sb is not None else src
    def v_a(ve, bc_mm=bc_mm):
        ve.wait_ge(mm.h, bc_mm)
        for oc in range(OC):
            ev.inc(ve.tensor_tensor(p.xsq[:, 0:512], src[:, oc * 512:(oc + 1) * 512],
                                    p.ps0[:, 0:512], ALU.mult))
            ve.wait_ge(ev.h, ev.n)
            if gcol is None:
                ev.inc(ve.tensor_tensor(dst[:, oc * 512:(oc + 1) * 512],
                                        p.xsq[:, 0:512], p.ps1[:, 0:512], ALU.subtract))
            else:
                ev.inc(ve.tensor_tensor(p.xsq[:, 0:512], p.xsq[:, 0:512], p.ps1[:, 0:512], ALU.subtract))
                ve.wait_ge(ev.h, ev.n)
                ev.inc(ve.tensor_scalar(dst[:, oc * 512:(oc + 1) * 512], p.xsq[:, 0:512],
                                        gcol[:, oc:oc + 1], bcol[:, oc:oc + 1],
                                        ALU.mult, ALU.add))
            ve.wait_ge(ev.h, ev.n)
    vec_(p, v_a)
    app_ev = ev.n
    p._a2_x1_done = app_ev

    if x1_store_gidx is not None:
        gidx = x1_store_gidx
        def s_o(se, gidx=gidx, app_ev=app_ev):
            se.wait_ge(ev.h, app_ev)
            se.dma_start(out=fm(p.x1s, gidx * 512, 512), in_=sb3(dst[:, :], 512)).then_inc(st.h, 16)
            st.n += 16
            if p.mode != "A":
                se.dma_start(out=fm(p.Xs, HALO + gidx * 512, 512), in_=sb3(dst[:, :], 512)).then_inc(st.h, 16)
                st.n += 16
        blk.sync(s_o)
        p._a2_x1_store = st.n
    return app_ev



def _run(p, per_core, outs=("out_fm",)):
    p.ctx.close()
    in_names = {a.memorylocations[0].name if hasattr(a, 'memorylocations') else None
                for a in []}
    res = run_bass_kernel_spmd(p.nc, per_core, core_ids=list(range(NCORES)))
    return res


def kernel_stage_a_test(**inputs):
    per_core = prep_inputs(inputs)
    p = build(mode="A")
    emit_stage_a(p)

    st, dp = p.sems["st"], p.sems["dp"]
    st_all = st.n

    import os
    def g_out(gp, st_all=st_all):
        gp.wait_ge(st.h, st_all)
        if os.environ.get("MK_DBG_ST", "0") == "1":
            gp.dma_start(out=p.out_fm.rearrange("a b -> (a b)")[0:128 * 512].rearrange("(a b) -> a b", b=512),
                         in_=p.kvst[:, :]).then_inc(dp.h, 16); dp.n += 16
            gp.dma_start(out=p.out_fm.rearrange("a b -> (a b)")[70000:70000 + 128 * 8].rearrange("(a b) -> a b", b=8),
                         in_=p.ksst[:, :]).then_inc(dp.h, 16); dp.n += 16
        elif os.environ.get("MK_DBG_CC", "0") == "1":
            gp.dma_start(out=p.out_fm.rearrange("a b -> (a b)")[0:1040 * 64].rearrange("(a b) -> a b", b=64),
                         in_=p.cc_kv_in).then_inc(dp.h, 16); dp.n += 16
            gp.dma_start(out=p.out_fm.rearrange("a b -> (a b)")[70000:70000 + 4160 * 64].rearrange("(a b) -> a b", b=64),
                         in_=p.cc_kv_out).then_inc(dp.h, 16); dp.n += 16
        else:
            gp.dma_start(out=p.out_fm, in_=p.x1s).then_inc(dp.h, 16); dp.n += 16
        gp.wait_ge(dp.h, dp.n)
    p.block.gpsimd(g_out)

    keys = ["x_fm", "wqT", "wkT", "wvT", "woT", "bq128", "bk128", "bo128",
            "bkrow", "bvrow", "sel", "invrow"]
    in_maps = [{k: pc[k] for k in keys} for pc in per_core]
    res = _run(p, in_maps)
    outs = [res.results[c]["out_fm"] for c in range(NCORES)]
    return np.concatenate([o.T for o in outs], axis=0)



def pol_(p, f):
    n = p.sems["pl"].n
    h = p.sems["pl"].h

    def g(po):
        po.wait_ge(h, n)
        f(po)
    p.block.gpsimd(g)


def alloc_stage_b(p):
    nc = p.nc
    sb = lambda n, s, dt=F32: p.ctx.enter_context(nc.sbuf_tensor(n, s, dt))
    p.WSLOT = sb("WSLOT", [128, 8 * 3072], BF16)
    p.kbuf = sb("kbuf", [128, 8 * 1024], BF16)
    p.vbuf = sb("vbuf", [128, 8 * 1024], BF16)
    p.qbuf = sb("qbuf", [128, 8 * 512], BF16)
    p.xnb = sb("xnb", [128, 8 * 2048], BF16)
    p.w1slot = sb("w1slot", [128, 2 * 2048], BF16)
    p.pbuf = sb("pbuf", [128, 2 * 384], BF16)
    p.ptb = sb("ptb", [128, 2 * 384], BF16)
    p.mrow = sb("mrow", [128, 8])
    p.tabsb = sb("tabsb", [128, 2 * 512], BF16)
    p.msk = sb("msk", [128, 4 * 384])
    p.mb2 = sb("mb2", [128, 2048], BF16)
    p.qspb = sb("qspb", [2, 256])
    p.selhb = sb("selhb", [128, 4])
    p.blockpat = sb("blockpat", [128, 2])
    p.ubuf = sb("ubuf", [128, 512], BF16)
    p.rbuf = sb("rbuf", [128, 512], BF16)
    p.lncol = sb("lncol", [128, 3 * OC])
    p.b1col = sb("b1col", [128, 44])
    p.afm2 = p.xnb[:, 0:8 * 512]
    p.nrm0 = p.xsq[64:66, 0:512]
    p.nrm1 = p.xsq[32:34, 0:512]
    p.nrm2 = p.xsq[0:2, 0:512]

    ld = p.sems["ld"]

    lv = p.sems["lv"]

    def g0(gp):
        gp.wait_ge(p.sems["fn"].h, p._fence_ld)
        gp.dma_start(out=p.msk[:, 0:384], in_=p.maskstd).then_inc(lv.h, 16); lv.n += 16
        gp.dma_start(out=sb3(p.msk[:, 384:4 * 384], 384),
                     in_=p.maskfirst.rearrange("j p t -> p j t")).then_inc(lv.h, 16); lv.n += 16
        gp.dma_start(out=p.selhb[:, :], in_=p.selh).then_inc(lv.h, 16); lv.n += 16
    p.block.gpsimd(g0)
    p._b_const_ld = lv.n

    def v0(ve):
        ev = p.sems["ev"]
        ve.wait_ge(p.sems["fn"].h, p._fence_ld)
        ev.inc(ve.memset(p.blockpat[:, 0:2], 0.0))
        ve.wait_ge(ev.h, ev.n)
        ve.memset(p.blockpat[0:64, 0:1], 1.0)
        ev.inc(ve.memset(p.blockpat[64:128, 1:2], 1.0))
    vec_(p, v0)
    p._b_const_ev = p.sems["ev"].n


ST_B = [(0, 512), (512, 512), (1024, 512), (1536, 512), (2048, 384)]
QT_GROUPS = [[0, 1], [2, 3, 4, 5], [6, 7, 8, 9], [10, 11, 12, 13], [14, 15, 16]]


def emit_cc_x(p, i, ready_st):
    blk = p.block
    ev, cc, st, dp, wx = (p.sems[k] for k in ["ev", "cc", "st", "dp", "wx"])
    if not hasattr(p, "_ccx_dp"):
        p._ccx_dp = {}
        p._ccx_stt = {}

    ev0 = p.sems["ev"].n
    k = ev0 + 1
    for r in range(4):
        for half in range(2):
            k += 1
            p._ccx_stt[(i, r, half)] = k
    final_ev = k + 1

    def g(gp, i=i, ready_st=ready_st):
        gp.wait_ge(st.h, ready_st)
        gp.dma_start(out=p.cc_x_in, in_=p.Xs[:, T_EXT - HALO:T_EXT]).then_inc(dp.h, 16)
        dp.n += 16
        gp.wait_ge(dp.h, dp.n)
        cc.inc(gp.collective_compute(
            "AllGather", ALU.bypass, replica_groups=RG,
            ins=[p.cc_x_in.opt()], outs=[p.cc_x_out.opt()]))
        gp.wait_ge(cc.h, cc.n)
        for r in range(4):
            for half in range(2):
                if not (r == 0 and half == 0):
                    gp.wait_ge(ev.h, p._ccx_stt[(i, r, half)] - 1)
                gp.dma_start(
                    out=sb3(p.scr[:, 0:4 * 384], 384),
                    in_=p.cc_x_out[r * 1024 + half * 512: r * 1024 + half * 512 + 512, :]
                        .rearrange("(c p) t -> p c t", p=128),
                ).then_inc(dp.h, 16); dp.n += 16
                p._ccx_dp[(i, r, half)] = dp.n
                gp.wait_ge(dp.h, dp.n)
        gp.wait_ge(ev.h, final_ev)
        gp.dma_start(out=fm(p.Xs, 0, HALO), in_=sb3(p.xres[:, 0:8 * 384], 384)).then_inc(dp.h, 16)
        dp.n += 16
        gp.wait_ge(dp.h, dp.n)
    blk.gpsimd(g)

    def v(ve, i=i, ready_st=ready_st):
        ve.wait_ge(st.h, ready_st)
        ve.wait_ge(p.sems["lv"].h, p._b_const_ld)
        ev.inc(ve.memset(p.xres[:, 0:8 * 384], 0.0))
        for r in range(4):
            for half in range(2):
                ve.wait_ge(dp.h, p._ccx_dp[(i, r, half)])
                ve.wait_ge(ev.h, ev.n)
                ev.inc(ve.scalar_tensor_tensor(
                    p.xres[:, half * 4 * 384:(half + 1) * 4 * 384],
                    p.scr[:, 0:4 * 384], p.selhb[:, r:r + 1],
                    p.xres[:, half * 4 * 384:(half + 1) * 4 * 384],
                    ALU.mult, ALU.add))
                assert p._ccx_stt[(i, r, half)] == ev.n
        ve.wait_ge(ev.h, ev.n)
        ev.inc(ve.tensor_copy(p.xres[0:1, 0:1], p.xres[0:1, 0:1]))
        assert final_ev == ev.n
    vec_(p, v)
    return dp.n


def emit_p1(p, i, halo_dp):
    p._halo_dp = halo_dp
    blk = p.block
    ld, wx, mm, ev, pl, ac, cc, st, dp, xq = (p.sems[k] for k in
        ["ld", "wx", "mm", "ev", "pl", "ac", "cc", "st", "dp", "xr"])

    def g_w(gp, i=i):
        if i == 0:
            gp.wait_ge(p.sems["fn"].h, p._fence_ld)
        gp.wait_ge(mm.h, p._wslot_free)
        gp.dma_start(out=sb3(p.WSLOT[:, :], 3 * D),
                     in_=p.wqkv[i].rearrange("(c p) o -> p c o", p=128)).then_inc(ld.h, 16)
        ld.n += 16
        gp.wait_ge(mm.h, p._wbuf_free)
        gp.dma_start(out=sb3(p.wbuf[:, :], D),
                     in_=p.wo_b[i].rearrange("(c p) o -> p c o", p=128)).then_inc(ld.h, 16)
        ld.n += 16
        gp.dma_start(out=p.qspb[0:2, 0:128], in_=p.qsp[i]).then_inc(ld.h, 16); ld.n += 16
        gp.dma_start(out=p.qspb[0:2, 128:256], in_=p.ksp[i]).then_inc(ld.h, 16); ld.n += 16
    blk.gpsimd(g_w)
    w_rdy = ld.n

    M = {}
    M["afm_free"] = {0: None, 1: None}
    M["x2st"] = {}
    qdone = {}

    for sti, (t0, wid) in enumerate(ST_B):
        nt = wid // 128

        def g_x(gp, sti=sti, t0=t0, wid=wid, i=i, halo_dp=halo_dp):
            gp.wait_ge(mm.h, M.get(("xbfree", sti), 0))
            if sti == 0:
                gp.wait_ge(dp.h, halo_dp)
                if i == 0 and p.mode != "A":
                    gp.wait_ge(st.h, p._a2_x1_store)
            gp.dma_start(out=sb3(p.xb[:, 0:4 * 512], 512)[:, :, 0:wid],
                         in_=fm(p.Xs, t0, wid)[:, 0:4, :]).then_inc(wx.h, 16)
            wx.n += 16
            if t0 > 0:
                gp.dma_start(out=sb3(p.xb[:, 4 * 512:8 * 512], 512)[:, :, 0:wid],
                             in_=fm(p.Xs, t0 - 1, wid)[:, 4:8, :]).then_inc(wx.h, 16)
                wx.n += 16
            else:
                gp.dma_start(out=sb3(p.xb[:, 4 * 512:8 * 512], 512)[:, :, 1:wid],
                             in_=fm(p.Xs, 0, wid - 1)[:, 4:8, :]).then_inc(wx.h, 16)
                wx.n += 16
                gp.dma_start(out=sb3(p.xb[:, 4 * 512:8 * 512], 512)[:, :, 0:1],
                             in_=fm(p.Xs, 0, 1)[:, 4:8, :],
                             allow_slow_non_contiguous=True).then_inc(wx.h, 16)
                wx.n += 16
        blk.gpsimd(g_x)
        x_rdy = wx.n

        for qk in range(2):
            tabs_free = M.get(("tabfree",))

            def g_tab(gp, qk=qk, t0=t0, wid=wid, tabs_free=tabs_free, i=i):
                if tabs_free is not None:
                    gp.wait_ge(ev.h, tabs_free)
                elif i == 0:
                    gp.wait_ge(p.sems["fn"].h, p._fence_ld)
                srcs = [p.tab_cq, p.tab_sq] if qk == 0 else [p.tab_ck, p.tab_sk]
                fnm = p.sems["tb"]
                for tbl, src_t in enumerate(srcs):
                    gp.dma_start(out=p.tabsb[:, tbl * 512: tbl * 512 + wid],
                                 in_=src_t[:, t0:t0 + wid]).then_inc(fnm.h, 16)
                    fnm.n += 16
            blk.gpsimd(g_tab)
            tab_rdy = p.sems["tb"].n
            for oc in range(OC):
                pp = (qk * OC + oc) % 2
                bank = p.ps0 if pp == 0 else p.ps1
                fbank = p.ps2 if pp == 0 else p.ps3

                def t_p(te, qk=qk, oc=oc, sti=sti, wid=wid, bank=bank, x_rdy=x_rdy, pp=pp,
                        evbar=(p.sems["ev"].n if (qk == 0 and oc == 0) else None)):
                    te.wait_ge(wx.h, x_rdy)
                    if evbar is not None:
                        te.wait_ge(ev.h, evbar)
                    if sti == 0 and qk == 0 and oc == 0:
                        te.wait_ge(ld.h, w_rdy)
                        te.wait_ge(ev.h, p._b_const_ev)
                        te.wait_ge(p.sems["lv"].h, p._b_const_ld)
                        if getattr(p, "_p1_prev_done", None) is not None:
                            te.wait_ge(ev.h, p._p1_prev_done)
                    g = M.get(("projfree", pp))
                    if g is not None:
                        te.wait_ge(ac.h, g)
                    for dc in range(OC):
                        ins = te.matmul(
                            bank[:, 0:wid],
                            p.WSLOT[:, dc * 3 * D + qk * D + oc * 128: dc * 3 * D + qk * D + oc * 128 + 128],
                            p.xb[:, dc * 512: dc * 512 + wid],
                            start=(dc == 0), stop=(dc == OC - 1))
                    ins.then_inc(mm.h); mm.n += 1
                blk.tensor(t_p)
                mmv = mm.n

                def a_e(se, bank=bank, wid=wid, mmv=mmv):
                    se.wait_ge(mm.h, mmv)
                    g = M.get(("scrfree",))
                    if g is not None:
                        se.wait_ge(ev.h, g)
                    ac.inc(se.copy(p.scr[:, 0:wid], bank[:, 0:wid]))
                scl_(p, a_e)
                M[("projfree", pp)] = ac.n
                acv = ac.n

                def p_sq1(po, acv=acv, wid=wid):
                    po.wait_ge(ac.h, acv)
                    g = M.get(("sqfree",))
                    if g is not None:
                        po.wait_ge(mm.h, g)
                    pl.inc(po.tensor_tensor(p.scr[:, 512:512 + wid], p.scr[:, 0:wid],
                                            p.scr[:, 0:wid], ALU.mult))
                pol_(p, p_sq1)
                plv = pl.n

                def t_f(te, plv=plv, wid=wid, fbank=fbank, pp=pp):
                    te.wait_ge(pl.h, plv)
                    g = M.get(("facfree", pp))
                    if g is not None:
                        te.wait_ge(ev.h, g)
                    g2 = M.get(("nrmfree",))
                    if g2 is not None:
                        te.wait_ge(ev.h, g2)
                    te.matmul(p.ps6[0:2, 0:wid], p.blockpat[:, 0:2],
                              p.scr[:, 512:512 + wid],
                              start=True, stop=True).then_inc(mm.h)
                    mm.n += 1
                blk.tensor(t_f)
                nmv = mm.n
                M[("sqfree",)] = nmv

                def v_f(ve, nmv=nmv, wid=wid):
                    ve.wait_ge(mm.h, nmv)
                    ev.inc(ve.tensor_scalar(p.nrm0[0:2, 0:wid], p.ps6[0:2, 0:wid],
                                            1e-24, None, ALU.max))
                vec_(p, v_f)
                evf = ev.n
                M[("nrmfree",)] = ev.n

                def a_f(se, evf=evf, wid=wid):
                    se.wait_ge(ev.h, evf)
                    ac.inc(se.activation(p.nrm1[0:2, 0:wid],
                                         p.nrm0[0:2, 0:wid], AF.Ln))
                    se.wait_ge(ac.h, ac.n)
                    ac.inc(se.activation(p.nrm2[0:2, 0:wid],
                                         p.nrm1[0:2, 0:wid], AF.Exp, scale=-0.5))
                scl_(p, a_f)
                acf = ac.n

                def t_f2(te, qk=qk, acf=acf, wid=wid, fbank=fbank):
                    te.wait_ge(ac.h, acf)
                    te.matmul(fbank[:, 0:wid], p.qspb[0:2, qk * 128:qk * 128 + 128],
                              p.nrm2[0:2, 0:wid],
                              start=True, stop=True).then_inc(mm.h)
                    mm.n += 1
                blk.tensor(t_f2)
                fmv = mm.n

                def v_rot(ve, qk=qk, oc=oc, t0=t0, wid=wid, fmv=fmv, fbank=fbank, tab_rdy=tab_rdy):
                    ve.wait_ge(mm.h, fmv)
                    ve.wait_ge(p.sems["tb"].h, tab_rdy)
                    ev.inc(ve.tensor_tensor(p.ubuf[:, 0:wid], p.scr[:, 0:wid],
                                            fbank[:, 0:wid], ALU.mult))
                    ve.wait_ge(ev.h, ev.n)
                    ve.tensor_copy(p.rbuf[0:32, 0:wid], p.ubuf[32:64, 0:wid])
                    ve.tensor_copy(p.rbuf[32:64, 0:wid], p.ubuf[0:32, 0:wid])
                    ve.tensor_copy(p.rbuf[64:96, 0:wid], p.ubuf[96:128, 0:wid])
                    ev.inc(ve.tensor_copy(p.rbuf[96:128, 0:wid], p.ubuf[64:96, 0:wid]))
                    ve.wait_ge(ev.h, ev.n)
                    ctab = p.tabsb[:, 0:wid]
                    stab = p.tabsb[:, 512:512 + wid]
                    ev.inc(ve.tensor_tensor(p.ubuf[:, 0:wid], p.ubuf[:, 0:wid], ctab, ALU.mult))
                    ev.inc(ve.tensor_tensor(p.rbuf[:, 0:wid], p.rbuf[:, 0:wid], stab, ALU.mult))
                    ve.wait_ge(ev.h, ev.n)
                    if qk == 0:
                        dst = p.qbuf[:, oc * 512: oc * 512 + wid]
                    else:
                        ring0 = (t0 // 128) % 8
                        dst = p.kbuf[:, oc * 1024 + ring0 * 128: oc * 1024 + ring0 * 128 + wid]
                    ev.inc(ve.tensor_add(dst, p.ubuf[:, 0:wid], p.rbuf[:, 0:wid]))
                vec_(p, v_rot)
                M[("facfree", pp)] = ev.n
                M[("scrfree",)] = ev.n
                M[("tabfree",)] = ev.n

        for j in range(nt):
            for half in range(2):
                pp = (j * 2 + half) % 2
                bank = p.ps0 if pp == 0 else p.ps1

                def t_v(te, j=j, half=half, bank=bank, x_rdy=x_rdy, pp=pp):
                    te.wait_ge(wx.h, x_rdy)
                    g = M.get(("vfree", pp))
                    if g is not None:
                        te.wait_ge(ev.h, g)
                    g2 = M.get(("projfree", pp))
                    if g2 is not None:
                        te.wait_ge(ac.h, g2)
                    for dc in range(OC):
                        ins = te.matmul(
                            bank[:, 0:512],
                            p.xb[:, dc * 512 + j * 128: dc * 512 + j * 128 + 128],
                            p.WSLOT[:, dc * 3 * D + 2 * D + half * 512: dc * 3 * D + 2 * D + half * 512 + 512],
                            start=(dc == 0), stop=(dc == OC - 1))
                    ins.then_inc(mm.h); mm.n += 1
                blk.tensor(t_v)
                mmv = mm.n

                def v_v(ve, j=j, half=half, t0=t0, mmv=mmv, bank=bank):
                    ve.wait_ge(mm.h, mmv)
                    ring = ((t0 // 128) + j) % 8
                    ev.inc(ve.tensor_copy(
                        p.vbuf[:, ring * 1024 + half * 512: ring * 1024 + half * 512 + 512],
                        bank[:, 0:512]))
                vec_(p, v_v)
                M[("vfree", pp)] = ev.n
        proj_ev = ev.n
        M[("xbfree", sti + 1)] = mm.n

        ready = [j for j in range(17) if t0 <= (j + 2) * 128 < t0 + wid]
        for j in ready:
            qcol = (j + 2) * 128 - t0
            group = next(gi for gi, g_ in enumerate(QT_GROUPS) if j in g_)
            gslot = group % 2
            afm_t = p.afm if gslot == 0 else p.afm2
            jj = j - QT_GROUPS[group][0]
            for h in range(H):
                par = h % 2
                oc = h // 2
                slot = h % 2
                sbank = p.ps2 if h % 2 == 0 else p.ps3
                tbank = p.ps4 if h % 2 == 0 else p.ps5
                pvbank = p.ps6 if h % 2 == 0 else p.ps7

                def t_s(te, j=j, h=h, par=par, oc=oc, qcol=qcol, sbank=sbank,
                        proj_ev=proj_ev, first=(j == ready[0] and h == 0), group=group, gslot=gslot):
                    if first:
                        te.wait_ge(ev.h, proj_ev)
                        g0_ = M["afm_free"][gslot]
                        if g0_ is not None:
                            te.wait_ge(ev.h, g0_)
                    g = M.get(("sfree", id(sbank)))
                    if g is not None:
                        te.wait_ge(ev.h, g)
                    kt = 0
                    while kt < 3:
                        r0 = (j + kt) % 8
                        run = min(3 - kt, 8 - r0)
                        ins = te.matmul(
                            sbank[:, kt * 128: (kt + run) * 128],
                            p.qbuf[par * 64:(par + 1) * 64, oc * 512 + qcol: oc * 512 + qcol + 128],
                            p.kbuf[par * 64:(par + 1) * 64, oc * 1024 + r0 * 128: oc * 1024 + (r0 + run) * 128],
                            start=True, stop=True)
                        kt += run
                    ins.then_inc(mm.h); mm.n += 1
                blk.tensor(t_s)
                smm = mm.n

                def v_cpm(ve, j=j, smm=smm, sbank=sbank, slot=slot):
                    ve.wait_ge(mm.h, smm)
                    g = M.get(("scrAfree", slot))
                    if g is not None:
                        ve.wait_ge(ac.h, g)
                    mk = p.msk[:, (j + 1) * 384:(j + 2) * 384] if j < 3 else p.msk[:, 0:384]
                    ev.inc(ve.tensor_add(p.scr[:, slot * 512: slot * 512 + 384],
                                         sbank[:, 0:384], mk))
                vec_(p, v_cpm)
                M[("sfree", id(sbank))] = ev.n

                def v_mx(ve, slot=slot):
                    g = M.get(("mrowfree", slot))
                    if g is not None:
                        ve.wait_ge(ac.h, g)
                    ev.inc(ve.reduce_max(p.mrow[:, slot * 4: slot * 4 + 1],
                                         p.scr[:, slot * 512: slot * 512 + 384],
                                         axis=AX, negate=True))
                vec_(p, v_mx)
                evm = ev.n

                def a_ex(se, slot=slot, evm=evm):
                    se.wait_ge(ev.h, evm)
                    g = M.get(("pbfree", slot))
                    if g is not None:
                        se.wait_ge(ev.h, g)
                    ac.inc(se.activation(p.pbuf[:, slot * 384:(slot + 1) * 384],
                                         p.scr[:, slot * 512: slot * 512 + 384],
                                         AF.Exp, bias=p.mrow[:, slot * 4: slot * 4 + 1],
                                         accum_out=p.mrow[:, slot * 4 + 1: slot * 4 + 2]))
                scl_(p, a_ex)
                M[("scrAfree", slot)] = ac.n
                M[("mrowfree", slot)] = ac.n
                acex = ac.n

                def v_sc(ve, slot=slot, acex=acex):
                    ve.wait_ge(ac.h, acex)
                    ev.inc(ve.reciprocal(p.mrow[:, slot * 4 + 2: slot * 4 + 3],
                                         p.mrow[:, slot * 4 + 1: slot * 4 + 2]))
                    ve.wait_ge(ev.h, ev.n)
                    ev.inc(ve.tensor_scalar(p.pbuf[:, slot * 384:(slot + 1) * 384],
                                            p.pbuf[:, slot * 384:(slot + 1) * 384],
                                            p.mrow[:, slot * 4 + 2: slot * 4 + 3],
                                            None, ALU.mult))
                vec_(p, v_sc)
                evp = ev.n

                def t_tr(te, slot=slot, evp=evp, tbank=tbank):
                    te.wait_ge(ev.h, evp)
                    g = M.get(("trfree", id(tbank)))
                    if g is not None:
                        te.wait_ge(ev.h, g)
                    for kt in range(3):
                        ins = te.matmul(
                            tbank[:, kt * 128: kt * 128 + 128],
                            p.pbuf[:, slot * 384 + kt * 128: slot * 384 + kt * 128 + 128],
                            p.mident[:, :], start=True, stop=True)
                    ins.then_inc(mm.h); mm.n += 1
                blk.tensor(t_tr)
                trmm = mm.n

                def v_tr(ve, slot=slot, trmm=trmm, tbank=tbank):
                    ve.wait_ge(mm.h, trmm)
                    g = M.get(("ptfree", slot))
                    if g is not None:
                        ve.wait_ge(mm.h, g)
                    ev.inc(ve.tensor_copy(p.ptb[:, slot * 384:(slot + 1) * 384],
                                          tbank[:, 0:384]))
                vec_(p, v_tr)
                M[("trfree", id(tbank))] = ev.n
                M[("pbfree", slot)] = ev.n
                evt = ev.n

                def t_pv(te, j=j, h=h, slot=slot, evt=evt, pvbank=pvbank):
                    te.wait_ge(ev.h, evt)
                    g = M.get(("pvfree", id(pvbank)))
                    if g is not None:
                        te.wait_ge(ev.h, g)
                    for kt in range(3):
                        ring = (j + kt) % 8
                        ins = te.matmul(
                            pvbank[0:64, 0:128],
                            p.vbuf[:, ring * 1024 + h * HD: ring * 1024 + h * HD + 64],
                            p.ptb[:, slot * 384 + kt * 128: slot * 384 + kt * 128 + 128],
                            start=(kt == 0), stop=(kt == 2))
                    ins.then_inc(mm.h); mm.n += 1
                blk.tensor(t_pv)
                M[("ptfree", slot)] = mm.n
                pvmm = mm.n

                def v_pv(ve, h=h, par=par, oc=oc, jj=jj, pvmm=pvmm, pvbank=pvbank, afm_t=afm_t):
                    ve.wait_ge(mm.h, pvmm)
                    ev.inc(ve.tensor_copy(
                        afm_t[par * 64:(par + 1) * 64, oc * 512 + jj * 128: oc * 512 + jj * 128 + 128],
                        pvbank[0:64, 0:128]))
                vec_(p, v_pv)
                M[("pvfree", id(pvbank))] = ev.n

            if j == QT_GROUPS[group][-1]:
                emit_p1_wo(p, i, group, M)
        qdone[sti] = ev.n

    p._p1_prev_done = p.sems["ev"].n
    p._p1_last_mm = mm.n
    return M


def emit_p1_wo(p, i, group, M):
    blk = p.block
    ld, wx, mm, ev, pl, ac, st, xq, dp = (p.sems[k] for k in
        ["ld", "wx", "mm", "ev", "pl", "ac", "st", "xr", "dp"])
    js = QT_GROUPS[group]
    glen = len(js) * 128
    e0 = 256 + js[0] * 128
    gslot = group % 2
    afm_t = p.afm if gslot == 0 else p.afm2
    afm_rdy = p.sems["ev"].n

    def s_x(se, group=group, e0=e0, glen=glen):
        g = M.get(("xresfree",))
        if g is not None:
            se.wait_ge(ev.h, g)
        g2 = M["x2st"].get(group - 1)
        if g2 is not None:
            se.wait_ge(st.h, g2)
        if group == 0:
            se.wait_ge(dp.h, p._halo_dp)
        se.dma_start(out=sb3(p.xres[:, 0:8 * 512], 512)[:, :, 0:glen],
                     in_=fm(p.Xs, e0, glen)).then_inc(xq.h, 16)
        xq.n += 16
    blk.sync(s_x)
    xr_rdy = xq.n

    woev = {}
    for oco in range(OC):
        bank = p.ps0 if oco % 2 == 0 else p.ps1

        def t_w(te, oco=oco, bank=bank, glen=glen, afm_rdy=afm_rdy, group=group, afm_t=afm_t):
            te.wait_ge(ev.h, afm_rdy)
            g = woev.get(oco - 2)
            if g is not None:
                te.wait_ge(ev.h, g)
            g2 = M.get(("wo_projpend", oco % 2))
            if g2 is not None:
                te.wait_ge(ac.h, g2)
            for ic in range(OC):
                ins = te.matmul(
                    bank[:, 0:glen],
                    p.wbuf[:, ic * D + oco * 128: ic * D + oco * 128 + 128],
                    afm_t[:, ic * 512: ic * 512 + glen],
                    start=(ic == 0), stop=(ic == OC - 1))
            ins.then_inc(mm.h); mm.n += 1
        blk.tensor(t_w)
        wmm = mm.n

        def v_w(ve, oco=oco, bank=bank, wmm=wmm, xr_rdy=xr_rdy, glen=glen):
            ve.wait_ge(mm.h, wmm)
            ve.wait_ge(xq.h, xr_rdy)
            ev.inc(ve.tensor_add(
                p.xres[:, oco * 512: oco * 512 + glen],
                bank[:, 0:glen],
                p.xres[:, oco * 512: oco * 512 + glen]))
        vec_(p, v_w)
        woev[oco] = ev.n
    M["afm_free"][gslot] = ev.n
    M[("xresfree",)] = ev.n
    x2_ev = ev.n

    def s_o(se, group=group, e0=e0, glen=glen, x2_ev=x2_ev):
        se.wait_ge(ev.h, x2_ev)
        se.dma_start(out=fm(p.X2s, e0, glen),
                     in_=sb3(p.xres[:, 0:8 * 512], 512)[:, :, 0:glen]).then_inc(st.h, 16)
        st.n += 16
    blk.sync(s_o)
    M["x2st"][group] = st.n


def emit_p2(p, i):
    blk = p.block
    ld, wx, mm, ev, pl, ac, st, dp, xq = (p.sems[k] for k in
        ["ld", "wx", "mm", "ev", "pl", "ac", "st", "dp", "xr"])

    lv = p.sems["lv"]

    def g_lnv(gp, i=i):
        gp.wait_ge(st.h, p._p1_x2st_all)
        gp.dma_start(out=p.lncol[:, 0:OC], in_=p.ln_g[i]).then_inc(lv.h, 16); lv.n += 16
        gp.dma_start(out=p.lncol[:, OC:2 * OC], in_=p.ln_b[i]).then_inc(lv.h, 16); lv.n += 16
        gp.dma_start(out=p.lncol[:, 2 * OC:3 * OC], in_=p.b2c[i]).then_inc(lv.h, 16); lv.n += 16
        gp.dma_start(out=p.b1col[:, :], in_=p.b1c[i]).then_inc(lv.h, 16); lv.n += 16
    blk.gpsimd(g_lnv)
    lnv_rdy = lv.n

    last_store = None
    for tt in range(4):
        e0 = HALO + tt * 512

        def s_x2(se, tt=tt, e0=e0, i=i):
            g = getattr(p, "_p2_xres_free", None)
            if g is not None:
                se.wait_ge(ev.h, g)
            se.wait_ge(st.h, p._p1_x2st_all)
            se.dma_start(out=sb3(p.xres[:, 0:8 * 512], 512)[:, 0:4, :],
                         in_=fm(p.X2s, e0, 512)[:, 0:4, :]).then_inc(xq.h, 16)
            xq.n += 16
            se.dma_start(out=sb3(p.xres[:, 0:8 * 512], 512)[:, 4:8, :],
                         in_=fm(p.X2s, e0 - 1, 512)[:, 4:8, :]).then_inc(xq.h, 16)
            xq.n += 16
        blk.sync(s_x2)
        x2_rdy = xq.n

        emit_ln_fm(p, p.xres, None, gcol=p.lncol[:, 0:OC], bcol=p.lncol[:, OC:2 * OC],
                   out_sb=p.xnb[:, tt * 4096:(tt + 1) * 4096],
                   waits=[(xq.h, x2_rdy), (lv.h, lnv_rdy)])
        p._p2_xres_free = p._a2_x1_done

    xn_rdy = p._a2_x1_done

    w2s = p.sems["w2"]

    def g_w2(gp, i=i):
        gp.wait_ge(mm.h, p._p1_last_mm)
        gp.dma_start(out=sb3(p.WSLOT[:, 0:22 * 1024], 1024),
                     in_=p.w2[i].rearrange("(j p) o -> p j o", p=128)).then_inc(w2s.h, 16)
        w2s.n += 16
    blk.gpsimd(g_w2)
    w2_rdy = w2s.n

    ghst = {}
    for jp in range(22):
        wslot2 = (jp % 2) * 2048

        def g_w1(gp, jp=jp, i=i, wslot2=wslot2):
            g = getattr(p, "_p2_w1_free", {}).get(jp - 2)
            if g is not None:
                gp.wait_ge(mm.h, g)
            if jp < 2:
                gp.wait_ge(mm.h, p._p1_last_mm)
            g2 = M2get(p, ("w1cons", jp - 1 if jp >= 1 else 21))
            if g2 is not None:
                gp.wait_ge(mm.h, g2)
            gp.dma_start(out=sb3(p.w1slot[:, wslot2:wslot2 + 1024], 128),
                         in_=p.w1[i].rearrange("(c p) o -> p c o", p=128)[:, :, jp * 128:(jp + 1) * 128]
                         ).then_inc(ld.h, 16)
            ld.n += 16
            gp.dma_start(out=sb3(p.w1slot[:, wslot2 + 1024:wslot2 + 2048], 128),
                         in_=p.w1[i].rearrange("(c p) o -> p c o", p=128)[:, :, (jp + 22) * 128:(jp + 23) * 128]
                         ).then_inc(ld.h, 16)
            ld.n += 16
        blk.gpsimd(g_w1)
        w1_rdy = ld.n

        mb_ev = {}
        for tt in range(4):
            pa = p.ps0 if tt % 2 == 0 else p.ps2
            pg = p.ps1 if tt % 2 == 0 else p.ps3

            def t_h(te, jp=jp, tt=tt, pa=pa, pg=pg, w1_rdy=w1_rdy, xn_rdy=xn_rdy, wslot2=wslot2):
                if tt == 0:
                    te.wait_ge(ld.h, w1_rdy)
                    if jp == 0:
                        te.wait_ge(ev.h, xn_rdy)
                g = M2get(p, ("hfree", tt % 2))
                if g is not None:
                    te.wait_ge(ev.h, g)
                for dc in range(OC):
                    te.matmul(pa[:, 0:512],
                              p.w1slot[:, wslot2 + dc * 128: wslot2 + dc * 128 + 128],
                              p.xnb[:, tt * 4096 + dc * 512: tt * 4096 + (dc + 1) * 512],
                              start=(dc == 0), stop=(dc == OC - 1))
                for dc in range(OC):
                    ins = te.matmul(pg[:, 0:512],
                                    p.w1slot[:, wslot2 + 1024 + dc * 128: wslot2 + 1024 + dc * 128 + 128],
                                    p.xnb[:, tt * 4096 + dc * 512: tt * 4096 + (dc + 1) * 512],
                                    start=(dc == 0), stop=(dc == OC - 1))
                ins.then_inc(mm.h); mm.n += 1
            blk.tensor(t_h)
            hmm = mm.n
            if tt == 0:
                M2set(p, ("w1cons", jp), hmm)
            if not hasattr(p, "_p2_w1_free"):
                p._p2_w1_free = {}
            p._p2_w1_free[jp] = hmm

            def a_g(se, jp=jp, tt=tt, pg=pg, hmm=hmm):
                se.wait_ge(mm.h, hmm)
                g = M2get(p, ("gactfree", tt % 2))
                if g is not None:
                    se.wait_ge(ev.h, g)
                g3 = M2get(p, ("xsqfree",))
                if g3 is not None:
                    se.wait_ge(ev.h, g3)
                ac.inc(se.activation(p.xsq[:, 0:512], pg[:, 0:512], AF.Gelu,
                                     bias=p.b1col[:, jp + 22: jp + 23]))
            scl_(p, a_g)
            agm = ac.n

            def v_m(ve, jp=jp, tt=tt, pa=pa, agm=agm):
                ve.wait_ge(ac.h, agm)
                g = M2get(p, ("mbfree",))
                if g is not None:
                    ve.wait_ge(dp.h, g)
                ev.inc(ve.scalar_tensor_tensor(
                    p.mb2[:, tt * 512:(tt + 1) * 512],
                    pa[:, 0:512], p.b1col[:, jp:jp + 1],
                    p.xsq[:, 0:512], ALU.add, ALU.mult))
            vec_(p, v_m)
            M2set(p, ("hfree", tt % 2), ev.n)
            M2set(p, ("gactfree", tt % 2), ev.n)
            M2set(p, ("xsqfree",), ev.n)
            mb_ev[tt] = ev.n

        def g_mb(gp, jp=jp, mb_ev=dict(mb_ev)):
            gp.wait_ge(ev.h, mb_ev[3])
            gp.dma_start(out=p.gh[jp * 128:(jp + 1) * 128, :], in_=p.mb2[:, :]).then_inc(dp.h, 16)
            dp.n += 16
        blk.gpsimd(g_mb)
        ghst[jp] = dp.n
        M2set(p, ("mbfree",), dp.n)
    gh_all = dp.n

    x3st = 0
    for tt in range(8):
        e0 = HALO + tt * 256

        def g_m(gp, tt=tt, gh_all=gh_all):
            gp.wait_ge(dp.h, gh_all)
            g = M2get(p, ("xnfree", tt % 2))
            if g is not None:
                gp.wait_ge(mm.h, g)
            g2 = M2get(p, ("mcons", tt - 1 if tt >= 1 else 7))
            if g2 is not None:
                gp.wait_ge(mm.h, g2)
            gp.dma_start(out=sb3(p.xnb[:, (tt % 2) * 8192:(tt % 2) * 8192 + 22 * 256], 256),
                         in_=p.gh.rearrange("(j p) t -> p j t", p=128)[:, :, tt * 256:(tt + 1) * 256]
                         ).then_inc(wx.h, 16)
            wx.n += 16
        blk.gpsimd(g_m)
        m_rdy = wx.n

        def s_x2b(se, tt=tt, e0=e0, xn_rdy=xn_rdy):
            g = M2get(p, ("x2bfree",))
            if g is not None:
                se.wait_ge(ev.h, g)
            if tt == 0:
                se.wait_ge(ev.h, xn_rdy)
            g2 = M2get(p, ("x3st",))
            if g2 is not None:
                se.wait_ge(st.h, g2)
            se.dma_start(out=sb3(p.xres[:, 0:8 * 512], 512)[:, :, 0:256],
                         in_=fm(p.X2s, e0, 256)).then_inc(xq.h, 16)
            xq.n += 16
        blk.sync(s_x2b)
        x2b_rdy = xq.n

        for oco in range(OC):
            bank = p.ps4 if oco % 2 == 0 else p.ps5

            def t_o(te, tt=tt, oco=oco, bank=bank, m_rdy=m_rdy):
                te.wait_ge(wx.h, m_rdy)
                if tt == 0 and oco == 0:
                    te.wait_ge(w2s.h, w2_rdy)
                g = M2get(p, ("obfree", oco % 2))
                if g is not None:
                    te.wait_ge(ev.h, g)
                for jc in range(22):
                    ins = te.matmul(
                        bank[:, 0:256],
                        p.WSLOT[:, jc * 1024 + oco * 128: jc * 1024 + oco * 128 + 128],
                        p.xnb[:, (tt % 2) * 8192 + jc * 256:(tt % 2) * 8192 + (jc + 1) * 256],
                        start=(jc == 0), stop=(jc == 21))
                ins.then_inc(mm.h); mm.n += 1
            blk.tensor(t_o)
            omm = mm.n
            if oco == 0:
                M2set(p, ("mcons", tt), omm)

            def v_o(ve, tt=tt, oco=oco, bank=bank, omm=omm, x2b_rdy=x2b_rdy):
                ve.wait_ge(mm.h, omm)
                ve.wait_ge(xq.h, x2b_rdy)
                ev.inc(ve.scalar_tensor_tensor(
                    p.xres[:, oco * 512:oco * 512 + 256],
                    bank[:, 0:256], p.lncol[:, 2 * OC + oco: 2 * OC + oco + 1],
                    p.xres[:, oco * 512:oco * 512 + 256],
                    ALU.add, ALU.add))
            vec_(p, v_o)
            M2set(p, ("obfree", oco % 2), ev.n)
        M2set(p, ("xnfree", tt % 2), mm.n)
        M2set(p, ("x2bfree",), ev.n)
        x3_ev = ev.n

        def s_x3(se, tt=tt, e0=e0, x3_ev=x3_ev):
            se.wait_ge(ev.h, x3_ev)
            se.dma_start(out=fm(p.Xs, e0, 256),
                         in_=sb3(p.xres[:, 0:8 * 512], 512)[:, :, 0:256]).then_inc(st.h, 16)
            st.n += 16
        blk.sync(s_x3)
        x3st = st.n
        M2set(p, ("x3st",), st.n)
    p._wslot_free = mm.n
    return x3st


def M2get(p, k):
    return getattr(p, "_M2", {}).get(k)


def M2set(p, k, v):
    if not hasattr(p, "_M2"):
        p._M2 = {}
    p._M2[k] = v


def emit_final(p):
    blk = p.block
    ev, st, xq, dp = (p.sems[k] for k in ["ev", "st", "xr", "dp"])
    for gidx in range(4):
        def s_l(se, gidx=gidx):
            g = getattr(p, "_fin_free", None)
            if g is not None:
                se.wait_ge(ev.h, g)
            g2 = getattr(p, "_fin_stp", None)
            if g2 is not None:
                se.wait_ge(st.h, g2)
            se.wait_ge(st.h, p._x6_st)
            se.dma_start(out=sb3(p.xres[:, 0:8 * 512], 512),
                         in_=fm(p.Xs, HALO + gidx * 512, 512)).then_inc(xq.h, 16)
            xq.n += 16
            se.dma_start(out=sb3(p.xnb.bitcast(F32)[:, 0:4096], 512),
                         in_=fm(p.x1s, gidx * 512, 512)).then_inc(xq.h, 16)
            xq.n += 16
        blk.sync(s_l)
        l_rdy = xq.n

        def v_add(ve, l_rdy=l_rdy):
            ve.wait_ge(xq.h, l_rdy)
            ev.inc(ve.tensor_add(p.xres[:, 0:4096], p.xres[:, 0:4096],
                                 p.xnb.bitcast(F32)[:, 0:4096]))
        vec_(p, v_add)
        emit_ln_fm(p, p.xres, None, waits=[(ev.h, ev.n)])
        app = p._a2_x1_done
        p._fin_free = app

        def s_out(se, gidx=gidx, app=app):
            se.wait_ge(ev.h, app)
            se.dma_start(out=fm(p.out_fm, gidx * 512, 512),
                         in_=sb3(p.xres[:, 0:8 * 512], 512)).then_inc(st.h, 16)
            st.n += 16
        blk.sync(s_out)
        p._fin_stp = st.n
    fin_st = st.n

    def g_end(gp, fin_st=fin_st):
        gp.wait_ge(st.h, fin_st)
    blk.gpsimd(g_end)


def kernel(**inputs):
    per_core = prep_inputs(inputs)
    p = build(mode="full")
    emit_stage_a(p)
    p.actx.close()
    alloc_stage_b(p)
    p._wslot_free = 0
    p._wbuf_free = p.sems["mm"].n
    p._p1_x2st_all = 0
    ready_st = p._a2_x1_store
    for i in range(DEPTH):
        halo_dp = emit_cc_x(p, i, ready_st)
        M = emit_p1(p, i, halo_dp)
        p._p1_x2st_all = p.sems["st"].n
        ready_st = emit_p2(p, i)
        p._wbuf_free = p.sems["mm"].n
    p._x6_st = ready_st
    emit_final(p)
    p.ctx.close()

    keys = ["x_fm", "wqT", "wkT", "wvT", "woT", "bq128", "bk128", "bo128",
            "bkrow", "bvrow", "sel", "invrow",
            "wqkv", "wo_b", "w1", "w2", "ln_g", "ln_b", "b2c", "b1c",
            "qsp", "ksp", "tab_cq", "tab_sq", "tab_ck", "tab_sk",
            "maskstd", "maskfirst", "selh"]
    in_maps = [{k: pc[k] for k in keys} for pc in per_core]
    res = run_bass_kernel_spmd(p.nc, in_maps, core_ids=list(range(NCORES)))
    outs = [res.results[c]["out_fm"] for c in range(NCORES)]
    full = np.concatenate([o.T for o in outs], axis=0)
    return np.ascontiguousarray(full.reshape(B, TSEQ, D))



# version 03 (final)
# speedup vs baseline: 1.1172x; 1.0074x over previous
import contextlib
import os
import numpy as np
import ml_dtypes
import concourse.bass as bass
import concourse.mybir as mybir
from concourse.bass_utils import run_bass_kernel_spmd

F32 = mybir.dt.float32
BF16 = mybir.dt.bfloat16
AF = mybir.ActivationFunctionType
ALU = mybir.AluOpType
AX = mybir.AxisListType.X
BF16NP = ml_dtypes.bfloat16

NCORES = 8
D = 1024
H = 16
HD = 64
OC = 8
T_OWN = 2048
HALO = 384
T_EXT = T_OWN + HALO
B = 2
TSEQ = 8192
EPS_A = 1e-6
DEPTH = 6
DI = 2816
DH = 2 * DI
DI_REAL = 2730
NEG_BIG = -1e30
RG = [[0, 1, 2, 3], [4, 5, 6, 7]]



def _perm_qk():
    p = np.zeros(D, np.int64)
    for c in range(OC):
        ha, hb = 2 * c, 2 * c + 1
        b0 = c * 128
        p[b0 + 0:b0 + 64] = ha * HD + np.arange(64)
        p[b0 + 64:b0 + 128] = hb * HD + np.arange(64)
    return p


PERM_QK = _perm_qk()


def _core_tables(core):
    half = np.arange(0, HD, 2, dtype=np.float32)
    inv_freq = 1.0 / 10000.0 ** (half / HD)
    pos0 = (core % 4) * T_OWN
    idx = np.clip(pos0 - HALO + np.arange(T_EXT), 0, TSEQ - 1).astype(np.float32)
    f = idx[:, None] * inv_freq[None, :]
    base_scale = (half + 0.4 * HD) / (1.4 * HD)
    power = (idx - TSEQ // 2) / 128.0
    sc = base_scale[None, :] ** power[:, None]
    cos, sin = np.cos(f), np.sin(f)
    inv = 1.0 / sc

    def build(ct, st):
        o = np.empty((128, T_EXT), np.float32)
        o[0:32] = ct.T; o[32:64] = ct.T; o[64:96] = ct.T; o[96:128] = ct.T
        s = np.empty((128, T_EXT), np.float32)
        s[0:32] = -st.T; s[32:64] = st.T; s[64:96] = -st.T; s[96:128] = st.T
        return o, s

    cq, sq = build(cos * sc, sin * sc)
    ck, sk = build(cos * inv, sin * inv)
    return cq, sq, ck, sk


def _fmcols(v):
    return np.ascontiguousarray(v.reshape(OC, 128).T)


def prep_inputs(inputs):
    x = np.ascontiguousarray(inputs["L0_attn_x"].astype(np.float32).reshape(B * TSEQ, D))
    shared = {
        "wqT": np.ascontiguousarray(inputs["L0_attn_Wq"].T),
        "wkT": np.ascontiguousarray(inputs["L0_attn_Wk"].T),
        "wvT": np.ascontiguousarray(inputs["L0_attn_Wv"].T),
        "woT": np.ascontiguousarray(inputs["L0_attn_Wo"].T),
        "bq128": _fmcols(inputs["L0_attn_bq"]),
        "bk128": _fmcols(inputs["L0_attn_bk"]),
        "bo128": _fmcols(inputs["L0_attn_bo"]),
        "bkrow": inputs["L0_attn_bk"].reshape(1, D),
        "bvrow": inputs["L0_attn_bv"].reshape(1, D),
    }
    wqkv = inputs["L0_ffn_Wqkv"]
    wq_b = wqkv[:, :, 0:D][:, :, PERM_QK]
    wk_b = wqkv[:, :, D:2 * D][:, :, PERM_QK]
    wv_b = wqkv[:, :, 2 * D:3 * D]
    shared["wqkv"] = np.ascontiguousarray(np.concatenate([wq_b, wk_b, wv_b], axis=2))
    shared["wo_b"] = np.ascontiguousarray(inputs["L0_ffn_Wo"])
    w1 = inputs["L0_ffn_W1"]; b1 = inputs["L0_ffn_b1"]
    w1p = np.zeros((DEPTH, D, DH), np.float32)
    w1p[:, :, 0:DI_REAL] = w1[:, :, 0:DI_REAL]
    w1p[:, :, DI:DI + DI_REAL] = w1[:, :, DI_REAL:2 * DI_REAL]
    shared["w1"] = np.ascontiguousarray(w1p)
    b1p = np.zeros((DEPTH, DH), np.float32)
    b1p[:, 0:DI_REAL] = b1[:, 0:DI_REAL]
    b1p[:, DI:DI + DI_REAL] = b1[:, DI_REAL:2 * DI_REAL]
    shared["b1c"] = np.ascontiguousarray(b1p.reshape(DEPTH, 44, 128).transpose(0, 2, 1))
    w2p = np.zeros((DEPTH, DI, D), np.float32)
    w2p[:, 0:DI_REAL, :] = inputs["L0_ffn_W2"]
    shared["w2"] = w2p
    shared["ln_g"] = np.stack([_fmcols(inputs["L0_ffn_ln_g"][i]) for i in range(DEPTH)])
    shared["ln_b"] = np.stack([_fmcols(inputs["L0_ffn_ln_b"][i]) for i in range(DEPTH)])
    shared["b2c"] = np.stack([_fmcols(inputs["L0_ffn_b2"][i]) for i in range(DEPTH)])
    src_rows = np.concatenate([np.arange(64), np.arange(64)])
    ha = np.zeros(128, np.float32); ha[0:64] = 1
    qsp = np.zeros((DEPTH, 2, 128), np.float32)
    ksp = np.zeros((DEPTH, 2, 128), np.float32)
    for i in range(DEPTH):
        qs = inputs["L0_ffn_q_scale"][i]; ks = inputs["L0_ffn_k_scale"][i]
        qsp[i, 0] = 8.0 * qs[src_rows] * ha; qsp[i, 1] = 8.0 * qs[src_rows] * (1 - ha)
        ksp[i, 0] = ks[src_rows] * ha; ksp[i, 1] = ks[src_rows] * (1 - ha)
    shared["qsp"] = qsp
    shared["ksp"] = ksp
    ii = np.arange(128)[:, None]; ss = np.arange(384)[None, :]
    mstd = np.where((ss >= ii) & (ss <= ii + 256), 0.0, NEG_BIG).astype(np.float32)
    shared["maskstd"] = mstd

    per_core = []
    for core in range(NCORES):
        s = core * T_OWN
        cq, sq, ck, sk = _core_tables(core)
        sel = np.zeros((128, 4), np.float32)
        sel[:, 0:core % 4] = 1.0
        selh = np.zeros((128, 4), np.float32)
        if core % 4 > 0:
            selh[:, core % 4 - 1] = 1.0
        inv = np.zeros((1, T_EXT), np.float32)
        if core % 4 == 0:
            inv[0, 0:HALO] = 1.0
        mfirst = np.stack([shared["maskstd"]] * 3)
        if core % 4 == 0:
            ii = np.arange(128)[:, None]; ss = np.arange(384)[None, :]
            for j in range(3):
                kb = 128 * j - 384 + ss
                mfirst[j] = np.where(kb < 0, NEG_BIG, mfirst[j])
        d = dict(shared)
        d["x_fm"] = np.ascontiguousarray(x[s:s + T_OWN].T)
        d["tab_cq"] = cq; d["tab_sq"] = sq; d["tab_ck"] = ck; d["tab_sk"] = sk
        d["sel"] = sel
        d["selh"] = selh
        d["invrow"] = inv
        d["maskfirst"] = np.ascontiguousarray(mfirst)
        per_core.append(d)
    return per_core



class Sem:
    __slots__ = ("h", "n")

    def __init__(self, h):
        self.h = h
        self.n = 0

    def inc(self, ins, k=1):
        if k == 1:
            ins.then_inc(self.h)
        else:
            ins.then_inc(self.h, k)
        self.n += k
        return self.n


class P:
    pass


def fm(ap_dram, c0, w, nchunk=OC):
    return ap_dram.rearrange("(c p) t -> p c t", p=128)[:, :, c0:c0 + w]


def sb3(ap_sb, w):
    return ap_sb.rearrange("p (c t) -> p c t", t=w)


def wview(w_dram):
    return w_dram.rearrange("(c p) o -> p c o", p=128)



def vec_(p, f):
    n = p.sems["ev"].n
    h = p.sems["ev"].h

    def g(ve):
        ve.wait_ge(h, n)
        f(ve)
    p.block.vector(g)


def scl_(p, f):
    n = p.sems["ac"].n
    h = p.sems["ac"].h

    def g(se):
        se.wait_ge(h, n)
        f(se)
    p.block.scalar(g)


def build(mode="A"):
    ctx = contextlib.ExitStack()
    import os
    drc = os.environ.get("MK_DETECT_RACES", "0") == "1"
    nc = bass.Bass("TRN2", target_bir_lowering=False, debug=False,
                   detect_race_conditions=drc)
    p = P()
    p.nc, p.ctx, p.mode = nc, ctx, mode

    din = lambda n, s, dt=F32: nc.dram_tensor(n, s, dt, kind="ExternalInput").ap()
    _dbg = os.environ.get("MK_DBG_DRAM", "0") == "1"
    dint = lambda n, s, dt=F32: nc.dram_tensor(
        n, s, dt, kind="ExternalOutput" if _dbg and not n.startswith("cc_") else "Internal").ap()

    p.x_fm = din("x_fm", [D, T_OWN])
    p.wqT = din("wqT", [D, D]); p.wkT = din("wkT", [D, D])
    p.wvT = din("wvT", [D, D]); p.woT = din("woT", [D, D])
    p.bq128 = din("bq128", [128, OC]); p.bk128 = din("bk128", [128, OC])
    p.bo128 = din("bo128", [128, OC])
    p.bkrow = din("bkrow", [1, D]); p.bvrow = din("bvrow", [1, D])
    p.sel = din("sel", [128, 4])
    p.invrow = din("invrow", [1, T_EXT])
    if mode != "A":
        p.wqkv = din("wqkv", [DEPTH, D, 3 * D])
        p.wo_b = din("wo_b", [DEPTH, D, D])
        p.w1 = din("w1", [DEPTH, D, DH])
        p.w2 = din("w2", [DEPTH, DI, D])
        p.ln_g = din("ln_g", [DEPTH, 128, OC]); p.ln_b = din("ln_b", [DEPTH, 128, OC])
        p.b2c = din("b2c", [DEPTH, 128, OC]); p.b1c = din("b1c", [DEPTH, 128, 44])
        p.qsp = din("qsp", [DEPTH, 2, 128]); p.ksp = din("ksp", [DEPTH, 2, 128])
        p.maskstd = din("maskstd", [128, 384])
        p.maskfirst = din("maskfirst", [3, 128, 384])
        p.selh = din("selh", [128, 4])
        p.tab_cq = din("tab_cq", [128, T_EXT]); p.tab_sq = din("tab_sq", [128, T_EXT])
        p.tab_ck = din("tab_ck", [128, T_EXT]); p.tab_sk = din("tab_sk", [128, T_EXT])

    p.out_fm = nc.dram_tensor("out_fm", [D, T_OWN], F32, kind="ExternalOutput").ap()

    p.Xs = dint("Xs", [D, T_EXT])
    p.x1s = dint("x1s", [D, T_OWN])
    p.cc_kv_in = dint("cc_kv_in", [1040, 64])
    p.cc_kv_out = dint("cc_kv_out", [4160, 64])
    if mode != "A":
        p.X2s = dint("X2s", [D, T_EXT])
        p.ms = dint("ms", [64, 128])
        p.gh = dint("gh", [DI, T_OWN], BF16)
        p.cc_x_in = dint("cc_x_in", [D, HALO])
        p.cc_x_out = dint("cc_x_out", [4 * D, HALO])

    p.c_ident = nc.inline_tensor(np.eye(128, dtype=BF16NP), "c_ident").ap()
    p.c_maskA = nc.inline_tensor(np.triu(np.ones((128, 128), np.float32)).astype(BF16NP), "c_maskA").ap()
    if mode != "A":
        p.c_maskB0 = nc.inline_tensor(np.tril(np.ones((128, 128), np.float32)).astype(BF16NP), "c_maskB0").ap()
        p.c_maskB2 = nc.inline_tensor(np.triu(np.ones((128, 128), np.float32)).astype(BF16NP), "c_maskB2").ap()

    p.actx = contextlib.ExitStack()
    sb = lambda n, s, dt=F32: ctx.enter_context(nc.sbuf_tensor(n, s, dt))
    sba = lambda n, s, dt=F32: p.actx.enter_context(nc.sbuf_tensor(n, s, dt))
    psa = lambda n, s, dt=F32: ctx.enter_context(nc.psum_tensor(n, s, dt))

    p.ps0 = psa("ps0", [128, 512]); p.ps1 = psa("ps1", [128, 512])
    p.ps2 = psa("ps2", [128, 512]); p.ps3 = psa("ps3", [128, 512])
    p.ps4 = psa("ps4", [128, 512]); p.ps5 = psa("ps5", [128, 512])
    p.ps6 = psa("ps6", [128, 512])
    p.ps7 = psa("ps7", [128, 512])

    p.wbuf = sb("wbuf", [128, OC * D], BF16)
    p.xb = sb("xb", [128, OC * 512], BF16)
    p.scr = sb("scr", [128, 3 * 512])
    p.brow = sb("brow", [1, 2 * D], BF16)
    p.bcolq = sb("bcolq", [128, OC]); p.bcolk = sb("bcolk", [128, OC])
    p.bcolo = sb("bcolo", [128, OC])
    p.onesb = sb("onesb", [128, 1], BF16)
    p.onesf = sb("onesf", [128, 1])
    p.onesrow_b = sb("onesrow_b", [1, 128], BF16)
    p.onesrow_f = sb("onesrow_f", [1, 128])
    p.selb = sb("selb", [128, 4])
    p.afm = sb("afm", [128, OC * 512], BF16)
    p.xres = sb("xres", [128, OC * 512])
    p.xsq = sb("xsq", [128, 512])
    p.mident = sb("mident", [128, 128], BF16)
    p.cvals = sb("cvals", [128, 2])
    nc.const_aps.register(F32, 1e-5, p.cvals[:, 0:1])
    nc.const_aps.register(F32, -0.5, p.cvals[:, 1:2])
    p.kvpre = p.scr[:, 0:1024]

    p.qA = sba("qA", [128, OC * T_OWN], BF16)
    p.kA = sba("kA", [128, OC * T_OWN], BF16)
    p.ktokA = sba("ktokA", [128, 16 * D], BF16)
    p.vtokA = sba("vtokA", [128, 16 * D], BF16)
    p.kvseg = sba("kvseg", [64, 512])
    p.kvst = sba("kvst", [128, 8 * 64])
    p.kvstb = sba("kvstb", [128, 8 * 64], BF16)
    p.ksst = sba("ksst", [128, 8]); p.ksstb = sba("ksstb", [128, 8], BF16)
    p.ksst2 = sba("ksst2", [128, 8])
    p.kspre = sba("kspre", [128, 32])
    p.smt = sba("smt", [128, 2 * 128], BF16)
    p.atok = sba("atok", [128, 4 * D], BF16)
    p.rden = sba("rden", [128, 16])
    p.mmaskA = sba("mmaskA", [128, 128], BF16)

    p.sems = {}
    for name in ["ld", "wx", "mm", "ev", "pl", "ac", "cc", "st", "dp", "xr", "fn", "tb", "lv", "w2"]:
        p.sems[name] = Sem(ctx.enter_context(nc.semaphore("s_" + name)))

    p.block = ctx.enter_context(nc.Block())
    return p



def emit_stage_a(p):
    blk = p.block
    ld, wx, mm, ev, pl, ac, cc, st, dp = (p.sems[k] for k in
        ["ld", "wx", "mm", "ev", "pl", "ac", "cc", "st", "dp"])

    def g_ld(gp):
        gp.dma_start(out=p.mident[:, :], in_=p.c_ident).then_inc(ld.h, 16); ld.n += 16
        gp.dma_start(out=p.mmaskA[:, :], in_=p.c_maskA).then_inc(ld.h, 16); ld.n += 16
        gp.dma_start(out=p.bcolq[:, :], in_=p.bq128).then_inc(ld.h, 16); ld.n += 16
        gp.dma_start(out=p.bcolk[:, :], in_=p.bk128).then_inc(ld.h, 16); ld.n += 16
        gp.dma_start(out=p.bcolo[:, :], in_=p.bo128).then_inc(ld.h, 16); ld.n += 16
        gp.dma_start(out=p.brow[:, 0:D], in_=p.bkrow).then_inc(ld.h, 16); ld.n += 16
        gp.dma_start(out=p.brow[:, D:2 * D], in_=p.bvrow).then_inc(ld.h, 16); ld.n += 16
        gp.dma_start(out=p.selb[:, :], in_=p.sel).then_inc(ld.h, 16); ld.n += 16
        gp.dma_start(out=sb3(p.wbuf[:, :], D), in_=wview(p.wqT)).then_inc(ld.h, 16); ld.n += 16
    blk.gpsimd(g_ld)
    ld_init = ld.n

    def v_ones(ve):
        ve.memset(p.cvals[:, 0:1], 1e-5)
        ve.memset(p.cvals[:, 1:2], -0.5)
        ve.memset(p.onesb[:, :], 1.0)
        ve.memset(p.onesf[:, :], 1.0)
        ve.memset(p.onesrow_b[:, :], 1.0)
        ev.inc(ve.memset(p.onesrow_f[:, :], 1.0))
    vec_(p, v_ones)
    ones_rdy = ev.n

    st8 = {"evlast": None, "aclast": None, "xfree": 0, "wrdy": ld_init}

    def emit_fm_pass(dst, bcol, has_phi=True):
        for ti in range(4):
            def s_x(se, ti=ti):
                se.wait_ge(mm.h, st8["xfree"])
                se.dma_start(out=sb3(p.xb[:, :], 512), in_=fm(p.x_fm, ti * 512, 512)).then_inc(wx.h, 16)
                wx.n += 16
            blk.gpsimd(s_x)
            xr = wx.n
            for oc in range(OC):
                def t_p(te, oc=oc, ti=ti, xr=xr, wrdy=st8["wrdy"], g=st8["evlast"]):
                    te.wait_ge(wx.h, xr)
                    if ti == 0 and oc == 0:
                        te.wait_ge(ld.h, wrdy)
                        te.wait_ge(ev.h, ones_rdy)
                    if g is not None:
                        te.wait_ge(ev.h, g)
                    for dc in range(OC):
                        ins = te.matmul(
                            p.ps0[:, 0:512],
                            p.wbuf[:, dc * D + oc * 128: dc * D + oc * 128 + 128],
                            p.xb[:, dc * 512:(dc + 1) * 512],
                            start=(dc == 0), stop=(dc == OC - 1))
                    ins.then_inc(mm.h); mm.n += 1
                blk.tensor(t_p)
                mmv = mm.n

                def v_p(ve, oc=oc, bcol=bcol, mmv=mmv, g=st8["aclast"]):
                    ve.wait_ge(mm.h, mmv)
                    if g is not None:
                        ve.wait_ge(ac.h, g)
                    ve.tensor_scalar(p.scr[:, 0:512], p.ps0[:, 0:512],
                                     bcol[:, oc:oc + 1], 0.0, ALU.add, ALU.max)
                    ev.inc(ve.tensor_scalar(p.scr[:, 512:1024], p.ps0[:, 0:512],
                                            bcol[:, oc:oc + 1], 0.0, ALU.add, ALU.min))
                vec_(p, v_p)
                st8["evlast"] = ev.n
                evv = ev.n

                def a_p(se, evv=evv):
                    se.wait_ge(ev.h, evv)
                    ac.inc(se.activation(p.scr[:, 1024:1536], p.scr[:, 512:1024], AF.Exp))
                scl_(p, a_p)
                st8["aclast"] = ac.n
                acv = ac.n

                def v_p2(ve, oc=oc, ti=ti, acv=acv, dst=dst):
                    ve.wait_ge(ac.h, acv)
                    ev.inc(ve.tensor_add(
                        dst[:, oc * T_OWN + ti * 512: oc * T_OWN + ti * 512 + 512],
                        p.scr[:, 0:512], p.scr[:, 1024:1536]))
                vec_(p, v_p2)
            st8["xfree"] = mm.n

    def emit_tok_pass(dst, brow_off, has_phi):
        for ti in range(4):
            def s_x(se, ti=ti):
                se.wait_ge(mm.h, st8["xfree"])
                se.dma_start(out=sb3(p.xb[:, :], 512), in_=fm(p.x_fm, ti * 512, 512)).then_inc(wx.h, 16)
                wx.n += 16
            blk.gpsimd(s_x)
            xr = wx.n
            for j in range(4):
                tt = ti * 4 + j
                for half in range(2):
                    def t_t(te, j=j, half=half, ti=ti, xr=xr, wrdy=st8["wrdy"], g=st8["evlast"]):
                        te.wait_ge(wx.h, xr)
                        if ti == 0 and j == 0 and half == 0:
                            te.wait_ge(ld.h, wrdy)
                        if g is not None:
                            te.wait_ge(ev.h, g)
                        for dc in range(OC):
                            te.matmul(
                                p.ps0[:, 0:512],
                                p.xb[:, dc * 512 + j * 128: dc * 512 + j * 128 + 128],
                                p.wbuf[:, dc * D + half * 512: dc * D + half * 512 + 512],
                                start=(dc == 0), stop=False,
                            )
                        te.matmul(
                            p.ps0[:, 0:512],
                            p.onesrow_b[0:1, :],
                            p.brow[0:1, brow_off + half * 512: brow_off + half * 512 + 512],
                            start=False, stop=True,
                        ).then_inc(mm.h); mm.n += 1
                    blk.tensor(t_t)
                    mmv = mm.n

                    if has_phi:
                        def v_t(ve, mmv=mmv, g=st8["aclast"]):
                            ve.wait_ge(mm.h, mmv)
                            if g is not None:
                                ve.wait_ge(ac.h, g)
                            ve.tensor_scalar(p.scr[:, 0:512], p.ps0[:, 0:512], 0.0, None, ALU.max)
                            ev.inc(ve.tensor_scalar(p.scr[:, 512:1024], p.ps0[:, 0:512], 0.0, None, ALU.min))
                        vec_(p, v_t)
                        st8["evlast"] = ev.n
                        evv = ev.n

                        def a_t(se, evv=evv):
                            se.wait_ge(ev.h, evv)
                            ac.inc(se.activation(p.scr[:, 1024:1536], p.scr[:, 512:1024], AF.Exp))
                        scl_(p, a_t)
                        st8["aclast"] = ac.n
                        acv = ac.n

                        def v_t2(ve, tt=tt, half=half, acv=acv, dst=dst):
                            ve.wait_ge(ac.h, acv)
                            ev.inc(ve.tensor_add(
                                dst[:, tt * D + half * 512: tt * D + half * 512 + 512],
                                p.scr[:, 0:512], p.scr[:, 1024:1536]))
                        vec_(p, v_t2)
                    else:
                        def v_t(ve, tt=tt, half=half, mmv=mmv, dst=dst):
                            ve.wait_ge(mm.h, mmv)
                            ev.inc(ve.tensor_copy(
                                dst[:, tt * D + half * 512: tt * D + half * 512 + 512],
                                p.ps0[:, 0:512]))
                        vec_(p, v_t)
                        st8["evlast"] = ev.n
            st8["xfree"] = mm.n

    def load_w(src):
        mm_before = mm.n

        def g_w(gp, mm_before=mm_before):
            gp.wait_ge(mm.h, mm_before)
            gp.dma_start(out=sb3(p.wbuf[:, :], D), in_=wview(src)).then_inc(ld.h, 16)
            ld.n += 16
        blk.gpsimd(g_w)
        st8["wrdy"] = ld.n

    emit_fm_pass(p.qA, p.bcolq)
    load_w(p.wkT)
    emit_fm_pass(p.kA, p.bcolk)
    emit_tok_pass(p.ktokA, 0, True)
    load_w(p.wvT)
    emit_tok_pass(p.vtokA, D, False)
    a1_done_ev = ev.n
    a1_done_mm = mm.n

    def t_seg(te):
        te.wait_ge(ev.h, a1_done_ev)
        for h in range(H):
            bank = p.ps4 if h < 8 else p.ps5
            col = (h % 8) * 64
            for j in range(16):
                ins = te.matmul(
                    bank[0:64, col:col + 64],
                    p.ktokA[:, j * D + h * HD: j * D + h * HD + 64],
                    p.vtokA[:, j * D + h * HD: j * D + h * HD + 64],
                    start=(j == 0), stop=(j == 15),
                )
        ins.then_inc(mm.h); mm.n += 1
    blk.tensor(t_seg)
    seg_mm = mm.n

    def v_seg(ve):
        ve.wait_ge(ev.h, a1_done_ev)
        for oc in range(OC):
            ve.reduce_sum(p.ksst[:, oc:oc + 1], p.kA[:, oc * T_OWN:(oc + 1) * T_OWN], axis=AX)
        ve.wait_ge(mm.h, seg_mm)
        ev.inc(ve.tensor_copy(p.kvseg[0:64, 0:512], p.ps4[0:64, 0:512]))
    vec_(p, v_seg)
    seg_ev1 = ev.n

    import os
    if os.environ.get("MK_CUT_CC", "0") == "1":
        def v_zero(ve):
            ve.memset(p.kvst[:, :], 0.0)
            ve.memset(p.ksst[:, :], 0.0)
            ve.memset(p.kvstb[:, :], 0.0)
            ev.inc(ve.memset(p.ksstb[:, :], 0.0))
        vec_(p, v_zero)
        state_rdy = ev.n
        if os.environ.get("MK_CUT_A2", "0") != "1":
            emit_a2(p, state_rdy, a1_done_mm)
        return

    def g_cc1(gp):
        gp.wait_ge(ev.h, seg_ev1)
        gp.dma_start(
            out=p.cc_kv_in[0:512, :].rearrange("(hh d) e -> d hh e", hh=8),
            in_=sb3(p.kvseg[0:64, :], 64)).then_inc(dp.h, 16); dp.n += 16
        gp.dma_start(
            out=p.cc_kv_in[1024:1040, :].rearrange("r (pp c) -> (r pp) c", c=8),
            in_=p.ksst[:, :]).then_inc(dp.h, 16); dp.n += 16
    blk.gpsimd(g_cc1)
    dma1 = dp.n

    def v_seg2(ve):
        ve.wait_ge(dp.h, dma1)
        ev.inc(ve.tensor_copy(p.kvseg[0:64, 0:512], p.ps5[0:64, 0:512]))
    vec_(p, v_seg2)
    seg_ev2 = ev.n

    def g_cc(gp):
        gp.wait_ge(ev.h, seg_ev2)
        gp.dma_start(
            out=p.cc_kv_in[512:1024, :].rearrange("(hh d) e -> d hh e", hh=8),
            in_=sb3(p.kvseg[0:64, :], 64)).then_inc(dp.h, 16); dp.n += 16
        gp.wait_ge(dp.h, dp.n)
        cc.inc(gp.collective_compute(
            "AllGather", ALU.bypass, replica_groups=RG,
            ins=[p.cc_kv_in.opt()], outs=[p.cc_kv_out.opt()]))
        gp.wait_ge(cc.h, cc.n)
        for r in range(4):
            b0 = r * 1040
            src2 = p.cc_kv_out[b0:b0 + 1024, :].rearrange("(j t d) e -> t d j e", j=8, t=2)
            if r >= 2:
                gp.wait_ge(ev.h, p._cc_stt[r - 2])
            sl = (r % 2) * 512
            gp.dma_start(out=sb3(p.kvpre[0:64, sl:sl + 512], 64), in_=src2[0]).then_inc(dp.h, 16); dp.n += 16
            gp.dma_start(out=sb3(p.kvpre[64:128, sl:sl + 512], 64), in_=src2[1]).then_inc(dp.h, 16); dp.n += 16
            gp.dma_start(out=p.kspre[:, r * 8:(r + 1) * 8],
                         in_=p.cc_kv_out[b0 + 1024:b0 + 1040, :].rearrange(
                             "r (pp c) -> (r pp) c", c=8)).then_inc(dp.h, 16); dp.n += 16
            gp.wait_ge(dp.h, dp.n)
            p._cc_dp[r] = dp.n
    p._cc_dp = {}
    p._cc_stt = {r: ev.n + 2 + 2 * (r + 1) for r in range(4)}
    blk.gpsimd(g_cc)

    def v_cc(ve):
        ev.inc(ve.memset(p.kvst[:, :], 0.0))
        ev.inc(ve.memset(p.ksst2[:, :], 0.0))
        for r in range(4):
            sl = (r % 2) * 512
            ve.wait_ge(dp.h, p._cc_dp[r])
            ve.wait_ge(ev.h, ev.n)
            ev.inc(ve.scalar_tensor_tensor(p.kvst[:, :], p.kvpre[:, sl:sl + 512], p.selb[:, r:r + 1],
                                           p.kvst[:, :], ALU.mult, ALU.add))
            ve.wait_ge(ev.h, ev.n)
            ev.inc(ve.scalar_tensor_tensor(p.ksst2[:, :], p.kspre[:, r * 8:(r + 1) * 8],
                                           p.selb[:, r:r + 1], p.ksst2[:, :], ALU.mult, ALU.add))
            assert p._cc_stt[r] == ev.n
        ve.wait_ge(ev.h, ev.n)
        ev.inc(ve.tensor_copy(p.ksst[:, :], p.ksst2[:, :]))
        ve.wait_ge(ev.h, ev.n)
        ve.tensor_copy(p.kvstb[:, :], p.kvst[:, :])
        ev.inc(ve.tensor_copy(p.ksstb[:, :], p.ksst[:, :]))
    vec_(p, v_cc)
    state_rdy = ev.n
    import os as _os
    if _os.environ.get("MK_CUT_A2", "0") == "1":
        return


    emit_a2(p, state_rdy, a1_done_mm)

    a_end = {k: p.sems[k].n for k in ("mm", "ev", "ac", "pl")}

    fn = p.sems["fn"]

    def g_fence(gp, a_end=a_end):
        for k in ("mm", "ev", "ac", "pl"):
            gp.wait_ge(p.sems[k].h, a_end[k])
        gp.sem_inc(fn.h, 1)
    blk.gpsimd(g_fence)
    fn.n += 1
    p._fence_ld = fn.n


def emit_a2(p, state_rdy, a1_done_mm):
    import os
    NCH = int(os.environ.get("MK_A2_NCH", "16"))
    DO_WOLN = os.environ.get("MK_A2_WOLN", "1") == "1"
    blk = p.block
    ld, wx, mm, ev, pl, ac, cc, st, dp = (p.sems[k] for k in
        ["ld", "wx", "mm", "ev", "pl", "ac", "cc", "st", "dp"])

    def g_wo(gp):
        gp.wait_ge(mm.h, a1_done_mm)
        gp.dma_start(out=sb3(p.wbuf[:, :], D), in_=wview(p.woT)).then_inc(ld.h, 16)
        ld.n += 16
    blk.gpsimd(g_wo)
    wo_rdy = ld.n

    sev = {}
    smt_free = {}
    vdiv_ev = {}
    vu_ev = {}

    for c in range(NCH):
        for h in range(H):
            par = h % 2
            sbank = p.ps0 if h % 2 == 0 else p.ps1
            sslot = h % 2

            def t_s(te, c=c, h=h, par=par, sbank=sbank):
                if c == 0 and h == 0:
                    te.wait_ge(ev.h, state_rdy)
                if h == 0:
                    g = vdiv_ev.get(c - 1)
                    if g is not None:
                        te.wait_ge(ev.h, g)
                    gu = vu_ev.get(c - 1)
                    if gu is not None:
                        te.wait_ge(ev.h, gu)
                    if c % 4 == 0 and c > 0 and getattr(p, "_a2_x1_done", None) is not None:
                        te.wait_ge(ev.h, p._a2_x1_done)
                g = sev.get((c, h - 2)) or (sev.get((c - 1, h + 14)) if h < 2 else None)
                if g is not None:
                    te.wait_ge(ev.h, g)
                te.matmul(
                    sbank[:, 0:128],
                    p.kA[par * 64:(par + 1) * 64, (h // 2) * T_OWN + c * 128:(h // 2) * T_OWN + c * 128 + 128],
                    p.qA[par * 64:(par + 1) * 64, (h // 2) * T_OWN + c * 128:(h // 2) * T_OWN + c * 128 + 128],
                    start=True, stop=True,
                ).then_inc(mm.h); mm.n += 1
            blk.tensor(t_s)
            s_mm = mm.n

            def v_m(ve, s_mm=s_mm, sbank=sbank, sslot=sslot, c=c, h=h):
                ve.wait_ge(mm.h, s_mm)
                g = smt_free.get((c, h - 2)) or (smt_free.get((c - 1, h + 14)) if h < 2 else None)
                if g is not None:
                    ve.wait_ge(mm.h, g)
                ev.inc(ve.tensor_tensor(
                    p.smt[:, sslot * 128:(sslot + 1) * 128],
                    sbank[:, 0:128], p.mmaskA[:, :], ALU.mult))
            vec_(p, v_m)
            sev[(c, h)] = ev.n
            smt_rdy = ev.n

            numbank = p.ps2 if h < 8 else p.ps3
            ncol = (h % 8) * 64

            def t_n(te, c=c, h=h, par=par, sslot=sslot, numbank=numbank, ncol=ncol,
                    smt_rdy=smt_rdy):
                te.wait_ge(ev.h, smt_rdy)
                te.matmul(numbank[:, ncol:ncol + 64],
                          p.smt[:, sslot * 128:(sslot + 1) * 128],
                          p.vtokA[:, c * D + h * HD: c * D + h * HD + 64],
                          start=True, stop=False)
                te.matmul(numbank[:, ncol:ncol + 64],
                          p.qA[par * 64:(par + 1) * 64, (h // 2) * T_OWN + c * 128:(h // 2) * T_OWN + c * 128 + 128],
                          p.kvstb[par * 64:(par + 1) * 64, (h // 2) * 64:(h // 2) * 64 + 64],
                          start=False, stop=True)
                te.matmul(p.ps6[:, h:h + 1],
                          p.smt[:, sslot * 128:(sslot + 1) * 128],
                          p.onesb[:, 0:1], start=True, stop=False)
                te.matmul(p.ps6[:, h:h + 1],
                          p.qA[par * 64:(par + 1) * 64, (h // 2) * T_OWN + c * 128:(h // 2) * T_OWN + c * 128 + 128],
                          p.ksstb[par * 64:(par + 1) * 64, h // 2:h // 2 + 1],
                          start=False, stop=True).then_inc(mm.h)
                mm.n += 1
            blk.tensor(t_n)
            smt_free[(c, h)] = mm.n
        ph1_mm = mm.n

        def v_dd(ve, c=c, ph1_mm=ph1_mm):
            ve.wait_ge(mm.h, ph1_mm)
            ev.inc(ve.tensor_scalar(p.rden[:, 0:16], p.ps6[:, 0:16], EPS_A, None, ALU.add))
            ve.wait_ge(ev.h, ev.n)
            ev.inc(ve.reciprocal(p.rden[:, 0:16], p.rden[:, 0:16]))
            ve.wait_ge(ev.h, ev.n)
            for h in range(H):
                numbank = p.ps2 if h < 8 else p.ps3
                ncol = (h % 8) * 64
                ins = ve.tensor_scalar(
                    p.atok[:, (c % 4) * D + h * HD:(c % 4) * D + h * HD + 64],
                    numbank[:, ncol:ncol + 64],
                    p.rden[:, h:h + 1], None, ALU.mult)
            ev.inc(ins)
        vec_(p, v_dd)
        vdiv_ev[c] = ev.n
        atok_rdy = ev.n

        def t_u(te, c=c, ph1_mm=ph1_mm):
            g = vu_ev.get(c - 1)
            if g is not None:
                te.wait_ge(ev.h, g)
            if c % 4 == 0 and c > 0 and getattr(p, "_a2_ln_var", None) is not None:
                te.wait_ge(ev.h, p._a2_ln_var)
            for h in range(H):
                par = h % 2
                te.matmul(p.ps4[par * 64:(par + 1) * 64, (h // 2) * 64:(h // 2) * 64 + 64],
                          p.ktokA[:, c * D + h * HD: c * D + h * HD + 64],
                          p.vtokA[:, c * D + h * HD: c * D + h * HD + 64],
                          start=True, stop=True, tile_position=(0, par * 64))
                ins = te.matmul(p.ps5[par * 64:(par + 1) * 64, h // 2:h // 2 + 1],
                                p.ktokA[:, c * D + h * HD: c * D + h * HD + 64],
                                p.onesb[:, 0:1],
                                start=True, stop=True, tile_position=(0, par * 64))
            ins.then_inc(mm.h); mm.n += 1
        blk.tensor(t_u)
        u_mm = mm.n

        def v_u(ve, c=c, u_mm=u_mm):
            ve.wait_ge(mm.h, u_mm)
            ve.tensor_add(p.kvst[:, :], p.kvst[:, :], p.ps4[:, 0:512])
            ev.inc(ve.tensor_add(p.ksst[:, 0:8], p.ksst[:, 0:8], p.ps5[:, 0:8]))
            ve.wait_ge(ev.h, ev.n)
            ve.tensor_copy(p.kvstb[:, :], p.kvst[:, :])
            ev.inc(ve.tensor_copy(p.ksstb[:, :], p.ksst[:, :]))
        vec_(p, v_u)
        vu_ev[c] = ev.n

        if c % 4 == 3 and DO_WOLN:
            emit_a2_wo_ln(p, c // 4, atok_rdy, wo_rdy, vu_ev[c])


def emit_a2_wo_ln(p, gidx, atok_rdy, wo_rdy, vu_done):
    import os
    blk = p.block
    ld, wx, mm, ev, pl, ac, cc, st, dp = (p.sems[k] for k in
        ["ld", "wx", "mm", "ev", "pl", "ac", "cc", "st", "dp"])
    xq = p.sems["xr"]

    def s_xr(se, gidx=gidx):
        if gidx > 0:
            se.wait_ge(ev.h, p._a2_x1_done)
            if getattr(p, "_a2_x1_store", None) is not None:
                se.wait_ge(st.h, p._a2_x1_store)
        se.dma_start(out=sb3(p.xres[:, :], 512), in_=fm(p.x_fm, gidx * 512, 512)).then_inc(xq.h, 16)
        xq.n += 16
    blk.sync(s_xr)
    xr_rdy = xq.n

    tr_ev0 = ev.n
    tr_mm0 = mm.n

    def t_tr(te, atok_rdy=atok_rdy, gidx=gidx):
        te.wait_ge(ev.h, atok_rdy)
        if gidx > 0:
            te.wait_ge(ev.h, p._a2_afm_free)
        for idx in range(32):
            j, oc = divmod(idx, OC)
            bank = p.ps7 if idx % 2 == 0 else p.ps6
            if idx >= 2:
                te.wait_ge(ev.h, tr_ev0 + idx - 1)
            te.matmul(bank[:, 0:128],
                      p.atok[:, j * D + oc * 128: j * D + oc * 128 + 128],
                      p.mident[:, :], start=True, stop=True).then_inc(mm.h)
            mm.n += 1
    blk.tensor(t_tr)

    def v_tr(ve, tr_mm0=tr_mm0):
        for idx in range(32):
            j, oc = divmod(idx, OC)
            bank = p.ps7 if idx % 2 == 0 else p.ps6
            ve.wait_ge(mm.h, tr_mm0 + idx + 1)
            ev.inc(ve.tensor_copy(
                p.afm[:, oc * 512 + j * 128: oc * 512 + j * 128 + 128],
                bank[:, 0:128]))
    vec_(p, v_tr)
    afm_rdy = ev.n

    wo_ev = {}
    for oco in range(OC):
        bank = p.ps2 if oco % 2 == 0 else p.ps3

        def t_w(te, oco=oco, bank=bank, afm_rdy=afm_rdy):
            te.wait_ge(ev.h, afm_rdy)
            if oco == 0:
                te.wait_ge(ld.h, wo_rdy)
            g = wo_ev.get(oco - 2)
            if g is not None:
                te.wait_ge(ev.h, g)
            for ic in range(OC):
                ins = te.matmul(
                    bank[:, 0:512],
                    p.wbuf[:, ic * D + oco * 128: ic * D + oco * 128 + 128],
                    p.afm[:, ic * 512:(ic + 1) * 512],
                    start=(ic == 0), stop=(ic == OC - 1))
            ins.then_inc(mm.h); mm.n += 1
        blk.tensor(t_w)
        w_mm = mm.n

        def v_w(ve, oco=oco, bank=bank, w_mm=w_mm, xr_rdy=xr_rdy):
            ve.wait_ge(mm.h, w_mm)
            ve.wait_ge(p.sems["xr"].h, xr_rdy)
            ev.inc(ve.scalar_tensor_tensor(
                p.xres[:, oco * 512:(oco + 1) * 512],
                bank[:, 0:512], p.bcolo[:, oco:oco + 1],
                p.xres[:, oco * 512:(oco + 1) * 512],
                ALU.add, ALU.add))
        vec_(p, v_w)
        wo_ev[oco] = ev.n
    p._a2_afm_free = ev.n
    xsum_rdy = ev.n

    emit_ln_fm(p, p.xres, xsum_rdy, x1_store_gidx=gidx, vu_done=vu_done)


def emit_ln_fm(p, src, src_rdy, x1_store_gidx=None, gcol=None, bcol=None,
               out_sb=None, out_cb=None, lvl=4, vu_done=None, waits=()):
    blk = p.block
    mm, ev, ac, st, wx = (p.sems[k] for k in ["mm", "ev", "ac", "st", "wx"])

    pl = p.sems["pl"]
    ac0 = pl.n
    mm0 = mm.n

    def p_sq(po, src_rdy=src_rdy, mm0=mm0):
        if src_rdy is not None:
            po.wait_ge(ev.h, src_rdy)
        for hh, vv in waits:
            po.wait_ge(hh, vv)
        for oc in range(OC):
            if oc > 0:
                po.wait_ge(mm.h, mm0 + oc)
            pl.inc(po.tensor_tensor(p.xsq[:, 0:512], src[:, oc * 512:(oc + 1) * 512],
                                    src[:, oc * 512:(oc + 1) * 512], ALU.mult))
    pol_(p, p_sq)

    def t_sums(te, src_rdy=src_rdy, ac0=ac0, vu_done=vu_done):
        if src_rdy is not None:
            te.wait_ge(ev.h, src_rdy)
        for hh, vv in waits:
            te.wait_ge(hh, vv)
        if vu_done is not None:
            te.wait_ge(ev.h, vu_done)
        for oc in range(OC):
            te.matmul(p.ps4[0:1, 0:512], p.onesf[:, 0:1], src[:, oc * 512:(oc + 1) * 512],
                      start=(oc == 0), stop=(oc == OC - 1))
        for oc in range(OC):
            te.wait_ge(p.sems["pl"].h, ac0 + oc + 1)
            te.matmul(p.ps5[0:1, 0:512], p.onesf[:, 0:1], p.xsq[:, 0:512],
                      start=(oc == 0), stop=(oc == OC - 1)).then_inc(mm.h)
            mm.n += 1
    blk.tensor(t_sums)
    sums_mm = mm.n

    R = p.scr
    def v_r(ve, sums_mm=sums_mm):
        ve.wait_ge(mm.h, sums_mm)
        ev.inc(ve.tensor_scalar(R[0:1, 0:512], p.ps4[0:1, 0:512], 1.0 / D, None, ALU.mult))
        ve.wait_ge(ev.h, ev.n)
        ev.inc(ve.tensor_tensor(R[0:1, 512:1024], R[0:1, 0:512], R[0:1, 0:512], ALU.mult))
        ve.wait_ge(ev.h, ev.n)
        ev.inc(ve.scalar_tensor_tensor(
            R[0:1, 1024:1536], p.ps5[0:1, 0:512], 1.0 / D, R[0:1, 512:1024],
            ALU.mult, ALU.subtract))
    vec_(p, v_r)
    var_ev = ev.n
    p._a2_ln_var = var_ev

    def a_r(se, var_ev=var_ev):
        se.wait_ge(ev.h, var_ev)
        ac.inc(se.activation(R[0:1, 512:1024], R[0:1, 1024:1536], AF.Ln, bias=1e-5))
        se.wait_ge(ac.h, ac.n)
        ac.inc(se.activation(R[0:1, 1024:1536], R[0:1, 512:1024], AF.Exp, scale=-0.5))
    scl_(p, a_r)
    rstd_ac = ac.n

    def v_r2(ve, rstd_ac=rstd_ac):
        ve.wait_ge(ac.h, rstd_ac)
        ev.inc(ve.tensor_tensor(R[0:1, 512:1024], R[0:1, 0:512], R[0:1, 1024:1536], ALU.mult))
    vec_(p, v_r2)
    mur_ev = ev.n
    if lvl < 4:
        p._a2_x1_done = mur_ev
        return mur_ev

    def t_bc(te, mur_ev=mur_ev):
        te.wait_ge(ev.h, mur_ev)
        te.matmul(p.ps0[:, 0:512], p.onesrow_f[0:1, :], R[0:1, 1024:1536], start=True, stop=True)
        te.matmul(p.ps1[:, 0:512], p.onesrow_f[0:1, :], R[0:1, 512:1024],
                  start=True, stop=True).then_inc(mm.h)
        mm.n += 1
    blk.tensor(t_bc)
    bc_mm = mm.n

    dst = out_sb if out_sb is not None else src
    def v_a(ve, bc_mm=bc_mm):
        ve.wait_ge(mm.h, bc_mm)
        for oc in range(OC):
            ev.inc(ve.tensor_tensor(p.xsq[:, 0:512], src[:, oc * 512:(oc + 1) * 512],
                                    p.ps0[:, 0:512], ALU.mult))
            ve.wait_ge(ev.h, ev.n)
            if gcol is None:
                ev.inc(ve.tensor_tensor(dst[:, oc * 512:(oc + 1) * 512],
                                        p.xsq[:, 0:512], p.ps1[:, 0:512], ALU.subtract))
            else:
                ev.inc(ve.tensor_tensor(p.xsq[:, 0:512], p.xsq[:, 0:512], p.ps1[:, 0:512], ALU.subtract))
                ve.wait_ge(ev.h, ev.n)
                ev.inc(ve.tensor_scalar(dst[:, oc * 512:(oc + 1) * 512], p.xsq[:, 0:512],
                                        gcol[:, oc:oc + 1], bcol[:, oc:oc + 1],
                                        ALU.mult, ALU.add))
            ve.wait_ge(ev.h, ev.n)
    vec_(p, v_a)
    app_ev = ev.n
    p._a2_x1_done = app_ev

    if x1_store_gidx is not None:
        gidx = x1_store_gidx
        def s_o(se, gidx=gidx, app_ev=app_ev):
            se.wait_ge(ev.h, app_ev)
            se.dma_start(out=fm(p.x1s, gidx * 512, 512), in_=sb3(dst[:, :], 512)).then_inc(st.h, 16)
            st.n += 16
            if p.mode != "A":
                se.dma_start(out=fm(p.Xs, HALO + gidx * 512, 512), in_=sb3(dst[:, :], 512)).then_inc(st.h, 16)
                st.n += 16
        blk.sync(s_o)
        p._a2_x1_store = st.n
    return app_ev



def _run(p, per_core, outs=("out_fm",)):
    p.ctx.close()
    in_names = {a.memorylocations[0].name if hasattr(a, 'memorylocations') else None
                for a in []}
    res = run_bass_kernel_spmd(p.nc, per_core, core_ids=list(range(NCORES)))
    return res


def kernel_stage_a_test(**inputs):
    per_core = prep_inputs(inputs)
    p = build(mode="A")
    emit_stage_a(p)

    st, dp = p.sems["st"], p.sems["dp"]
    st_all = st.n

    import os
    def g_out(gp, st_all=st_all):
        gp.wait_ge(st.h, st_all)
        if os.environ.get("MK_DBG_ST", "0") == "1":
            gp.dma_start(out=p.out_fm.rearrange("a b -> (a b)")[0:128 * 512].rearrange("(a b) -> a b", b=512),
                         in_=p.kvst[:, :]).then_inc(dp.h, 16); dp.n += 16
            gp.dma_start(out=p.out_fm.rearrange("a b -> (a b)")[70000:70000 + 128 * 8].rearrange("(a b) -> a b", b=8),
                         in_=p.ksst[:, :]).then_inc(dp.h, 16); dp.n += 16
        elif os.environ.get("MK_DBG_CC", "0") == "1":
            gp.dma_start(out=p.out_fm.rearrange("a b -> (a b)")[0:1040 * 64].rearrange("(a b) -> a b", b=64),
                         in_=p.cc_kv_in).then_inc(dp.h, 16); dp.n += 16
            gp.dma_start(out=p.out_fm.rearrange("a b -> (a b)")[70000:70000 + 4160 * 64].rearrange("(a b) -> a b", b=64),
                         in_=p.cc_kv_out).then_inc(dp.h, 16); dp.n += 16
        else:
            gp.dma_start(out=p.out_fm, in_=p.x1s).then_inc(dp.h, 16); dp.n += 16
        gp.wait_ge(dp.h, dp.n)
    p.block.gpsimd(g_out)

    keys = ["x_fm", "wqT", "wkT", "wvT", "woT", "bq128", "bk128", "bo128",
            "bkrow", "bvrow", "sel", "invrow"]
    in_maps = [{k: pc[k] for k in keys} for pc in per_core]
    res = _run(p, in_maps)
    outs = [res.results[c]["out_fm"] for c in range(NCORES)]
    return np.concatenate([o.T for o in outs], axis=0)



def pol_(p, f):
    n = p.sems["pl"].n
    h = p.sems["pl"].h

    def g(po):
        po.wait_ge(h, n)
        f(po)
    p.block.gpsimd(g)


def alloc_stage_b(p):
    nc = p.nc
    sb = lambda n, s, dt=F32: p.ctx.enter_context(nc.sbuf_tensor(n, s, dt))
    p.WSLOT = sb("WSLOT", [128, 8 * 3072], BF16)
    p.kbuf = sb("kbuf", [128, 8 * 1024], BF16)
    p.vbuf = sb("vbuf", [128, 8 * 1024], BF16)
    p.qbuf = sb("qbuf", [128, 8 * 512], BF16)
    p.xnb = sb("xnb", [128, 8 * 2048], BF16)
    p.w1slot = sb("w1slot", [128, 2 * 2048], BF16)
    p.pbuf = sb("pbuf", [128, 2 * 384], BF16)
    p.ptb = sb("ptb", [128, 2 * 384], BF16)
    p.mrow = sb("mrow", [128, 8])
    p.tabsb = sb("tabsb", [128, 2 * 512], BF16)
    p.msk = sb("msk", [128, 4 * 384])
    p.mb2 = sb("mb2", [128, 2048], BF16)
    p.qspb = sb("qspb", [2, 256])
    p.selhb = sb("selhb", [128, 4])
    p.blockpat = sb("blockpat", [128, 2])
    p.ubuf = sb("ubuf", [128, 512], BF16)
    p.rbuf = sb("rbuf", [128, 512], BF16)
    p.lncol = sb("lncol", [128, 3 * OC])
    p.b1col = sb("b1col", [128, 44])
    p.afm2 = p.xnb[:, 0:8 * 512]
    p.nrm0 = p.xsq[64:66, 0:512]
    p.nrm1 = p.xsq[32:34, 0:512]
    p.nrm2 = p.xsq[0:2, 0:512]

    ld = p.sems["ld"]

    lv = p.sems["lv"]

    def g0(gp):
        gp.wait_ge(p.sems["fn"].h, p._fence_ld)
        gp.dma_start(out=p.msk[:, 0:384], in_=p.maskstd).then_inc(lv.h, 16); lv.n += 16
        gp.dma_start(out=sb3(p.msk[:, 384:4 * 384], 384),
                     in_=p.maskfirst.rearrange("j p t -> p j t")).then_inc(lv.h, 16); lv.n += 16
        gp.dma_start(out=p.selhb[:, :], in_=p.selh).then_inc(lv.h, 16); lv.n += 16
    p.block.gpsimd(g0)
    p._b_const_ld = lv.n

    def v0(ve):
        ev = p.sems["ev"]
        ve.wait_ge(p.sems["fn"].h, p._fence_ld)
        ev.inc(ve.memset(p.blockpat[:, 0:2], 0.0))
        ve.wait_ge(ev.h, ev.n)
        ve.memset(p.blockpat[0:64, 0:1], 1.0)
        ev.inc(ve.memset(p.blockpat[64:128, 1:2], 1.0))
    vec_(p, v0)
    p._b_const_ev = p.sems["ev"].n


ST_B = [(0, 512), (512, 512), (1024, 512), (1536, 512), (2048, 384)]
QT_GROUPS = [[0, 1], [2, 3, 4, 5], [6, 7, 8, 9], [10, 11, 12, 13], [14, 15, 16]]


def emit_cc_x(p, i, ready_st):
    blk = p.block
    ev, cc, st, dp, wx = (p.sems[k] for k in ["ev", "cc", "st", "dp", "wx"])
    if not hasattr(p, "_ccx_dp"):
        p._ccx_dp = {}
        p._ccx_stt = {}

    ev0 = p.sems["ev"].n
    k = ev0 + 1
    for r in range(4):
        for half in range(2):
            k += 1
            p._ccx_stt[(i, r, half)] = k
    final_ev = k + 1

    def g(gp, i=i, ready_st=ready_st):
        gp.wait_ge(st.h, ready_st)
        gp.dma_start(out=p.cc_x_in, in_=p.Xs[:, T_EXT - HALO:T_EXT]).then_inc(dp.h, 16)
        dp.n += 16
        gp.wait_ge(dp.h, dp.n)
        cc.inc(gp.collective_compute(
            "AllGather", ALU.bypass, replica_groups=RG,
            ins=[p.cc_x_in.opt()], outs=[p.cc_x_out.opt()]))
        gp.wait_ge(cc.h, cc.n)
        for r in range(4):
            for half in range(2):
                if not (r == 0 and half == 0):
                    gp.wait_ge(ev.h, p._ccx_stt[(i, r, half)] - 1)
                gp.dma_start(
                    out=sb3(p.scr[:, 0:4 * 384], 384),
                    in_=p.cc_x_out[r * 1024 + half * 512: r * 1024 + half * 512 + 512, :]
                        .rearrange("(c p) t -> p c t", p=128),
                ).then_inc(dp.h, 16); dp.n += 16
                p._ccx_dp[(i, r, half)] = dp.n
                gp.wait_ge(dp.h, dp.n)
        gp.wait_ge(ev.h, final_ev)
        gp.dma_start(out=fm(p.Xs, 0, HALO), in_=sb3(p.xres[:, 0:8 * 384], 384)).then_inc(dp.h, 16)
        dp.n += 16
        gp.wait_ge(dp.h, dp.n)
    blk.gpsimd(g)

    def v(ve, i=i, ready_st=ready_st):
        ve.wait_ge(st.h, ready_st)
        ve.wait_ge(p.sems["lv"].h, p._b_const_ld)
        ev.inc(ve.memset(p.xres[:, 0:8 * 384], 0.0))
        for r in range(4):
            for half in range(2):
                ve.wait_ge(dp.h, p._ccx_dp[(i, r, half)])
                ve.wait_ge(ev.h, ev.n)
                ev.inc(ve.scalar_tensor_tensor(
                    p.xres[:, half * 4 * 384:(half + 1) * 4 * 384],
                    p.scr[:, 0:4 * 384], p.selhb[:, r:r + 1],
                    p.xres[:, half * 4 * 384:(half + 1) * 4 * 384],
                    ALU.mult, ALU.add))
                assert p._ccx_stt[(i, r, half)] == ev.n
        ve.wait_ge(ev.h, ev.n)
        ev.inc(ve.tensor_copy(p.xres[0:1, 0:1], p.xres[0:1, 0:1]))
        assert final_ev == ev.n
    vec_(p, v)
    return dp.n


def emit_p1(p, i, halo_dp):
    p._halo_dp = halo_dp
    blk = p.block
    ld, wx, mm, ev, pl, ac, cc, st, dp, xq = (p.sems[k] for k in
        ["ld", "wx", "mm", "ev", "pl", "ac", "cc", "st", "dp", "xr"])

    def g_w(gp, i=i):
        if i == 0:
            gp.wait_ge(p.sems["fn"].h, p._fence_ld)
        gp.wait_ge(mm.h, p._wslot_free)
        gp.dma_start(out=sb3(p.WSLOT[:, :], 3 * D),
                     in_=p.wqkv[i].rearrange("(c p) o -> p c o", p=128)).then_inc(ld.h, 16)
        ld.n += 16
        gp.wait_ge(mm.h, p._wbuf_free)
        gp.dma_start(out=sb3(p.wbuf[:, :], D),
                     in_=p.wo_b[i].rearrange("(c p) o -> p c o", p=128)).then_inc(ld.h, 16)
        ld.n += 16
        gp.dma_start(out=p.qspb[0:2, 0:128], in_=p.qsp[i]).then_inc(ld.h, 16); ld.n += 16
        gp.dma_start(out=p.qspb[0:2, 128:256], in_=p.ksp[i]).then_inc(ld.h, 16); ld.n += 16
    blk.gpsimd(g_w)
    w_rdy = ld.n

    M = {}
    M["afm_free"] = {0: None, 1: None}
    M["x2st"] = {}
    qdone = {}

    for sti, (t0, wid) in enumerate(ST_B):
        nt = wid // 128

        def g_x(gp, sti=sti, t0=t0, wid=wid, i=i, halo_dp=halo_dp):
            gp.wait_ge(mm.h, M.get(("xbfree", sti), 0))
            if sti == 0:
                gp.wait_ge(dp.h, halo_dp)
                if i == 0 and p.mode != "A":
                    gp.wait_ge(st.h, p._a2_x1_store)
            gp.dma_start(out=sb3(p.xb[:, 0:4 * 512], 512)[:, :, 0:wid],
                         in_=fm(p.Xs, t0, wid)[:, 0:4, :]).then_inc(wx.h, 16)
            wx.n += 16
            if t0 > 0:
                gp.dma_start(out=sb3(p.xb[:, 4 * 512:8 * 512], 512)[:, :, 0:wid],
                             in_=fm(p.Xs, t0 - 1, wid)[:, 4:8, :]).then_inc(wx.h, 16)
                wx.n += 16
            else:
                gp.dma_start(out=sb3(p.xb[:, 4 * 512:8 * 512], 512)[:, :, 1:wid],
                             in_=fm(p.Xs, 0, wid - 1)[:, 4:8, :]).then_inc(wx.h, 16)
                wx.n += 16
                gp.dma_start(out=sb3(p.xb[:, 4 * 512:8 * 512], 512)[:, :, 0:1],
                             in_=fm(p.Xs, 0, 1)[:, 4:8, :],
                             allow_slow_non_contiguous=True).then_inc(wx.h, 16)
                wx.n += 16
        blk.gpsimd(g_x)
        x_rdy = wx.n

        for qk in range(2):
            tabs_free = M.get(("tabfree",))

            def g_tab(gp, qk=qk, t0=t0, wid=wid, tabs_free=tabs_free, i=i):
                if tabs_free is not None:
                    gp.wait_ge(ev.h, tabs_free)
                elif i == 0:
                    gp.wait_ge(p.sems["fn"].h, p._fence_ld)
                srcs = [p.tab_cq, p.tab_sq] if qk == 0 else [p.tab_ck, p.tab_sk]
                fnm = p.sems["tb"]
                for tbl, src_t in enumerate(srcs):
                    gp.dma_start(out=p.tabsb[:, tbl * 512: tbl * 512 + wid],
                                 in_=src_t[:, t0:t0 + wid]).then_inc(fnm.h, 16)
                    fnm.n += 16
            blk.gpsimd(g_tab)
            tab_rdy = p.sems["tb"].n
            for oc in range(OC):
                pp = (qk * OC + oc) % 2
                bank = p.ps0 if pp == 0 else p.ps1
                fbank = p.ps2 if pp == 0 else p.ps3

                def t_p(te, qk=qk, oc=oc, sti=sti, wid=wid, bank=bank, x_rdy=x_rdy, pp=pp,
                        evbar=(p.sems["ev"].n if (qk == 0 and oc == 0) else None)):
                    te.wait_ge(wx.h, x_rdy)
                    if evbar is not None:
                        te.wait_ge(ev.h, evbar)
                    if sti == 0 and qk == 0 and oc == 0:
                        te.wait_ge(ld.h, w_rdy)
                        te.wait_ge(ev.h, p._b_const_ev)
                        te.wait_ge(p.sems["lv"].h, p._b_const_ld)
                        if getattr(p, "_p1_prev_done", None) is not None:
                            te.wait_ge(ev.h, p._p1_prev_done)
                    g = M.get(("projfree", pp))
                    if g is not None:
                        te.wait_ge(ac.h, g)
                    for dc in range(OC):
                        ins = te.matmul(
                            bank[:, 0:wid],
                            p.WSLOT[:, dc * 3 * D + qk * D + oc * 128: dc * 3 * D + qk * D + oc * 128 + 128],
                            p.xb[:, dc * 512: dc * 512 + wid],
                            start=(dc == 0), stop=(dc == OC - 1))
                    ins.then_inc(mm.h); mm.n += 1
                blk.tensor(t_p)
                mmv = mm.n

                def a_e(se, bank=bank, wid=wid, mmv=mmv):
                    se.wait_ge(mm.h, mmv)
                    g = M.get(("scrfree",))
                    if g is not None:
                        se.wait_ge(ev.h, g)
                    ac.inc(se.copy(p.scr[:, 0:wid], bank[:, 0:wid]))
                scl_(p, a_e)
                M[("projfree", pp)] = ac.n
                acv = ac.n

                def p_sq1(po, acv=acv, wid=wid):
                    po.wait_ge(ac.h, acv)
                    g = M.get(("sqfree",))
                    if g is not None:
                        po.wait_ge(mm.h, g)
                    pl.inc(po.tensor_tensor(p.scr[:, 512:512 + wid], p.scr[:, 0:wid],
                                            p.scr[:, 0:wid], ALU.mult))
                pol_(p, p_sq1)
                plv = pl.n

                def t_f(te, plv=plv, wid=wid, fbank=fbank, pp=pp):
                    te.wait_ge(pl.h, plv)
                    g = M.get(("facfree", pp))
                    if g is not None:
                        te.wait_ge(ev.h, g)
                    g2 = M.get(("nrmfree",))
                    if g2 is not None:
                        te.wait_ge(ev.h, g2)
                    te.matmul(p.ps6[0:2, 0:wid], p.blockpat[:, 0:2],
                              p.scr[:, 512:512 + wid],
                              start=True, stop=True).then_inc(mm.h)
                    mm.n += 1
                blk.tensor(t_f)
                nmv = mm.n
                M[("sqfree",)] = nmv

                def v_f(ve, nmv=nmv, wid=wid):
                    ve.wait_ge(mm.h, nmv)
                    ev.inc(ve.tensor_scalar(p.nrm0[0:2, 0:wid], p.ps6[0:2, 0:wid],
                                            1e-24, None, ALU.max))
                vec_(p, v_f)
                evf = ev.n
                M[("nrmfree",)] = ev.n

                def a_f(se, evf=evf, wid=wid):
                    se.wait_ge(ev.h, evf)
                    ac.inc(se.activation(p.nrm1[0:2, 0:wid],
                                         p.nrm0[0:2, 0:wid], AF.Ln))
                    se.wait_ge(ac.h, ac.n)
                    ac.inc(se.activation(p.nrm2[0:2, 0:wid],
                                         p.nrm1[0:2, 0:wid], AF.Exp, scale=-0.5))
                scl_(p, a_f)
                acf = ac.n

                def t_f2(te, qk=qk, acf=acf, wid=wid, fbank=fbank):
                    te.wait_ge(ac.h, acf)
                    te.matmul(fbank[:, 0:wid], p.qspb[0:2, qk * 128:qk * 128 + 128],
                              p.nrm2[0:2, 0:wid],
                              start=True, stop=True).then_inc(mm.h)
                    mm.n += 1
                blk.tensor(t_f2)
                fmv = mm.n

                def v_rot(ve, qk=qk, oc=oc, t0=t0, wid=wid, fmv=fmv, fbank=fbank, tab_rdy=tab_rdy):
                    ve.wait_ge(mm.h, fmv)
                    ve.wait_ge(p.sems["tb"].h, tab_rdy)
                    ev.inc(ve.tensor_tensor(p.ubuf[:, 0:wid], p.scr[:, 0:wid],
                                            fbank[:, 0:wid], ALU.mult))
                    ve.wait_ge(ev.h, ev.n)
                    ve.tensor_copy(p.rbuf[0:32, 0:wid], p.ubuf[32:64, 0:wid])
                    ve.tensor_copy(p.rbuf[32:64, 0:wid], p.ubuf[0:32, 0:wid])
                    ve.tensor_copy(p.rbuf[64:96, 0:wid], p.ubuf[96:128, 0:wid])
                    ev.inc(ve.tensor_copy(p.rbuf[96:128, 0:wid], p.ubuf[64:96, 0:wid]))
                    ve.wait_ge(ev.h, ev.n)
                    ctab = p.tabsb[:, 0:wid]
                    stab = p.tabsb[:, 512:512 + wid]
                    ev.inc(ve.tensor_tensor(p.ubuf[:, 0:wid], p.ubuf[:, 0:wid], ctab, ALU.mult))
                    ev.inc(ve.tensor_tensor(p.rbuf[:, 0:wid], p.rbuf[:, 0:wid], stab, ALU.mult))
                    ve.wait_ge(ev.h, ev.n)
                    if qk == 0:
                        dst = p.qbuf[:, oc * 512: oc * 512 + wid]
                    else:
                        ring0 = (t0 // 128) % 8
                        dst = p.kbuf[:, oc * 1024 + ring0 * 128: oc * 1024 + ring0 * 128 + wid]
                    ev.inc(ve.tensor_add(dst, p.ubuf[:, 0:wid], p.rbuf[:, 0:wid]))
                vec_(p, v_rot)
                M[("facfree", pp)] = ev.n
                M[("scrfree",)] = ev.n
                M[("tabfree",)] = ev.n

        for j in range(nt):
            for half in range(2):
                pp = (j * 2 + half) % 2
                bank = p.ps0 if pp == 0 else p.ps1

                def t_v(te, j=j, half=half, bank=bank, x_rdy=x_rdy, pp=pp):
                    te.wait_ge(wx.h, x_rdy)
                    g = M.get(("vfree", pp))
                    if g is not None:
                        te.wait_ge(ev.h, g)
                    g2 = M.get(("projfree", pp))
                    if g2 is not None:
                        te.wait_ge(ac.h, g2)
                    for dc in range(OC):
                        ins = te.matmul(
                            bank[:, 0:512],
                            p.xb[:, dc * 512 + j * 128: dc * 512 + j * 128 + 128],
                            p.WSLOT[:, dc * 3 * D + 2 * D + half * 512: dc * 3 * D + 2 * D + half * 512 + 512],
                            start=(dc == 0), stop=(dc == OC - 1))
                    ins.then_inc(mm.h); mm.n += 1
                blk.tensor(t_v)
                mmv = mm.n

                def v_v(ve, j=j, half=half, t0=t0, mmv=mmv, bank=bank):
                    ve.wait_ge(mm.h, mmv)
                    ring = ((t0 // 128) + j) % 8
                    ev.inc(ve.tensor_copy(
                        p.vbuf[:, ring * 1024 + half * 512: ring * 1024 + half * 512 + 512],
                        bank[:, 0:512]))
                vec_(p, v_v)
                M[("vfree", pp)] = ev.n
        proj_ev = ev.n
        M[("xbfree", sti + 1)] = mm.n

        ready = [j for j in range(17) if t0 <= (j + 2) * 128 < t0 + wid]
        for j in ready:
            qcol = (j + 2) * 128 - t0
            group = next(gi for gi, g_ in enumerate(QT_GROUPS) if j in g_)
            gslot = group % 2
            afm_t = p.afm if gslot == 0 else p.afm2
            jj = j - QT_GROUPS[group][0]
            for h in range(H):
                par = h % 2
                oc = h // 2
                slot = h % 2
                sbank = p.ps2 if h % 2 == 0 else p.ps3
                tbank = p.ps4 if h % 2 == 0 else p.ps5
                pvbank = p.ps6 if h % 2 == 0 else p.ps7

                def t_s(te, j=j, h=h, par=par, oc=oc, qcol=qcol, sbank=sbank,
                        proj_ev=proj_ev, first=(j == ready[0] and h == 0), group=group, gslot=gslot):
                    if first:
                        te.wait_ge(ev.h, proj_ev)
                        g0_ = M["afm_free"][gslot]
                        if g0_ is not None:
                            te.wait_ge(ev.h, g0_)
                    g = M.get(("sfree", id(sbank)))
                    if g is not None:
                        te.wait_ge(ev.h, g)
                    kt = 0
                    while kt < 3:
                        r0 = (j + kt) % 8
                        run = min(3 - kt, 8 - r0)
                        ins = te.matmul(
                            sbank[:, kt * 128: (kt + run) * 128],
                            p.qbuf[par * 64:(par + 1) * 64, oc * 512 + qcol: oc * 512 + qcol + 128],
                            p.kbuf[par * 64:(par + 1) * 64, oc * 1024 + r0 * 128: oc * 1024 + (r0 + run) * 128],
                            start=True, stop=True)
                        kt += run
                    ins.then_inc(mm.h); mm.n += 1
                blk.tensor(t_s)
                smm = mm.n

                def v_cpm(ve, j=j, smm=smm, sbank=sbank, slot=slot):
                    ve.wait_ge(mm.h, smm)
                    g = M.get(("scrAfree", slot))
                    if g is not None:
                        ve.wait_ge(ac.h, g)
                    mk = p.msk[:, (j + 1) * 384:(j + 2) * 384] if j < 3 else p.msk[:, 0:384]
                    ev.inc(ve.tensor_add(p.scr[:, slot * 512: slot * 512 + 384],
                                         sbank[:, 0:384], mk))
                vec_(p, v_cpm)
                M[("sfree", id(sbank))] = ev.n

                def v_mx(ve, slot=slot):
                    g = M.get(("mrowfree", slot))
                    if g is not None:
                        ve.wait_ge(ac.h, g)
                    ev.inc(ve.reduce_max(p.mrow[:, slot * 4: slot * 4 + 1],
                                         p.scr[:, slot * 512: slot * 512 + 384],
                                         axis=AX, negate=True))
                vec_(p, v_mx)
                evm = ev.n

                def a_ex(se, slot=slot, evm=evm):
                    se.wait_ge(ev.h, evm)
                    g = M.get(("pbfree", slot))
                    if g is not None:
                        se.wait_ge(ev.h, g)
                    ac.inc(se.activation(p.pbuf[:, slot * 384:(slot + 1) * 384],
                                         p.scr[:, slot * 512: slot * 512 + 384],
                                         AF.Exp, bias=p.mrow[:, slot * 4: slot * 4 + 1],
                                         accum_out=p.mrow[:, slot * 4 + 1: slot * 4 + 2]))
                scl_(p, a_ex)
                M[("scrAfree", slot)] = ac.n
                M[("mrowfree", slot)] = ac.n
                acex = ac.n

                def v_sc(ve, slot=slot, acex=acex):
                    ve.wait_ge(ac.h, acex)
                    ev.inc(ve.reciprocal(p.mrow[:, slot * 4 + 2: slot * 4 + 3],
                                         p.mrow[:, slot * 4 + 1: slot * 4 + 2]))
                    ve.wait_ge(ev.h, ev.n)
                    ev.inc(ve.tensor_scalar(p.pbuf[:, slot * 384:(slot + 1) * 384],
                                            p.pbuf[:, slot * 384:(slot + 1) * 384],
                                            p.mrow[:, slot * 4 + 2: slot * 4 + 3],
                                            None, ALU.mult))
                vec_(p, v_sc)
                evp = ev.n

                def t_tr(te, slot=slot, evp=evp, tbank=tbank):
                    te.wait_ge(ev.h, evp)
                    g = M.get(("trfree", id(tbank)))
                    if g is not None:
                        te.wait_ge(ev.h, g)
                    for kt in range(3):
                        ins = te.matmul(
                            tbank[:, kt * 128: kt * 128 + 128],
                            p.pbuf[:, slot * 384 + kt * 128: slot * 384 + kt * 128 + 128],
                            p.mident[:, :], start=True, stop=True)
                    ins.then_inc(mm.h); mm.n += 1
                blk.tensor(t_tr)
                trmm = mm.n

                def v_tr(ve, slot=slot, trmm=trmm, tbank=tbank):
                    ve.wait_ge(mm.h, trmm)
                    g = M.get(("ptfree", slot))
                    if g is not None:
                        ve.wait_ge(mm.h, g)
                    ev.inc(ve.tensor_copy(p.ptb[:, slot * 384:(slot + 1) * 384],
                                          tbank[:, 0:384]))
                vec_(p, v_tr)
                M[("trfree", id(tbank))] = ev.n
                M[("pbfree", slot)] = ev.n
                evt = ev.n

                def t_pv(te, j=j, h=h, slot=slot, evt=evt, pvbank=pvbank):
                    te.wait_ge(ev.h, evt)
                    g = M.get(("pvfree", id(pvbank)))
                    if g is not None:
                        te.wait_ge(ev.h, g)
                    for kt in range(3):
                        ring = (j + kt) % 8
                        ins = te.matmul(
                            pvbank[0:64, 0:128],
                            p.vbuf[:, ring * 1024 + h * HD: ring * 1024 + h * HD + 64],
                            p.ptb[:, slot * 384 + kt * 128: slot * 384 + kt * 128 + 128],
                            start=(kt == 0), stop=(kt == 2))
                    ins.then_inc(mm.h); mm.n += 1
                blk.tensor(t_pv)
                M[("ptfree", slot)] = mm.n
                pvmm = mm.n

                def v_pv(ve, h=h, par=par, oc=oc, jj=jj, pvmm=pvmm, pvbank=pvbank, afm_t=afm_t):
                    ve.wait_ge(mm.h, pvmm)
                    ev.inc(ve.tensor_copy(
                        afm_t[par * 64:(par + 1) * 64, oc * 512 + jj * 128: oc * 512 + jj * 128 + 128],
                        pvbank[0:64, 0:128]))
                vec_(p, v_pv)
                M[("pvfree", id(pvbank))] = ev.n

            if j == QT_GROUPS[group][-1]:
                emit_p1_wo(p, i, group, M)
        qdone[sti] = ev.n

    p._p1_prev_done = p.sems["ev"].n
    p._p1_last_mm = mm.n
    return M


def emit_p1_wo(p, i, group, M):
    blk = p.block
    ld, wx, mm, ev, pl, ac, st, xq, dp = (p.sems[k] for k in
        ["ld", "wx", "mm", "ev", "pl", "ac", "st", "xr", "dp"])
    js = QT_GROUPS[group]
    glen = len(js) * 128
    e0 = 256 + js[0] * 128
    gslot = group % 2
    afm_t = p.afm if gslot == 0 else p.afm2
    afm_rdy = p.sems["ev"].n

    def s_x(se, group=group, e0=e0, glen=glen):
        g = M.get(("xresfree",))
        if g is not None:
            se.wait_ge(ev.h, g)
        g2 = M["x2st"].get(group - 1)
        if g2 is not None:
            se.wait_ge(st.h, g2)
        if group == 0:
            se.wait_ge(dp.h, p._halo_dp)
        se.dma_start(out=sb3(p.xres[:, 0:8 * 512], 512)[:, :, 0:glen],
                     in_=fm(p.Xs, e0, glen)).then_inc(xq.h, 16)
        xq.n += 16
    blk.sync(s_x)
    xr_rdy = xq.n

    woev = {}
    for oco in range(OC):
        bank = p.ps0 if oco % 2 == 0 else p.ps1

        def t_w(te, oco=oco, bank=bank, glen=glen, afm_rdy=afm_rdy, group=group, afm_t=afm_t):
            te.wait_ge(ev.h, afm_rdy)
            g = woev.get(oco - 2)
            if g is not None:
                te.wait_ge(ev.h, g)
            g2 = M.get(("wo_projpend", oco % 2))
            if g2 is not None:
                te.wait_ge(ac.h, g2)
            for ic in range(OC):
                ins = te.matmul(
                    bank[:, 0:glen],
                    p.wbuf[:, ic * D + oco * 128: ic * D + oco * 128 + 128],
                    afm_t[:, ic * 512: ic * 512 + glen],
                    start=(ic == 0), stop=(ic == OC - 1))
            ins.then_inc(mm.h); mm.n += 1
        blk.tensor(t_w)
        wmm = mm.n

        def v_w(ve, oco=oco, bank=bank, wmm=wmm, xr_rdy=xr_rdy, glen=glen):
            ve.wait_ge(mm.h, wmm)
            ve.wait_ge(xq.h, xr_rdy)
            ev.inc(ve.tensor_add(
                p.xres[:, oco * 512: oco * 512 + glen],
                bank[:, 0:glen],
                p.xres[:, oco * 512: oco * 512 + glen]))
        vec_(p, v_w)
        woev[oco] = ev.n
    M["afm_free"][gslot] = ev.n
    M[("xresfree",)] = ev.n
    x2_ev = ev.n

    def s_o(se, group=group, e0=e0, glen=glen, x2_ev=x2_ev):
        se.wait_ge(ev.h, x2_ev)
        se.dma_start(out=fm(p.X2s, e0, glen),
                     in_=sb3(p.xres[:, 0:8 * 512], 512)[:, :, 0:glen]).then_inc(st.h, 16)
        st.n += 16
    blk.sync(s_o)
    M["x2st"][group] = st.n


def emit_p2(p, i):
    blk = p.block
    ld, wx, mm, ev, pl, ac, st, dp, xq = (p.sems[k] for k in
        ["ld", "wx", "mm", "ev", "pl", "ac", "st", "dp", "xr"])

    lv = p.sems["lv"]

    def g_lnv(gp, i=i):
        gp.wait_ge(st.h, p._p1_x2st_all)
        gp.dma_start(out=p.lncol[:, 0:OC], in_=p.ln_g[i]).then_inc(lv.h, 16); lv.n += 16
        gp.dma_start(out=p.lncol[:, OC:2 * OC], in_=p.ln_b[i]).then_inc(lv.h, 16); lv.n += 16
        gp.dma_start(out=p.lncol[:, 2 * OC:3 * OC], in_=p.b2c[i]).then_inc(lv.h, 16); lv.n += 16
        gp.dma_start(out=p.b1col[:, :], in_=p.b1c[i]).then_inc(lv.h, 16); lv.n += 16
    blk.gpsimd(g_lnv)
    lnv_rdy = lv.n

    last_store = None
    for tt in range(4):
        e0 = HALO + tt * 512

        def s_x2(se, tt=tt, e0=e0, i=i):
            g = getattr(p, "_p2_xres_free", None)
            if g is not None:
                se.wait_ge(ev.h, g)
            se.wait_ge(st.h, p._p1_x2st_all)
            se.dma_start(out=sb3(p.xres[:, 0:8 * 512], 512)[:, 0:4, :],
                         in_=fm(p.X2s, e0, 512)[:, 0:4, :]).then_inc(xq.h, 16)
            xq.n += 16
            se.dma_start(out=sb3(p.xres[:, 0:8 * 512], 512)[:, 4:8, :],
                         in_=fm(p.X2s, e0 - 1, 512)[:, 4:8, :]).then_inc(xq.h, 16)
            xq.n += 16
        blk.sync(s_x2)
        x2_rdy = xq.n

        emit_ln_fm(p, p.xres, None, gcol=p.lncol[:, 0:OC], bcol=p.lncol[:, OC:2 * OC],
                   out_sb=p.xnb[:, tt * 4096:(tt + 1) * 4096],
                   waits=[(xq.h, x2_rdy), (lv.h, lnv_rdy)])
        p._p2_xres_free = p._a2_x1_done

    xn_rdy = p._a2_x1_done

    w2s = p.sems["w2"]

    def g_w2(gp, i=i):
        gp.wait_ge(mm.h, p._p1_last_mm)
        gp.dma_start(out=sb3(p.WSLOT[:, 0:22 * 1024], 1024),
                     in_=p.w2[i].rearrange("(j p) o -> p j o", p=128)).then_inc(w2s.h, 16)
        w2s.n += 16
    blk.gpsimd(g_w2)
    w2_rdy = w2s.n

    ghst = {}
    for jp in range(22):
        wslot2 = (jp % 2) * 2048

        def g_w1(gp, jp=jp, i=i, wslot2=wslot2):
            g = getattr(p, "_p2_w1_free", {}).get(jp - 2)
            if g is not None:
                gp.wait_ge(mm.h, g)
            if jp < 2:
                gp.wait_ge(mm.h, p._p1_last_mm)
            g2 = M2get(p, ("w1cons", jp - 1 if jp >= 1 else 21))
            if g2 is not None:
                gp.wait_ge(mm.h, g2)
            gp.dma_start(out=sb3(p.w1slot[:, wslot2:wslot2 + 1024], 128),
                         in_=p.w1[i].rearrange("(c p) o -> p c o", p=128)[:, :, jp * 128:(jp + 1) * 128]
                         ).then_inc(ld.h, 16)
            ld.n += 16
            gp.dma_start(out=sb3(p.w1slot[:, wslot2 + 1024:wslot2 + 2048], 128),
                         in_=p.w1[i].rearrange("(c p) o -> p c o", p=128)[:, :, (jp + 22) * 128:(jp + 23) * 128]
                         ).then_inc(ld.h, 16)
            ld.n += 16
        blk.gpsimd(g_w1)
        w1_rdy = ld.n

        mb_ev = {}
        for tt in range(4):
            pa = p.ps0 if tt % 2 == 0 else p.ps2
            pg = p.ps1 if tt % 2 == 0 else p.ps3

            def t_h(te, jp=jp, tt=tt, pa=pa, pg=pg, w1_rdy=w1_rdy, xn_rdy=xn_rdy, wslot2=wslot2):
                if tt == 0:
                    te.wait_ge(ld.h, w1_rdy)
                    if jp == 0:
                        te.wait_ge(ev.h, xn_rdy)
                g = M2get(p, ("hfree", tt % 2))
                if g is not None:
                    te.wait_ge(ev.h, g)
                for dc in range(OC):
                    te.matmul(pa[:, 0:512],
                              p.w1slot[:, wslot2 + dc * 128: wslot2 + dc * 128 + 128],
                              p.xnb[:, tt * 4096 + dc * 512: tt * 4096 + (dc + 1) * 512],
                              start=(dc == 0), stop=(dc == OC - 1))
                for dc in range(OC):
                    ins = te.matmul(pg[:, 0:512],
                                    p.w1slot[:, wslot2 + 1024 + dc * 128: wslot2 + 1024 + dc * 128 + 128],
                                    p.xnb[:, tt * 4096 + dc * 512: tt * 4096 + (dc + 1) * 512],
                                    start=(dc == 0), stop=(dc == OC - 1))
                ins.then_inc(mm.h); mm.n += 1
            blk.tensor(t_h)
            hmm = mm.n
            if tt == 0:
                M2set(p, ("w1cons", jp), hmm)
            if not hasattr(p, "_p2_w1_free"):
                p._p2_w1_free = {}
            p._p2_w1_free[jp] = hmm

            def a_g(se, jp=jp, tt=tt, pg=pg, hmm=hmm):
                se.wait_ge(mm.h, hmm)
                g = M2get(p, ("gactfree", tt % 2))
                if g is not None:
                    se.wait_ge(ev.h, g)
                g3 = M2get(p, ("xsqfree",))
                if g3 is not None:
                    se.wait_ge(ev.h, g3)
                ac.inc(se.activation(p.xsq[:, 0:512], pg[:, 0:512], AF.Gelu,
                                     bias=p.b1col[:, jp + 22: jp + 23]))
            scl_(p, a_g)
            agm = ac.n

            def v_m(ve, jp=jp, tt=tt, pa=pa, agm=agm):
                ve.wait_ge(ac.h, agm)
                g = M2get(p, ("mbfree",))
                if g is not None:
                    ve.wait_ge(dp.h, g)
                ev.inc(ve.scalar_tensor_tensor(
                    p.mb2[:, tt * 512:(tt + 1) * 512],
                    pa[:, 0:512], p.b1col[:, jp:jp + 1],
                    p.xsq[:, 0:512], ALU.add, ALU.mult))
            vec_(p, v_m)
            M2set(p, ("hfree", tt % 2), ev.n)
            M2set(p, ("gactfree", tt % 2), ev.n)
            M2set(p, ("xsqfree",), ev.n)
            mb_ev[tt] = ev.n

        def g_mb(gp, jp=jp, mb_ev=dict(mb_ev)):
            gp.wait_ge(ev.h, mb_ev[3])
            gp.dma_start(out=p.gh[jp * 128:(jp + 1) * 128, :], in_=p.mb2[:, :]).then_inc(dp.h, 16)
            dp.n += 16
        blk.gpsimd(g_mb)
        ghst[jp] = dp.n
        M2set(p, ("mbfree",), dp.n)
    gh_all = dp.n

    def mchunk(slot, jc):
        if slot == 0:
            return p.xnb[:, jc * 512:(jc + 1) * 512]
        if jc < 10:
            return p.xnb[:, 11264 + jc * 512: 11264 + (jc + 1) * 512]
        return p.kbuf[:, (jc - 10) * 512:(jc - 9) * 512]

    ghv = p.gh.rearrange("(j p) t -> p j t", p=128)
    x3st = 0
    for tt in range(4):
        e0 = HALO + tt * 512
        slot = tt % 2

        def g_m(gp, tt=tt, slot=slot, gh_all=gh_all):
            gp.wait_ge(dp.h, gh_all)
            g = M2get(p, ("xnfree", slot))
            if g is not None:
                gp.wait_ge(mm.h, g)
            g2 = M2get(p, ("mcons", tt - 1 if tt >= 1 else 3))
            if g2 is not None:
                gp.wait_ge(mm.h, g2)
            ts = slice(tt * 512, (tt + 1) * 512)
            if slot == 0:
                gp.dma_start(out=sb3(p.xnb[:, 0:22 * 512], 512),
                             in_=ghv[:, :, ts]).then_inc(wx.h, 16)
                wx.n += 16
            else:
                gp.dma_start(out=sb3(p.xnb[:, 11264:11264 + 10 * 512], 512),
                             in_=ghv[:, 0:10, ts]).then_inc(wx.h, 16)
                wx.n += 16
                gp.dma_start(out=sb3(p.kbuf[:, 0:12 * 512], 512),
                             in_=ghv[:, 10:22, ts]).then_inc(wx.h, 16)
                wx.n += 16
        blk.gpsimd(g_m)
        m_rdy = wx.n

        def s_x2b(se, tt=tt, e0=e0, xn_rdy=xn_rdy):
            g = M2get(p, ("x2bfree",))
            if g is not None:
                se.wait_ge(ev.h, g)
            if tt == 0:
                se.wait_ge(ev.h, xn_rdy)
            g2 = M2get(p, ("x3st",))
            if g2 is not None:
                se.wait_ge(st.h, g2)
            se.dma_start(out=sb3(p.xres[:, 0:8 * 512], 512),
                         in_=fm(p.X2s, e0, 512)).then_inc(xq.h, 16)
            xq.n += 16
        blk.sync(s_x2b)
        x2b_rdy = xq.n

        for oco in range(OC):
            bank = p.ps4 if oco % 2 == 0 else p.ps5

            def t_o(te, tt=tt, slot=slot, oco=oco, bank=bank, m_rdy=m_rdy):
                te.wait_ge(wx.h, m_rdy)
                if tt == 0 and oco == 0:
                    te.wait_ge(w2s.h, w2_rdy)
                g = M2get(p, ("obfree", oco % 2))
                if g is not None:
                    te.wait_ge(ev.h, g)
                for jc in range(22):
                    ins = te.matmul(
                        bank[:, 0:512],
                        p.WSLOT[:, jc * 1024 + oco * 128: jc * 1024 + oco * 128 + 128],
                        mchunk(slot, jc),
                        start=(jc == 0), stop=(jc == 21))
                ins.then_inc(mm.h); mm.n += 1
            blk.tensor(t_o)
            omm = mm.n
            if oco == 0:
                M2set(p, ("mcons", tt), omm)

            def v_o(ve, tt=tt, oco=oco, bank=bank, omm=omm, x2b_rdy=x2b_rdy):
                ve.wait_ge(mm.h, omm)
                ve.wait_ge(xq.h, x2b_rdy)
                ev.inc(ve.scalar_tensor_tensor(
                    p.xres[:, oco * 512:(oco + 1) * 512],
                    bank[:, 0:512], p.lncol[:, 2 * OC + oco: 2 * OC + oco + 1],
                    p.xres[:, oco * 512:(oco + 1) * 512],
                    ALU.add, ALU.add))
            vec_(p, v_o)
            M2set(p, ("obfree", oco % 2), ev.n)
        M2set(p, ("xnfree", slot), mm.n)
        M2set(p, ("x2bfree",), ev.n)
        x3_ev = ev.n

        def s_x3(se, tt=tt, e0=e0, x3_ev=x3_ev):
            se.wait_ge(ev.h, x3_ev)
            se.dma_start(out=fm(p.Xs, e0, 512),
                         in_=sb3(p.xres[:, 0:8 * 512], 512)).then_inc(st.h, 16)
            st.n += 16
        blk.sync(s_x3)
        x3st = st.n
        M2set(p, ("x3st",), st.n)
    p._wslot_free = mm.n
    return x3st


def M2get(p, k):
    return getattr(p, "_M2", {}).get(k)


def M2set(p, k, v):
    if not hasattr(p, "_M2"):
        p._M2 = {}
    p._M2[k] = v


def emit_final(p):
    blk = p.block
    ev, st, xq, dp = (p.sems[k] for k in ["ev", "st", "xr", "dp"])
    for gidx in range(4):
        def s_l(se, gidx=gidx):
            g = getattr(p, "_fin_free", None)
            if g is not None:
                se.wait_ge(ev.h, g)
            g2 = getattr(p, "_fin_stp", None)
            if g2 is not None:
                se.wait_ge(st.h, g2)
            se.wait_ge(st.h, p._x6_st)
            se.dma_start(out=sb3(p.xres[:, 0:8 * 512], 512),
                         in_=fm(p.Xs, HALO + gidx * 512, 512)).then_inc(xq.h, 16)
            xq.n += 16
            se.dma_start(out=sb3(p.xnb.bitcast(F32)[:, 0:4096], 512),
                         in_=fm(p.x1s, gidx * 512, 512)).then_inc(xq.h, 16)
            xq.n += 16
        blk.sync(s_l)
        l_rdy = xq.n

        def v_add(ve, l_rdy=l_rdy):
            ve.wait_ge(xq.h, l_rdy)
            ev.inc(ve.tensor_add(p.xres[:, 0:4096], p.xres[:, 0:4096],
                                 p.xnb.bitcast(F32)[:, 0:4096]))
        vec_(p, v_add)
        emit_ln_fm(p, p.xres, None, waits=[(ev.h, ev.n)])
        app = p._a2_x1_done
        p._fin_free = app

        def s_out(se, gidx=gidx, app=app):
            se.wait_ge(ev.h, app)
            se.dma_start(out=fm(p.out_fm, gidx * 512, 512),
                         in_=sb3(p.xres[:, 0:8 * 512], 512)).then_inc(st.h, 16)
            st.n += 16
        blk.sync(s_out)
        p._fin_stp = st.n
    fin_st = st.n

    def g_end(gp, fin_st=fin_st):
        gp.wait_ge(st.h, fin_st)
    blk.gpsimd(g_end)


def kernel(**inputs):
    per_core = prep_inputs(inputs)
    p = build(mode="full")
    emit_stage_a(p)
    p.actx.close()
    alloc_stage_b(p)
    p._wslot_free = 0
    p._wbuf_free = p.sems["mm"].n
    p._p1_x2st_all = 0
    ready_st = p._a2_x1_store
    for i in range(DEPTH):
        halo_dp = emit_cc_x(p, i, ready_st)
        M = emit_p1(p, i, halo_dp)
        p._p1_x2st_all = p.sems["st"].n
        ready_st = emit_p2(p, i)
        p._wbuf_free = p.sems["mm"].n
    p._x6_st = ready_st
    emit_final(p)
    p.ctx.close()

    keys = ["x_fm", "wqT", "wkT", "wvT", "woT", "bq128", "bk128", "bo128",
            "bkrow", "bvrow", "sel", "invrow",
            "wqkv", "wo_b", "w1", "w2", "ln_g", "ln_b", "b2c", "b1c",
            "qsp", "ksp", "tab_cq", "tab_sq", "tab_ck", "tab_sk",
            "maskstd", "maskfirst", "selh"]
    in_maps = [{k: pc[k] for k in keys} for pc in per_core]
    res = run_bass_kernel_spmd(p.nc, in_maps, core_ids=list(range(NCORES)))
    outs = [res.results[c]["out_fm"] for c in range(NCORES)]
    full = np.concatenate([o.T for o in outs], axis=0)
    return np.ascontiguousarray(full.reshape(B, TSEQ, D))



# version 04
# speedup vs baseline: 1.1563x; 1.0351x over previous
import contextlib
import os
import numpy as np
import ml_dtypes
import concourse.bass as bass
import concourse.mybir as mybir
from concourse.bass_utils import run_bass_kernel_spmd

F32 = mybir.dt.float32
BF16 = mybir.dt.bfloat16
AF = mybir.ActivationFunctionType
ALU = mybir.AluOpType
AX = mybir.AxisListType.X
BF16NP = ml_dtypes.bfloat16

NCORES = 8
D = 1024
H = 16
HD = 64
OC = 8
T_OWN = 2048
HALO = 384
T_EXT = T_OWN + HALO
B = 2
TSEQ = 8192
EPS_A = 1e-6
DEPTH = 6
DI = 2816
DH = 2 * DI
DI_REAL = 2730
NEG_BIG = -1e30
RG = [[0, 1, 2, 3], [4, 5, 6, 7]]



def _perm_qk():
    p = np.zeros(D, np.int64)
    for c in range(OC):
        ha, hb = 2 * c, 2 * c + 1
        b0 = c * 128
        p[b0 + 0:b0 + 64] = ha * HD + np.arange(64)
        p[b0 + 64:b0 + 128] = hb * HD + np.arange(64)
    return p


PERM_QK = _perm_qk()


def _core_tables(core):
    half = np.arange(0, HD, 2, dtype=np.float32)
    inv_freq = 1.0 / 10000.0 ** (half / HD)
    pos0 = (core % 4) * T_OWN
    idx = np.clip(pos0 - HALO + np.arange(T_EXT), 0, TSEQ - 1).astype(np.float32)
    f = idx[:, None] * inv_freq[None, :]
    base_scale = (half + 0.4 * HD) / (1.4 * HD)
    power = (idx - TSEQ // 2) / 128.0
    sc = base_scale[None, :] ** power[:, None]
    cos, sin = np.cos(f), np.sin(f)
    inv = 1.0 / sc

    def build(ct, st):
        o = np.empty((128, T_EXT), np.float32)
        o[0:32] = ct.T; o[32:64] = ct.T; o[64:96] = ct.T; o[96:128] = ct.T
        s = np.empty((128, T_EXT), np.float32)
        s[0:32] = -st.T; s[32:64] = st.T; s[64:96] = -st.T; s[96:128] = st.T
        return o, s

    cq, sq = build(cos * sc, sin * sc)
    ck, sk = build(cos * inv, sin * inv)
    return cq, sq, ck, sk


def _fmcols(v):
    return np.ascontiguousarray(v.reshape(OC, 128).T)


def prep_inputs(inputs):
    x = np.ascontiguousarray(inputs["L0_attn_x"].astype(np.float32).reshape(B * TSEQ, D))
    shared = {
        "wqT": np.ascontiguousarray(inputs["L0_attn_Wq"].T),
        "wkT": np.ascontiguousarray(inputs["L0_attn_Wk"].T),
        "wvT": np.ascontiguousarray(inputs["L0_attn_Wv"].T),
        "woT": np.ascontiguousarray(inputs["L0_attn_Wo"].T),
        "bq128": _fmcols(inputs["L0_attn_bq"]),
        "bk128": _fmcols(inputs["L0_attn_bk"]),
        "bo128": _fmcols(inputs["L0_attn_bo"]),
        "bkrow": inputs["L0_attn_bk"].reshape(1, D),
        "bvrow": inputs["L0_attn_bv"].reshape(1, D),
    }
    wqkv = inputs["L0_ffn_Wqkv"]
    wq_b = wqkv[:, :, 0:D][:, :, PERM_QK]
    wk_b = wqkv[:, :, D:2 * D][:, :, PERM_QK]
    wv_b = wqkv[:, :, 2 * D:3 * D]
    shared["wqkv"] = np.ascontiguousarray(np.concatenate([wq_b, wk_b, wv_b], axis=2))
    shared["wo_b"] = np.ascontiguousarray(inputs["L0_ffn_Wo"])
    w1 = inputs["L0_ffn_W1"]; b1 = inputs["L0_ffn_b1"]
    w1p = np.zeros((DEPTH, D, DH), np.float32)
    w1p[:, :, 0:DI_REAL] = w1[:, :, 0:DI_REAL]
    w1p[:, :, DI:DI + DI_REAL] = w1[:, :, DI_REAL:2 * DI_REAL]
    shared["w1"] = np.ascontiguousarray(w1p)
    b1p = np.zeros((DEPTH, DH), np.float32)
    b1p[:, 0:DI_REAL] = b1[:, 0:DI_REAL]
    b1p[:, DI:DI + DI_REAL] = b1[:, DI_REAL:2 * DI_REAL]
    shared["b1c"] = np.ascontiguousarray(b1p.reshape(DEPTH, 44, 128).transpose(0, 2, 1))
    w2p = np.zeros((DEPTH, DI, D), np.float32)
    w2p[:, 0:DI_REAL, :] = inputs["L0_ffn_W2"]
    shared["w2"] = w2p
    shared["ln_g"] = np.stack([_fmcols(inputs["L0_ffn_ln_g"][i]) for i in range(DEPTH)])
    shared["ln_b"] = np.stack([_fmcols(inputs["L0_ffn_ln_b"][i]) for i in range(DEPTH)])
    shared["b2c"] = np.stack([_fmcols(inputs["L0_ffn_b2"][i]) for i in range(DEPTH)])
    src_rows = np.concatenate([np.arange(64), np.arange(64)])
    ha = np.zeros(128, np.float32); ha[0:64] = 1
    qsp = np.zeros((DEPTH, 2, 128), np.float32)
    ksp = np.zeros((DEPTH, 2, 128), np.float32)
    for i in range(DEPTH):
        qs = inputs["L0_ffn_q_scale"][i]; ks = inputs["L0_ffn_k_scale"][i]
        qsp[i, 0] = 8.0 * qs[src_rows] * ha; qsp[i, 1] = 8.0 * qs[src_rows] * (1 - ha)
        ksp[i, 0] = ks[src_rows] * ha; ksp[i, 1] = ks[src_rows] * (1 - ha)
    shared["qsp"] = qsp
    shared["ksp"] = ksp
    ii = np.arange(128)[:, None]; ss = np.arange(384)[None, :]
    mstd = np.where((ss >= ii) & (ss <= ii + 256), 0.0, NEG_BIG).astype(np.float32)
    shared["maskstd"] = mstd

    per_core = []
    for core in range(NCORES):
        s = core * T_OWN
        cq, sq, ck, sk = _core_tables(core)
        sel = np.zeros((128, 4), np.float32)
        sel[:, 0:core % 4] = 1.0
        selh = np.zeros((128, 4), np.float32)
        if core % 4 > 0:
            selh[:, core % 4 - 1] = 1.0
        inv = np.zeros((1, T_EXT), np.float32)
        if core % 4 == 0:
            inv[0, 0:HALO] = 1.0
        mfirst = np.stack([shared["maskstd"]] * 3)
        if core % 4 == 0:
            ii = np.arange(128)[:, None]; ss = np.arange(384)[None, :]
            for j in range(3):
                kb = 128 * j - 384 + ss
                mfirst[j] = np.where(kb < 0, NEG_BIG, mfirst[j])
        d = dict(shared)
        d["x_fm"] = np.ascontiguousarray(x[s:s + T_OWN].T)
        d["tab_cq"] = cq; d["tab_sq"] = sq; d["tab_ck"] = ck; d["tab_sk"] = sk
        d["sel"] = sel
        d["selh"] = selh
        d["invrow"] = inv
        d["maskfirst"] = np.ascontiguousarray(mfirst)
        per_core.append(d)
    return per_core



class Sem:
    __slots__ = ("h", "n")

    def __init__(self, h):
        self.h = h
        self.n = 0

    def inc(self, ins, k=1):
        if k == 1:
            ins.then_inc(self.h)
        else:
            ins.then_inc(self.h, k)
        self.n += k
        return self.n


class P:
    pass


def fm(ap_dram, c0, w, nchunk=OC):
    return ap_dram.rearrange("(c p) t -> p c t", p=128)[:, :, c0:c0 + w]


def sb3(ap_sb, w):
    return ap_sb.rearrange("p (c t) -> p c t", t=w)


def wview(w_dram):
    return w_dram.rearrange("(c p) o -> p c o", p=128)



def vec_(p, f):
    n = p.sems["ev"].n
    h = p.sems["ev"].h

    def g(ve):
        ve.wait_ge(h, n)
        f(ve)
    p.block.vector(g)


def scl_(p, f):
    n = p.sems["ac"].n
    h = p.sems["ac"].h

    def g(se):
        se.wait_ge(h, n)
        f(se)
    p.block.scalar(g)


def build(mode="A"):
    ctx = contextlib.ExitStack()
    import os
    drc = os.environ.get("MK_DETECT_RACES", "0") == "1"
    nc = bass.Bass("TRN2", target_bir_lowering=False, debug=False,
                   detect_race_conditions=drc)
    p = P()
    p.nc, p.ctx, p.mode = nc, ctx, mode

    din = lambda n, s, dt=F32: nc.dram_tensor(n, s, dt, kind="ExternalInput").ap()
    _dbg = os.environ.get("MK_DBG_DRAM", "0") == "1"
    dint = lambda n, s, dt=F32: nc.dram_tensor(
        n, s, dt, kind="ExternalOutput" if _dbg and not n.startswith("cc_") else "Internal").ap()

    p.x_fm = din("x_fm", [D, T_OWN])
    p.wqT = din("wqT", [D, D]); p.wkT = din("wkT", [D, D])
    p.wvT = din("wvT", [D, D]); p.woT = din("woT", [D, D])
    p.bq128 = din("bq128", [128, OC]); p.bk128 = din("bk128", [128, OC])
    p.bo128 = din("bo128", [128, OC])
    p.bkrow = din("bkrow", [1, D]); p.bvrow = din("bvrow", [1, D])
    p.sel = din("sel", [128, 4])
    p.invrow = din("invrow", [1, T_EXT])
    if mode != "A":
        p.wqkv = din("wqkv", [DEPTH, D, 3 * D])
        p.wo_b = din("wo_b", [DEPTH, D, D])
        p.w1 = din("w1", [DEPTH, D, DH])
        p.w2 = din("w2", [DEPTH, DI, D])
        p.ln_g = din("ln_g", [DEPTH, 128, OC]); p.ln_b = din("ln_b", [DEPTH, 128, OC])
        p.b2c = din("b2c", [DEPTH, 128, OC]); p.b1c = din("b1c", [DEPTH, 128, 44])
        p.qsp = din("qsp", [DEPTH, 2, 128]); p.ksp = din("ksp", [DEPTH, 2, 128])
        p.maskstd = din("maskstd", [128, 384])
        p.maskfirst = din("maskfirst", [3, 128, 384])
        p.selh = din("selh", [128, 4])
        p.tab_cq = din("tab_cq", [128, T_EXT]); p.tab_sq = din("tab_sq", [128, T_EXT])
        p.tab_ck = din("tab_ck", [128, T_EXT]); p.tab_sk = din("tab_sk", [128, T_EXT])

    p.out_fm = nc.dram_tensor("out_fm", [D, T_OWN], F32, kind="ExternalOutput").ap()

    p.Xs = dint("Xs", [D, T_EXT])
    p.x1s = dint("x1s", [D, T_OWN])
    p.cc_kv_in = dint("cc_kv_in", [1040, 64])
    p.cc_kv_out = dint("cc_kv_out", [4160, 64])
    if mode != "A":
        p.X2s = dint("X2s", [D, T_EXT])
        p.ms = dint("ms", [64, 128])
        p.gh = dint("gh", [DI, T_OWN], BF16)
        p.cc_x_in = dint("cc_x_in", [D, HALO])
        p.cc_x_out = dint("cc_x_out", [4 * D, HALO])

    p.c_ident = nc.inline_tensor(np.eye(128, dtype=BF16NP), "c_ident").ap()
    p.c_maskA = nc.inline_tensor(np.triu(np.ones((128, 128), np.float32)).astype(BF16NP), "c_maskA").ap()
    if mode != "A":
        p.c_maskB0 = nc.inline_tensor(np.tril(np.ones((128, 128), np.float32)).astype(BF16NP), "c_maskB0").ap()
        p.c_maskB2 = nc.inline_tensor(np.triu(np.ones((128, 128), np.float32)).astype(BF16NP), "c_maskB2").ap()

    p.actx = contextlib.ExitStack()
    sb = lambda n, s, dt=F32: ctx.enter_context(nc.sbuf_tensor(n, s, dt))
    sba = lambda n, s, dt=F32: p.actx.enter_context(nc.sbuf_tensor(n, s, dt))
    psa = lambda n, s, dt=F32: ctx.enter_context(nc.psum_tensor(n, s, dt))

    p.ps0 = psa("ps0", [128, 512]); p.ps1 = psa("ps1", [128, 512])
    p.ps2 = psa("ps2", [128, 512]); p.ps3 = psa("ps3", [128, 512])
    p.ps4 = psa("ps4", [128, 512]); p.ps5 = psa("ps5", [128, 512])
    p.ps6 = psa("ps6", [128, 512])
    p.ps7 = psa("ps7", [128, 512])

    p.wbuf = sb("wbuf", [128, OC * D], BF16)
    p.xb = sb("xb", [128, OC * 512], BF16)
    p.scr = sb("scr", [128, 3 * 512])
    p.brow = sb("brow", [1, 2 * D], BF16)
    p.bcolq = sb("bcolq", [128, OC]); p.bcolk = sb("bcolk", [128, OC])
    p.bcolo = sb("bcolo", [128, OC])
    p.onesb = sb("onesb", [128, 1], BF16)
    p.onesf = sb("onesf", [128, 1])
    p.onesrow_b = sb("onesrow_b", [1, 128], BF16)
    p.onesrow_f = sb("onesrow_f", [1, 128])
    p.selb = sb("selb", [128, 4])
    p.afm = sb("afm", [128, OC * 512], BF16)
    p.xres = sb("xres", [128, OC * 512])
    p.xsq = sb("xsq", [128, 512])
    p.mident = sb("mident", [128, 128], BF16)
    p.cvals = sb("cvals", [128, 2])
    nc.const_aps.register(F32, 1e-5, p.cvals[:, 0:1])
    nc.const_aps.register(F32, -0.5, p.cvals[:, 1:2])
    p.kvpre = p.scr[:, 0:1024]

    p.qA = sba("qA", [128, OC * T_OWN], BF16)
    p.kA = sba("kA", [128, OC * T_OWN], BF16)
    p.ktokA = sba("ktokA", [128, 16 * D], BF16)
    p.vtokA = sba("vtokA", [128, 16 * D], BF16)
    p.kvseg = sba("kvseg", [64, 512])
    p.kvst = sba("kvst", [128, 8 * 64])
    p.kvstb = sba("kvstb", [128, 8 * 64], BF16)
    p.ksst = sba("ksst", [128, 8]); p.ksstb = sba("ksstb", [128, 8], BF16)
    p.ksst2 = sba("ksst2", [128, 8])
    p.kspre = sba("kspre", [128, 32])
    p.smt = sba("smt", [128, 2 * 128], BF16)
    p.atok = sba("atok", [128, 4 * D], BF16)
    p.rden = sba("rden", [128, 16])
    p.mmaskA = sba("mmaskA", [128, 128], BF16)

    p.sems = {}
    for name in ["ld", "wx", "mm", "ev", "pl", "ac", "cc", "st", "dp", "xr", "fn", "tb", "lv", "w2"]:
        p.sems[name] = Sem(ctx.enter_context(nc.semaphore("s_" + name)))

    p.block = ctx.enter_context(nc.Block())
    return p



def emit_stage_a(p):
    blk = p.block
    ld, wx, mm, ev, pl, ac, cc, st, dp = (p.sems[k] for k in
        ["ld", "wx", "mm", "ev", "pl", "ac", "cc", "st", "dp"])

    def g_ld(gp):
        gp.dma_start(out=p.mident[:, :], in_=p.c_ident).then_inc(ld.h, 16); ld.n += 16
        gp.dma_start(out=p.mmaskA[:, :], in_=p.c_maskA).then_inc(ld.h, 16); ld.n += 16
        gp.dma_start(out=p.bcolq[:, :], in_=p.bq128).then_inc(ld.h, 16); ld.n += 16
        gp.dma_start(out=p.bcolk[:, :], in_=p.bk128).then_inc(ld.h, 16); ld.n += 16
        gp.dma_start(out=p.bcolo[:, :], in_=p.bo128).then_inc(ld.h, 16); ld.n += 16
        gp.dma_start(out=p.brow[:, 0:D], in_=p.bkrow).then_inc(ld.h, 16); ld.n += 16
        gp.dma_start(out=p.brow[:, D:2 * D], in_=p.bvrow).then_inc(ld.h, 16); ld.n += 16
        gp.dma_start(out=p.selb[:, :], in_=p.sel).then_inc(ld.h, 16); ld.n += 16
        gp.dma_start(out=sb3(p.wbuf[:, :], D), in_=wview(p.wqT)).then_inc(ld.h, 16); ld.n += 16
    blk.gpsimd(g_ld)
    ld_init = ld.n

    def v_ones(ve):
        ve.memset(p.cvals[:, 0:1], 1e-5)
        ve.memset(p.cvals[:, 1:2], -0.5)
        ve.memset(p.onesb[:, :], 1.0)
        ve.memset(p.onesf[:, :], 1.0)
        ve.memset(p.onesrow_b[:, :], 1.0)
        ev.inc(ve.memset(p.onesrow_f[:, :], 1.0))
    vec_(p, v_ones)
    ones_rdy = ev.n

    st8 = {"evlast": None, "aclast": None, "xfree": 0, "wrdy": ld_init}

    def emit_fm_pass(dst, bcol, has_phi=True):
        for ti in range(4):
            def s_x(se, ti=ti):
                se.wait_ge(mm.h, st8["xfree"])
                se.dma_start(out=sb3(p.xb[:, :], 512), in_=fm(p.x_fm, ti * 512, 512)).then_inc(wx.h, 16)
                wx.n += 16
            blk.gpsimd(s_x)
            xr = wx.n
            for oc in range(OC):
                def t_p(te, oc=oc, ti=ti, xr=xr, wrdy=st8["wrdy"], g=st8["evlast"]):
                    te.wait_ge(wx.h, xr)
                    if ti == 0 and oc == 0:
                        te.wait_ge(ld.h, wrdy)
                        te.wait_ge(ev.h, ones_rdy)
                    if g is not None:
                        te.wait_ge(ev.h, g)
                    for dc in range(OC):
                        ins = te.matmul(
                            p.ps0[:, 0:512],
                            p.wbuf[:, dc * D + oc * 128: dc * D + oc * 128 + 128],
                            p.xb[:, dc * 512:(dc + 1) * 512],
                            start=(dc == 0), stop=(dc == OC - 1))
                    ins.then_inc(mm.h); mm.n += 1
                blk.tensor(t_p)
                mmv = mm.n

                def v_p(ve, oc=oc, bcol=bcol, mmv=mmv, g=st8["aclast"]):
                    ve.wait_ge(mm.h, mmv)
                    if g is not None:
                        ve.wait_ge(ac.h, g)
                    ve.tensor_scalar(p.scr[:, 0:512], p.ps0[:, 0:512],
                                     bcol[:, oc:oc + 1], 0.0, ALU.add, ALU.max)
                    ev.inc(ve.tensor_scalar(p.scr[:, 512:1024], p.ps0[:, 0:512],
                                            bcol[:, oc:oc + 1], 0.0, ALU.add, ALU.min))
                vec_(p, v_p)
                st8["evlast"] = ev.n
                evv = ev.n

                def a_p(se, evv=evv):
                    se.wait_ge(ev.h, evv)
                    ac.inc(se.activation(p.scr[:, 1024:1536], p.scr[:, 512:1024], AF.Exp))
                scl_(p, a_p)
                st8["aclast"] = ac.n
                acv = ac.n

                def v_p2(ve, oc=oc, ti=ti, acv=acv, dst=dst):
                    ve.wait_ge(ac.h, acv)
                    ev.inc(ve.tensor_add(
                        dst[:, oc * T_OWN + ti * 512: oc * T_OWN + ti * 512 + 512],
                        p.scr[:, 0:512], p.scr[:, 1024:1536]))
                vec_(p, v_p2)
            st8["xfree"] = mm.n

    def emit_tok_pass(dst, brow_off, has_phi):
        for ti in range(4):
            def s_x(se, ti=ti):
                se.wait_ge(mm.h, st8["xfree"])
                se.dma_start(out=sb3(p.xb[:, :], 512), in_=fm(p.x_fm, ti * 512, 512)).then_inc(wx.h, 16)
                wx.n += 16
            blk.gpsimd(s_x)
            xr = wx.n
            for j in range(4):
                tt = ti * 4 + j
                for half in range(2):
                    def t_t(te, j=j, half=half, ti=ti, xr=xr, wrdy=st8["wrdy"], g=st8["evlast"]):
                        te.wait_ge(wx.h, xr)
                        if ti == 0 and j == 0 and half == 0:
                            te.wait_ge(ld.h, wrdy)
                        if g is not None:
                            te.wait_ge(ev.h, g)
                        for dc in range(OC):
                            te.matmul(
                                p.ps0[:, 0:512],
                                p.xb[:, dc * 512 + j * 128: dc * 512 + j * 128 + 128],
                                p.wbuf[:, dc * D + half * 512: dc * D + half * 512 + 512],
                                start=(dc == 0), stop=False,
                            )
                        te.matmul(
                            p.ps0[:, 0:512],
                            p.onesrow_b[0:1, :],
                            p.brow[0:1, brow_off + half * 512: brow_off + half * 512 + 512],
                            start=False, stop=True,
                        ).then_inc(mm.h); mm.n += 1
                    blk.tensor(t_t)
                    mmv = mm.n

                    if has_phi:
                        def v_t(ve, mmv=mmv, g=st8["aclast"]):
                            ve.wait_ge(mm.h, mmv)
                            if g is not None:
                                ve.wait_ge(ac.h, g)
                            ve.tensor_scalar(p.scr[:, 0:512], p.ps0[:, 0:512], 0.0, None, ALU.max)
                            ev.inc(ve.tensor_scalar(p.scr[:, 512:1024], p.ps0[:, 0:512], 0.0, None, ALU.min))
                        vec_(p, v_t)
                        st8["evlast"] = ev.n
                        evv = ev.n

                        def a_t(se, evv=evv):
                            se.wait_ge(ev.h, evv)
                            ac.inc(se.activation(p.scr[:, 1024:1536], p.scr[:, 512:1024], AF.Exp))
                        scl_(p, a_t)
                        st8["aclast"] = ac.n
                        acv = ac.n

                        def v_t2(ve, tt=tt, half=half, acv=acv, dst=dst):
                            ve.wait_ge(ac.h, acv)
                            ev.inc(ve.tensor_add(
                                dst[:, tt * D + half * 512: tt * D + half * 512 + 512],
                                p.scr[:, 0:512], p.scr[:, 1024:1536]))
                        vec_(p, v_t2)
                    else:
                        def v_t(ve, tt=tt, half=half, mmv=mmv, dst=dst):
                            ve.wait_ge(mm.h, mmv)
                            ev.inc(ve.tensor_copy(
                                dst[:, tt * D + half * 512: tt * D + half * 512 + 512],
                                p.ps0[:, 0:512]))
                        vec_(p, v_t)
                        st8["evlast"] = ev.n
            st8["xfree"] = mm.n

    def load_w(src):
        mm_before = mm.n

        def g_w(gp, mm_before=mm_before):
            gp.wait_ge(mm.h, mm_before)
            gp.dma_start(out=sb3(p.wbuf[:, :], D), in_=wview(src)).then_inc(ld.h, 16)
            ld.n += 16
        blk.gpsimd(g_w)
        st8["wrdy"] = ld.n

    emit_fm_pass(p.qA, p.bcolq)
    load_w(p.wkT)
    emit_fm_pass(p.kA, p.bcolk)
    emit_tok_pass(p.ktokA, 0, True)
    load_w(p.wvT)
    emit_tok_pass(p.vtokA, D, False)
    a1_done_ev = ev.n
    a1_done_mm = mm.n

    def t_seg(te):
        te.wait_ge(ev.h, a1_done_ev)
        for h in range(H):
            bank = p.ps4 if h < 8 else p.ps5
            col = (h % 8) * 64
            for j in range(16):
                ins = te.matmul(
                    bank[0:64, col:col + 64],
                    p.ktokA[:, j * D + h * HD: j * D + h * HD + 64],
                    p.vtokA[:, j * D + h * HD: j * D + h * HD + 64],
                    start=(j == 0), stop=(j == 15),
                )
        ins.then_inc(mm.h); mm.n += 1
    blk.tensor(t_seg)
    seg_mm = mm.n

    def v_seg(ve):
        ve.wait_ge(ev.h, a1_done_ev)
        for oc in range(OC):
            ve.reduce_sum(p.ksst[:, oc:oc + 1], p.kA[:, oc * T_OWN:(oc + 1) * T_OWN], axis=AX)
        ve.wait_ge(mm.h, seg_mm)
        ev.inc(ve.tensor_copy(p.kvseg[0:64, 0:512], p.ps4[0:64, 0:512]))
    vec_(p, v_seg)
    seg_ev1 = ev.n

    import os
    if os.environ.get("MK_CUT_CC", "0") == "1":
        def v_zero(ve):
            ve.memset(p.kvst[:, :], 0.0)
            ve.memset(p.ksst[:, :], 0.0)
            ve.memset(p.kvstb[:, :], 0.0)
            ev.inc(ve.memset(p.ksstb[:, :], 0.0))
        vec_(p, v_zero)
        state_rdy = ev.n
        if os.environ.get("MK_CUT_A2", "0") != "1":
            emit_a2(p, state_rdy, a1_done_mm)
        return

    def g_cc1(gp):
        gp.wait_ge(ev.h, seg_ev1)
        gp.dma_start(
            out=p.cc_kv_in[0:512, :].rearrange("(hh d) e -> d hh e", hh=8),
            in_=sb3(p.kvseg[0:64, :], 64)).then_inc(dp.h, 16); dp.n += 16
        gp.dma_start(
            out=p.cc_kv_in[1024:1040, :].rearrange("r (pp c) -> (r pp) c", c=8),
            in_=p.ksst[:, :]).then_inc(dp.h, 16); dp.n += 16
    blk.gpsimd(g_cc1)
    dma1 = dp.n

    def v_seg2(ve):
        ve.wait_ge(dp.h, dma1)
        ev.inc(ve.tensor_copy(p.kvseg[0:64, 0:512], p.ps5[0:64, 0:512]))
    vec_(p, v_seg2)
    seg_ev2 = ev.n

    def g_cc(gp):
        gp.wait_ge(ev.h, seg_ev2)
        gp.dma_start(
            out=p.cc_kv_in[512:1024, :].rearrange("(hh d) e -> d hh e", hh=8),
            in_=sb3(p.kvseg[0:64, :], 64)).then_inc(dp.h, 16); dp.n += 16
        gp.wait_ge(dp.h, dp.n)
        cc.inc(gp.collective_compute(
            "AllGather", ALU.bypass, replica_groups=RG,
            ins=[p.cc_kv_in.opt()], outs=[p.cc_kv_out.opt()]))
        gp.wait_ge(cc.h, cc.n)
        for r in range(4):
            b0 = r * 1040
            src2 = p.cc_kv_out[b0:b0 + 1024, :].rearrange("(j t d) e -> t d j e", j=8, t=2)
            if r >= 2:
                gp.wait_ge(ev.h, p._cc_stt[r - 2])
            sl = (r % 2) * 512
            gp.dma_start(out=sb3(p.kvpre[0:64, sl:sl + 512], 64), in_=src2[0]).then_inc(dp.h, 16); dp.n += 16
            gp.dma_start(out=sb3(p.kvpre[64:128, sl:sl + 512], 64), in_=src2[1]).then_inc(dp.h, 16); dp.n += 16
            gp.dma_start(out=p.kspre[:, r * 8:(r + 1) * 8],
                         in_=p.cc_kv_out[b0 + 1024:b0 + 1040, :].rearrange(
                             "r (pp c) -> (r pp) c", c=8)).then_inc(dp.h, 16); dp.n += 16
            gp.wait_ge(dp.h, dp.n)
            p._cc_dp[r] = dp.n
    p._cc_dp = {}
    p._cc_stt = {r: ev.n + 2 + 2 * (r + 1) for r in range(4)}
    blk.gpsimd(g_cc)

    def v_cc(ve):
        ev.inc(ve.memset(p.kvst[:, :], 0.0))
        ev.inc(ve.memset(p.ksst2[:, :], 0.0))
        for r in range(4):
            sl = (r % 2) * 512
            ve.wait_ge(dp.h, p._cc_dp[r])
            ve.wait_ge(ev.h, ev.n)
            ev.inc(ve.scalar_tensor_tensor(p.kvst[:, :], p.kvpre[:, sl:sl + 512], p.selb[:, r:r + 1],
                                           p.kvst[:, :], ALU.mult, ALU.add))
            ve.wait_ge(ev.h, ev.n)
            ev.inc(ve.scalar_tensor_tensor(p.ksst2[:, :], p.kspre[:, r * 8:(r + 1) * 8],
                                           p.selb[:, r:r + 1], p.ksst2[:, :], ALU.mult, ALU.add))
            assert p._cc_stt[r] == ev.n
        ve.wait_ge(ev.h, ev.n)
        ev.inc(ve.tensor_copy(p.ksst[:, :], p.ksst2[:, :]))
        ve.wait_ge(ev.h, ev.n)
        ve.tensor_copy(p.kvstb[:, :], p.kvst[:, :])
        ev.inc(ve.tensor_copy(p.ksstb[:, :], p.ksst[:, :]))
    vec_(p, v_cc)
    state_rdy = ev.n
    import os as _os
    if _os.environ.get("MK_CUT_A2", "0") == "1":
        return


    emit_a2(p, state_rdy, a1_done_mm)

    a_end = {k: p.sems[k].n for k in ("mm", "ev", "ac", "pl")}

    fn = p.sems["fn"]

    def g_fence(gp, a_end=a_end):
        for k in ("mm", "ev", "ac", "pl"):
            gp.wait_ge(p.sems[k].h, a_end[k])
        gp.sem_inc(fn.h, 1)
    blk.gpsimd(g_fence)
    fn.n += 1
    p._fence_ld = fn.n


def emit_a2(p, state_rdy, a1_done_mm):
    import os
    NCH = int(os.environ.get("MK_A2_NCH", "16"))
    DO_WOLN = os.environ.get("MK_A2_WOLN", "1") == "1"
    blk = p.block
    ld, wx, mm, ev, pl, ac, cc, st, dp = (p.sems[k] for k in
        ["ld", "wx", "mm", "ev", "pl", "ac", "cc", "st", "dp"])

    def g_wo(gp):
        gp.wait_ge(mm.h, a1_done_mm)
        gp.dma_start(out=sb3(p.wbuf[:, :], D), in_=wview(p.woT)).then_inc(ld.h, 16)
        ld.n += 16
    blk.gpsimd(g_wo)
    wo_rdy = ld.n

    sev = {}
    smt_free = {}
    vdiv_ev = {}
    vu_ev = {}

    for c in range(NCH):
        for h in range(H):
            par = h % 2
            sbank = p.ps0 if h % 2 == 0 else p.ps1
            sslot = h % 2

            def t_s(te, c=c, h=h, par=par, sbank=sbank):
                if c == 0 and h == 0:
                    te.wait_ge(ev.h, state_rdy)
                if h == 0:
                    g = vdiv_ev.get(c - 1)
                    if g is not None:
                        te.wait_ge(ev.h, g)
                    gu = vu_ev.get(c - 1)
                    if gu is not None:
                        te.wait_ge(ev.h, gu)
                    if c % 4 == 0 and c > 0 and getattr(p, "_a2_x1_done", None) is not None:
                        te.wait_ge(ev.h, p._a2_x1_done)
                g = sev.get((c, h - 2)) or (sev.get((c - 1, h + 14)) if h < 2 else None)
                if g is not None:
                    te.wait_ge(ev.h, g)
                te.matmul(
                    sbank[:, 0:128],
                    p.kA[par * 64:(par + 1) * 64, (h // 2) * T_OWN + c * 128:(h // 2) * T_OWN + c * 128 + 128],
                    p.qA[par * 64:(par + 1) * 64, (h // 2) * T_OWN + c * 128:(h // 2) * T_OWN + c * 128 + 128],
                    start=True, stop=True,
                ).then_inc(mm.h); mm.n += 1
            blk.tensor(t_s)
            s_mm = mm.n

            def v_m(ve, s_mm=s_mm, sbank=sbank, sslot=sslot, c=c, h=h):
                ve.wait_ge(mm.h, s_mm)
                g = smt_free.get((c, h - 2)) or (smt_free.get((c - 1, h + 14)) if h < 2 else None)
                if g is not None:
                    ve.wait_ge(mm.h, g)
                ev.inc(ve.tensor_tensor(
                    p.smt[:, sslot * 128:(sslot + 1) * 128],
                    sbank[:, 0:128], p.mmaskA[:, :], ALU.mult))
            vec_(p, v_m)
            sev[(c, h)] = ev.n
            smt_rdy = ev.n

            numbank = p.ps2 if h < 8 else p.ps3
            ncol = (h % 8) * 64

            def t_n(te, c=c, h=h, par=par, sslot=sslot, numbank=numbank, ncol=ncol,
                    smt_rdy=smt_rdy):
                te.wait_ge(ev.h, smt_rdy)
                te.matmul(numbank[:, ncol:ncol + 64],
                          p.smt[:, sslot * 128:(sslot + 1) * 128],
                          p.vtokA[:, c * D + h * HD: c * D + h * HD + 64],
                          start=True, stop=False)
                te.matmul(numbank[:, ncol:ncol + 64],
                          p.qA[par * 64:(par + 1) * 64, (h // 2) * T_OWN + c * 128:(h // 2) * T_OWN + c * 128 + 128],
                          p.kvstb[par * 64:(par + 1) * 64, (h // 2) * 64:(h // 2) * 64 + 64],
                          start=False, stop=True)
                te.matmul(p.ps6[:, h:h + 1],
                          p.smt[:, sslot * 128:(sslot + 1) * 128],
                          p.onesb[:, 0:1], start=True, stop=False)
                te.matmul(p.ps6[:, h:h + 1],
                          p.qA[par * 64:(par + 1) * 64, (h // 2) * T_OWN + c * 128:(h // 2) * T_OWN + c * 128 + 128],
                          p.ksstb[par * 64:(par + 1) * 64, h // 2:h // 2 + 1],
                          start=False, stop=True).then_inc(mm.h)
                mm.n += 1
            blk.tensor(t_n)
            smt_free[(c, h)] = mm.n
        ph1_mm = mm.n

        def v_dd(ve, c=c, ph1_mm=ph1_mm):
            ve.wait_ge(mm.h, ph1_mm)
            ev.inc(ve.tensor_scalar(p.rden[:, 0:16], p.ps6[:, 0:16], EPS_A, None, ALU.add))
            ve.wait_ge(ev.h, ev.n)
            ev.inc(ve.reciprocal(p.rden[:, 0:16], p.rden[:, 0:16]))
            ve.wait_ge(ev.h, ev.n)
            for h in range(H):
                numbank = p.ps2 if h < 8 else p.ps3
                ncol = (h % 8) * 64
                ins = ve.tensor_scalar(
                    p.atok[:, (c % 4) * D + h * HD:(c % 4) * D + h * HD + 64],
                    numbank[:, ncol:ncol + 64],
                    p.rden[:, h:h + 1], None, ALU.mult)
            ev.inc(ins)
        vec_(p, v_dd)
        vdiv_ev[c] = ev.n
        atok_rdy = ev.n

        def t_u(te, c=c, ph1_mm=ph1_mm):
            g = vu_ev.get(c - 1)
            if g is not None:
                te.wait_ge(ev.h, g)
            if c % 4 == 0 and c > 0 and getattr(p, "_a2_ln_var", None) is not None:
                te.wait_ge(ev.h, p._a2_ln_var)
            for h in range(H):
                par = h % 2
                te.matmul(p.ps4[par * 64:(par + 1) * 64, (h // 2) * 64:(h // 2) * 64 + 64],
                          p.ktokA[:, c * D + h * HD: c * D + h * HD + 64],
                          p.vtokA[:, c * D + h * HD: c * D + h * HD + 64],
                          start=True, stop=True, tile_position=(0, par * 64))
                ins = te.matmul(p.ps5[par * 64:(par + 1) * 64, h // 2:h // 2 + 1],
                                p.ktokA[:, c * D + h * HD: c * D + h * HD + 64],
                                p.onesb[:, 0:1],
                                start=True, stop=True, tile_position=(0, par * 64))
            ins.then_inc(mm.h); mm.n += 1
        blk.tensor(t_u)
        u_mm = mm.n

        def v_u(ve, c=c, u_mm=u_mm):
            ve.wait_ge(mm.h, u_mm)
            ve.tensor_add(p.kvst[:, :], p.kvst[:, :], p.ps4[:, 0:512])
            ev.inc(ve.tensor_add(p.ksst[:, 0:8], p.ksst[:, 0:8], p.ps5[:, 0:8]))
            ve.wait_ge(ev.h, ev.n)
            ve.tensor_copy(p.kvstb[:, :], p.kvst[:, :])
            ev.inc(ve.tensor_copy(p.ksstb[:, :], p.ksst[:, :]))
        vec_(p, v_u)
        vu_ev[c] = ev.n

        if c % 4 == 3 and DO_WOLN:
            emit_a2_wo_ln(p, c // 4, atok_rdy, wo_rdy, vu_ev[c])


def emit_a2_wo_ln(p, gidx, atok_rdy, wo_rdy, vu_done):
    import os
    blk = p.block
    ld, wx, mm, ev, pl, ac, cc, st, dp = (p.sems[k] for k in
        ["ld", "wx", "mm", "ev", "pl", "ac", "cc", "st", "dp"])
    xq = p.sems["xr"]

    def s_xr(se, gidx=gidx):
        if gidx > 0:
            se.wait_ge(ev.h, p._a2_x1_done)
            if getattr(p, "_a2_x1_store", None) is not None:
                se.wait_ge(st.h, p._a2_x1_store)
        se.dma_start(out=sb3(p.xres[:, :], 512), in_=fm(p.x_fm, gidx * 512, 512)).then_inc(xq.h, 16)
        xq.n += 16
    blk.sync(s_xr)
    xr_rdy = xq.n

    tr_ev0 = ev.n
    tr_mm0 = mm.n

    def t_tr(te, atok_rdy=atok_rdy, gidx=gidx):
        te.wait_ge(ev.h, atok_rdy)
        if gidx > 0:
            te.wait_ge(ev.h, p._a2_afm_free)
        for idx in range(32):
            j, oc = divmod(idx, OC)
            bank = p.ps7 if idx % 2 == 0 else p.ps6
            if idx >= 2:
                te.wait_ge(ev.h, tr_ev0 + idx - 1)
            te.matmul(bank[:, 0:128],
                      p.atok[:, j * D + oc * 128: j * D + oc * 128 + 128],
                      p.mident[:, :], start=True, stop=True).then_inc(mm.h)
            mm.n += 1
    blk.tensor(t_tr)

    def v_tr(ve, tr_mm0=tr_mm0):
        for idx in range(32):
            j, oc = divmod(idx, OC)
            bank = p.ps7 if idx % 2 == 0 else p.ps6
            ve.wait_ge(mm.h, tr_mm0 + idx + 1)
            ev.inc(ve.tensor_copy(
                p.afm[:, oc * 512 + j * 128: oc * 512 + j * 128 + 128],
                bank[:, 0:128]))
    vec_(p, v_tr)
    afm_rdy = ev.n

    wo_ev = {}
    for oco in range(OC):
        bank = p.ps2 if oco % 2 == 0 else p.ps3

        def t_w(te, oco=oco, bank=bank, afm_rdy=afm_rdy):
            te.wait_ge(ev.h, afm_rdy)
            if oco == 0:
                te.wait_ge(ld.h, wo_rdy)
            g = wo_ev.get(oco - 2)
            if g is not None:
                te.wait_ge(ev.h, g)
            for ic in range(OC):
                ins = te.matmul(
                    bank[:, 0:512],
                    p.wbuf[:, ic * D + oco * 128: ic * D + oco * 128 + 128],
                    p.afm[:, ic * 512:(ic + 1) * 512],
                    start=(ic == 0), stop=(ic == OC - 1))
            ins.then_inc(mm.h); mm.n += 1
        blk.tensor(t_w)
        w_mm = mm.n

        def v_w(ve, oco=oco, bank=bank, w_mm=w_mm, xr_rdy=xr_rdy):
            ve.wait_ge(mm.h, w_mm)
            ve.wait_ge(p.sems["xr"].h, xr_rdy)
            ev.inc(ve.scalar_tensor_tensor(
                p.xres[:, oco * 512:(oco + 1) * 512],
                bank[:, 0:512], p.bcolo[:, oco:oco + 1],
                p.xres[:, oco * 512:(oco + 1) * 512],
                ALU.add, ALU.add))
        vec_(p, v_w)
        wo_ev[oco] = ev.n
    p._a2_afm_free = ev.n
    xsum_rdy = ev.n

    emit_ln_fm(p, p.xres, xsum_rdy, x1_store_gidx=gidx, vu_done=vu_done)


def emit_ln_fm(p, src, src_rdy, x1_store_gidx=None, gcol=None, bcol=None,
               out_sb=None, out_cb=None, lvl=4, vu_done=None, waits=()):
    blk = p.block
    mm, ev, ac, st, wx = (p.sems[k] for k in ["mm", "ev", "ac", "st", "wx"])

    pl = p.sems["pl"]
    ac0 = pl.n
    mm0 = mm.n

    def p_sq(po, src_rdy=src_rdy, mm0=mm0):
        if src_rdy is not None:
            po.wait_ge(ev.h, src_rdy)
        for hh, vv in waits:
            po.wait_ge(hh, vv)
        for oc in range(OC):
            if oc > 0:
                po.wait_ge(mm.h, mm0 + oc)
            pl.inc(po.tensor_tensor(p.xsq[:, 0:512], src[:, oc * 512:(oc + 1) * 512],
                                    src[:, oc * 512:(oc + 1) * 512], ALU.mult))
    pol_(p, p_sq)

    def t_sums(te, src_rdy=src_rdy, ac0=ac0, vu_done=vu_done):
        if src_rdy is not None:
            te.wait_ge(ev.h, src_rdy)
        for hh, vv in waits:
            te.wait_ge(hh, vv)
        if vu_done is not None:
            te.wait_ge(ev.h, vu_done)
        for oc in range(OC):
            te.matmul(p.ps4[0:1, 0:512], p.onesf[:, 0:1], src[:, oc * 512:(oc + 1) * 512],
                      start=(oc == 0), stop=(oc == OC - 1))
        for oc in range(OC):
            te.wait_ge(p.sems["pl"].h, ac0 + oc + 1)
            te.matmul(p.ps5[0:1, 0:512], p.onesf[:, 0:1], p.xsq[:, 0:512],
                      start=(oc == 0), stop=(oc == OC - 1)).then_inc(mm.h)
            mm.n += 1
    blk.tensor(t_sums)
    sums_mm = mm.n

    R = p.scr
    def v_r(ve, sums_mm=sums_mm):
        ve.wait_ge(mm.h, sums_mm)
        ev.inc(ve.tensor_scalar(R[0:1, 0:512], p.ps4[0:1, 0:512], 1.0 / D, None, ALU.mult))
        ve.wait_ge(ev.h, ev.n)
        ev.inc(ve.tensor_tensor(R[0:1, 512:1024], R[0:1, 0:512], R[0:1, 0:512], ALU.mult))
        ve.wait_ge(ev.h, ev.n)
        ev.inc(ve.scalar_tensor_tensor(
            R[0:1, 1024:1536], p.ps5[0:1, 0:512], 1.0 / D, R[0:1, 512:1024],
            ALU.mult, ALU.subtract))
    vec_(p, v_r)
    var_ev = ev.n
    p._a2_ln_var = var_ev

    def a_r(se, var_ev=var_ev):
        se.wait_ge(ev.h, var_ev)
        ac.inc(se.activation(R[0:1, 512:1024], R[0:1, 1024:1536], AF.Ln, bias=1e-5))
        se.wait_ge(ac.h, ac.n)
        ac.inc(se.activation(R[0:1, 1024:1536], R[0:1, 512:1024], AF.Exp, scale=-0.5))
    scl_(p, a_r)
    rstd_ac = ac.n

    def v_r2(ve, rstd_ac=rstd_ac):
        ve.wait_ge(ac.h, rstd_ac)
        ev.inc(ve.tensor_tensor(R[0:1, 512:1024], R[0:1, 0:512], R[0:1, 1024:1536], ALU.mult))
    vec_(p, v_r2)
    mur_ev = ev.n
    if lvl < 4:
        p._a2_x1_done = mur_ev
        return mur_ev

    def t_bc(te, mur_ev=mur_ev):
        te.wait_ge(ev.h, mur_ev)
        te.matmul(p.ps0[:, 0:512], p.onesrow_f[0:1, :], R[0:1, 1024:1536], start=True, stop=True)
        te.matmul(p.ps1[:, 0:512], p.onesrow_f[0:1, :], R[0:1, 512:1024],
                  start=True, stop=True).then_inc(mm.h)
        mm.n += 1
    blk.tensor(t_bc)
    bc_mm = mm.n

    dst = out_sb if out_sb is not None else src
    def v_a(ve, bc_mm=bc_mm):
        ve.wait_ge(mm.h, bc_mm)
        for oc in range(OC):
            ev.inc(ve.tensor_tensor(p.xsq[:, 0:512], src[:, oc * 512:(oc + 1) * 512],
                                    p.ps0[:, 0:512], ALU.mult))
            ve.wait_ge(ev.h, ev.n)
            if gcol is None:
                ev.inc(ve.tensor_tensor(dst[:, oc * 512:(oc + 1) * 512],
                                        p.xsq[:, 0:512], p.ps1[:, 0:512], ALU.subtract))
            else:
                ev.inc(ve.tensor_tensor(p.xsq[:, 0:512], p.xsq[:, 0:512], p.ps1[:, 0:512], ALU.subtract))
                ve.wait_ge(ev.h, ev.n)
                ev.inc(ve.tensor_scalar(dst[:, oc * 512:(oc + 1) * 512], p.xsq[:, 0:512],
                                        gcol[:, oc:oc + 1], bcol[:, oc:oc + 1],
                                        ALU.mult, ALU.add))
            ve.wait_ge(ev.h, ev.n)
    vec_(p, v_a)
    app_ev = ev.n
    p._a2_x1_done = app_ev

    if x1_store_gidx is not None:
        gidx = x1_store_gidx
        def s_o(se, gidx=gidx, app_ev=app_ev):
            se.wait_ge(ev.h, app_ev)
            se.dma_start(out=fm(p.x1s, gidx * 512, 512), in_=sb3(dst[:, :], 512)).then_inc(st.h, 16)
            st.n += 16
            if p.mode != "A":
                se.dma_start(out=fm(p.Xs, HALO + gidx * 512, 512), in_=sb3(dst[:, :], 512)).then_inc(st.h, 16)
                st.n += 16
        blk.sync(s_o)
        p._a2_x1_store = st.n
    return app_ev



def _run(p, per_core, outs=("out_fm",)):
    p.ctx.close()
    in_names = {a.memorylocations[0].name if hasattr(a, 'memorylocations') else None
                for a in []}
    res = run_bass_kernel_spmd(p.nc, per_core, core_ids=list(range(NCORES)))
    return res


def kernel_stage_a_test(**inputs):
    per_core = prep_inputs(inputs)
    p = build(mode="A")
    emit_stage_a(p)

    st, dp = p.sems["st"], p.sems["dp"]
    st_all = st.n

    import os
    def g_out(gp, st_all=st_all):
        gp.wait_ge(st.h, st_all)
        if os.environ.get("MK_DBG_ST", "0") == "1":
            gp.dma_start(out=p.out_fm.rearrange("a b -> (a b)")[0:128 * 512].rearrange("(a b) -> a b", b=512),
                         in_=p.kvst[:, :]).then_inc(dp.h, 16); dp.n += 16
            gp.dma_start(out=p.out_fm.rearrange("a b -> (a b)")[70000:70000 + 128 * 8].rearrange("(a b) -> a b", b=8),
                         in_=p.ksst[:, :]).then_inc(dp.h, 16); dp.n += 16
        elif os.environ.get("MK_DBG_CC", "0") == "1":
            gp.dma_start(out=p.out_fm.rearrange("a b -> (a b)")[0:1040 * 64].rearrange("(a b) -> a b", b=64),
                         in_=p.cc_kv_in).then_inc(dp.h, 16); dp.n += 16
            gp.dma_start(out=p.out_fm.rearrange("a b -> (a b)")[70000:70000 + 4160 * 64].rearrange("(a b) -> a b", b=64),
                         in_=p.cc_kv_out).then_inc(dp.h, 16); dp.n += 16
        else:
            gp.dma_start(out=p.out_fm, in_=p.x1s).then_inc(dp.h, 16); dp.n += 16
        gp.wait_ge(dp.h, dp.n)
    p.block.gpsimd(g_out)

    keys = ["x_fm", "wqT", "wkT", "wvT", "woT", "bq128", "bk128", "bo128",
            "bkrow", "bvrow", "sel", "invrow"]
    in_maps = [{k: pc[k] for k in keys} for pc in per_core]
    res = _run(p, in_maps)
    outs = [res.results[c]["out_fm"] for c in range(NCORES)]
    return np.concatenate([o.T for o in outs], axis=0)



def pol_(p, f):
    n = p.sems["pl"].n
    h = p.sems["pl"].h

    def g(po):
        po.wait_ge(h, n)
        f(po)
    p.block.gpsimd(g)


def alloc_stage_b(p):
    nc = p.nc
    sb = lambda n, s, dt=F32: p.ctx.enter_context(nc.sbuf_tensor(n, s, dt))
    p.WSLOT = sb("WSLOT", [128, 8 * 3072], BF16)
    p.kbuf = sb("kbuf", [128, 8 * 1024], BF16)
    p.vbuf = sb("vbuf", [128, 8 * 1024], BF16)
    p.qbuf = sb("qbuf", [128, 8 * 512], BF16)
    p.xnb = sb("xnb", [128, 8 * 2048], BF16)
    p.w1slot = sb("w1slot", [128, 2 * 2048], BF16)
    p.pbuf = sb("pbuf", [128, 2 * 384], BF16)
    p.ptb = sb("ptb", [128, 2 * 384], BF16)
    p.mrow = sb("mrow", [128, 8])
    p.tabsb = sb("tabsb", [128, 2 * 512], BF16)
    p.msk = sb("msk", [128, 4 * 384])
    p.mb2 = sb("mb2", [128, 2048], BF16)
    p.qspb = sb("qspb", [2, 256])
    p.selhb = sb("selhb", [128, 4])
    p.blockpat = sb("blockpat", [128, 2])
    p.ubuf = sb("ubuf", [128, 512], BF16)
    p.rbuf = sb("rbuf", [128, 512], BF16)
    p.lncol = sb("lncol", [128, 3 * OC])
    p.b1col = sb("b1col", [128, 44])
    p.afm2 = p.xnb[:, 0:8 * 512]
    p.nrm0 = p.xsq[64:66, 0:512]
    p.nrm1 = p.xsq[32:34, 0:512]
    p.nrm2 = p.xsq[0:2, 0:512]

    ld = p.sems["ld"]

    lv = p.sems["lv"]

    def g0(gp):
        gp.wait_ge(p.sems["fn"].h, p._fence_ld)
        gp.dma_start(out=p.msk[:, 0:384], in_=p.maskstd).then_inc(lv.h, 16); lv.n += 16
        gp.dma_start(out=sb3(p.msk[:, 384:4 * 384], 384),
                     in_=p.maskfirst.rearrange("j p t -> p j t")).then_inc(lv.h, 16); lv.n += 16
        gp.dma_start(out=p.selhb[:, :], in_=p.selh).then_inc(lv.h, 16); lv.n += 16
    p.block.gpsimd(g0)
    p._b_const_ld = lv.n

    def v0(ve):
        ev = p.sems["ev"]
        ve.wait_ge(p.sems["fn"].h, p._fence_ld)
        ev.inc(ve.memset(p.blockpat[:, 0:2], 0.0))
        ve.wait_ge(ev.h, ev.n)
        ve.memset(p.blockpat[0:64, 0:1], 1.0)
        ev.inc(ve.memset(p.blockpat[64:128, 1:2], 1.0))
    vec_(p, v0)
    p._b_const_ev = p.sems["ev"].n


ST_B = [(0, 512), (512, 512), (1024, 512), (1536, 512), (2048, 384)]
QT_GROUPS = [[0, 1], [2, 3, 4, 5], [6, 7, 8, 9], [10, 11, 12, 13], [14, 15, 16]]


def emit_cc_x(p, i, ready_st):
    blk = p.block
    ev, cc, st, dp, wx = (p.sems[k] for k in ["ev", "cc", "st", "dp", "wx"])
    if not hasattr(p, "_ccx_dp"):
        p._ccx_dp = {}
        p._ccx_stt = {}

    ev0 = p.sems["ev"].n
    k = ev0 + 1
    for r in range(4):
        for half in range(2):
            k += 1
            p._ccx_stt[(i, r, half)] = k
    final_ev = k + 1

    def g(gp, i=i, ready_st=ready_st):
        gp.wait_ge(st.h, ready_st)
        gp.dma_start(out=p.cc_x_in, in_=p.Xs[:, T_EXT - HALO:T_EXT]).then_inc(dp.h, 16)
        dp.n += 16
        gp.wait_ge(dp.h, dp.n)
        cc.inc(gp.collective_compute(
            "AllGather", ALU.bypass, replica_groups=RG,
            ins=[p.cc_x_in.opt()], outs=[p.cc_x_out.opt()]))
        gp.wait_ge(cc.h, cc.n)
        for r in range(4):
            for half in range(2):
                if not (r == 0 and half == 0):
                    gp.wait_ge(ev.h, p._ccx_stt[(i, r, half)] - 1)
                gp.dma_start(
                    out=sb3(p.scr[:, 0:4 * 384], 384),
                    in_=p.cc_x_out[r * 1024 + half * 512: r * 1024 + half * 512 + 512, :]
                        .rearrange("(c p) t -> p c t", p=128),
                ).then_inc(dp.h, 16); dp.n += 16
                p._ccx_dp[(i, r, half)] = dp.n
                gp.wait_ge(dp.h, dp.n)
        gp.wait_ge(ev.h, final_ev)
        gp.dma_start(out=fm(p.Xs, 0, HALO), in_=sb3(p.xres[:, 0:8 * 384], 384)).then_inc(dp.h, 16)
        dp.n += 16
        gp.wait_ge(dp.h, dp.n)
    blk.gpsimd(g)

    def v(ve, i=i, ready_st=ready_st):
        ve.wait_ge(st.h, ready_st)
        ve.wait_ge(p.sems["lv"].h, p._b_const_ld)
        ev.inc(ve.memset(p.xres[:, 0:8 * 384], 0.0))
        for r in range(4):
            for half in range(2):
                ve.wait_ge(dp.h, p._ccx_dp[(i, r, half)])
                ve.wait_ge(ev.h, ev.n)
                ev.inc(ve.scalar_tensor_tensor(
                    p.xres[:, half * 4 * 384:(half + 1) * 4 * 384],
                    p.scr[:, 0:4 * 384], p.selhb[:, r:r + 1],
                    p.xres[:, half * 4 * 384:(half + 1) * 4 * 384],
                    ALU.mult, ALU.add))
                assert p._ccx_stt[(i, r, half)] == ev.n
        ve.wait_ge(ev.h, ev.n)
        ev.inc(ve.tensor_copy(p.xres[0:1, 0:1], p.xres[0:1, 0:1]))
        assert final_ev == ev.n
    vec_(p, v)
    return dp.n


def emit_p1(p, i, halo_dp):
    p._halo_dp = halo_dp
    blk = p.block
    ld, wx, mm, ev, pl, ac, cc, st, dp, xq = (p.sems[k] for k in
        ["ld", "wx", "mm", "ev", "pl", "ac", "cc", "st", "dp", "xr"])

    def g_w(gp, i=i):
        if i == 0:
            gp.wait_ge(p.sems["fn"].h, p._fence_ld)
        gp.wait_ge(mm.h, p._wslot_free)
        gp.dma_start(out=sb3(p.WSLOT[:, :], 3 * D),
                     in_=p.wqkv[i].rearrange("(c p) o -> p c o", p=128)).then_inc(ld.h, 16)
        ld.n += 16
        gp.wait_ge(mm.h, p._wbuf_free)
        gp.dma_start(out=sb3(p.wbuf[:, :], D),
                     in_=p.wo_b[i].rearrange("(c p) o -> p c o", p=128)).then_inc(ld.h, 16)
        ld.n += 16
        gp.dma_start(out=p.qspb[0:2, 0:128], in_=p.qsp[i]).then_inc(ld.h, 16); ld.n += 16
        gp.dma_start(out=p.qspb[0:2, 128:256], in_=p.ksp[i]).then_inc(ld.h, 16); ld.n += 16
    blk.gpsimd(g_w)
    w_rdy = ld.n

    M = {}
    M["afm_free"] = {0: None, 1: None}
    M["x2st"] = {}
    qdone = {}

    for sti, (t0, wid) in enumerate(ST_B):
        nt = wid // 128

        def g_x(gp, sti=sti, t0=t0, wid=wid, i=i, halo_dp=halo_dp):
            gp.wait_ge(mm.h, M.get(("xbfree", sti), 0))
            if sti == 0:
                gp.wait_ge(dp.h, halo_dp)
                if i == 0 and p.mode != "A":
                    gp.wait_ge(st.h, p._a2_x1_store)
            gp.dma_start(out=sb3(p.xb[:, 0:4 * 512], 512)[:, :, 0:wid],
                         in_=fm(p.Xs, t0, wid)[:, 0:4, :]).then_inc(wx.h, 16)
            wx.n += 16
            if t0 > 0:
                gp.dma_start(out=sb3(p.xb[:, 4 * 512:8 * 512], 512)[:, :, 0:wid],
                             in_=fm(p.Xs, t0 - 1, wid)[:, 4:8, :]).then_inc(wx.h, 16)
                wx.n += 16
            else:
                gp.dma_start(out=sb3(p.xb[:, 4 * 512:8 * 512], 512)[:, :, 1:wid],
                             in_=fm(p.Xs, 0, wid - 1)[:, 4:8, :]).then_inc(wx.h, 16)
                wx.n += 16
                gp.dma_start(out=sb3(p.xb[:, 4 * 512:8 * 512], 512)[:, :, 0:1],
                             in_=fm(p.Xs, 0, 1)[:, 4:8, :],
                             allow_slow_non_contiguous=True).then_inc(wx.h, 16)
                wx.n += 16
        blk.gpsimd(g_x)
        x_rdy = wx.n

        for qk in range(2):
            tabs_free = M.get(("tabfree",))

            def g_tab(gp, qk=qk, t0=t0, wid=wid, tabs_free=tabs_free, i=i):
                if tabs_free is not None:
                    gp.wait_ge(ev.h, tabs_free)
                elif i == 0:
                    gp.wait_ge(p.sems["fn"].h, p._fence_ld)
                srcs = [p.tab_cq, p.tab_sq] if qk == 0 else [p.tab_ck, p.tab_sk]
                fnm = p.sems["tb"]
                for tbl, src_t in enumerate(srcs):
                    gp.dma_start(out=p.tabsb[:, tbl * 512: tbl * 512 + wid],
                                 in_=src_t[:, t0:t0 + wid]).then_inc(fnm.h, 16)
                    fnm.n += 16
            blk.gpsimd(g_tab)
            tab_rdy = p.sems["tb"].n
            for oc in range(OC):
                pp = (qk * OC + oc) % 2
                bank = p.ps0 if pp == 0 else p.ps1
                fbank = p.ps2 if pp == 0 else p.ps3

                def t_p(te, qk=qk, oc=oc, sti=sti, wid=wid, bank=bank, x_rdy=x_rdy, pp=pp,
                        evbar=(p.sems["ev"].n if (qk == 0 and oc == 0) else None)):
                    te.wait_ge(wx.h, x_rdy)
                    if evbar is not None:
                        te.wait_ge(ev.h, evbar)
                    if sti == 0 and qk == 0 and oc == 0:
                        te.wait_ge(ld.h, w_rdy)
                        te.wait_ge(ev.h, p._b_const_ev)
                        te.wait_ge(p.sems["lv"].h, p._b_const_ld)
                        if getattr(p, "_p1_prev_done", None) is not None:
                            te.wait_ge(ev.h, p._p1_prev_done)
                    g = M.get(("projfree", pp))
                    if g is not None:
                        te.wait_ge(ac.h, g)
                    for dc in range(OC):
                        ins = te.matmul(
                            bank[:, 0:wid],
                            p.WSLOT[:, dc * 3 * D + qk * D + oc * 128: dc * 3 * D + qk * D + oc * 128 + 128],
                            p.xb[:, dc * 512: dc * 512 + wid],
                            start=(dc == 0), stop=(dc == OC - 1))
                    ins.then_inc(mm.h); mm.n += 1
                blk.tensor(t_p)
                mmv = mm.n

                def a_e(se, bank=bank, wid=wid, mmv=mmv):
                    se.wait_ge(mm.h, mmv)
                    g = M.get(("scrfree",))
                    if g is not None:
                        se.wait_ge(ev.h, g)
                    ac.inc(se.copy(p.scr[:, 0:wid], bank[:, 0:wid]))
                scl_(p, a_e)
                M[("projfree", pp)] = ac.n
                acv = ac.n

                def p_sq1(po, acv=acv, wid=wid):
                    po.wait_ge(ac.h, acv)
                    g = M.get(("sqfree",))
                    if g is not None:
                        po.wait_ge(mm.h, g)
                    pl.inc(po.tensor_tensor(p.scr[:, 512:512 + wid], p.scr[:, 0:wid],
                                            p.scr[:, 0:wid], ALU.mult))
                pol_(p, p_sq1)
                plv = pl.n

                def t_f(te, plv=plv, wid=wid, fbank=fbank, pp=pp):
                    te.wait_ge(pl.h, plv)
                    g = M.get(("facfree", pp))
                    if g is not None:
                        te.wait_ge(ev.h, g)
                    g2 = M.get(("nrmfree",))
                    if g2 is not None:
                        te.wait_ge(ev.h, g2)
                    te.matmul(p.ps6[0:2, 0:wid], p.blockpat[:, 0:2],
                              p.scr[:, 512:512 + wid],
                              start=True, stop=True).then_inc(mm.h)
                    mm.n += 1
                blk.tensor(t_f)
                nmv = mm.n
                M[("sqfree",)] = nmv

                def v_f(ve, nmv=nmv, wid=wid):
                    ve.wait_ge(mm.h, nmv)
                    ev.inc(ve.tensor_scalar(p.nrm0[0:2, 0:wid], p.ps6[0:2, 0:wid],
                                            1e-24, None, ALU.max))
                vec_(p, v_f)
                evf = ev.n
                M[("nrmfree",)] = ev.n

                def a_f(se, evf=evf, wid=wid):
                    se.wait_ge(ev.h, evf)
                    ac.inc(se.activation(p.nrm1[0:2, 0:wid],
                                         p.nrm0[0:2, 0:wid], AF.Ln))
                    se.wait_ge(ac.h, ac.n)
                    ac.inc(se.activation(p.nrm2[0:2, 0:wid],
                                         p.nrm1[0:2, 0:wid], AF.Exp, scale=-0.5))
                scl_(p, a_f)
                acf = ac.n

                def t_f2(te, qk=qk, acf=acf, wid=wid, fbank=fbank):
                    te.wait_ge(ac.h, acf)
                    te.matmul(fbank[:, 0:wid], p.qspb[0:2, qk * 128:qk * 128 + 128],
                              p.nrm2[0:2, 0:wid],
                              start=True, stop=True).then_inc(mm.h)
                    mm.n += 1
                blk.tensor(t_f2)
                fmv = mm.n

                def v_rot(ve, qk=qk, oc=oc, t0=t0, wid=wid, fmv=fmv, fbank=fbank, tab_rdy=tab_rdy):
                    ve.wait_ge(mm.h, fmv)
                    ve.wait_ge(p.sems["tb"].h, tab_rdy)
                    ev.inc(ve.tensor_tensor(p.ubuf[:, 0:wid], p.scr[:, 0:wid],
                                            fbank[:, 0:wid], ALU.mult))
                    ve.wait_ge(ev.h, ev.n)
                    ve.tensor_copy(p.rbuf[0:32, 0:wid], p.ubuf[32:64, 0:wid])
                    ve.tensor_copy(p.rbuf[32:64, 0:wid], p.ubuf[0:32, 0:wid])
                    ve.tensor_copy(p.rbuf[64:96, 0:wid], p.ubuf[96:128, 0:wid])
                    ev.inc(ve.tensor_copy(p.rbuf[96:128, 0:wid], p.ubuf[64:96, 0:wid]))
                    ve.wait_ge(ev.h, ev.n)
                    ctab = p.tabsb[:, 0:wid]
                    stab = p.tabsb[:, 512:512 + wid]
                    ev.inc(ve.tensor_tensor(p.ubuf[:, 0:wid], p.ubuf[:, 0:wid], ctab, ALU.mult))
                    ev.inc(ve.tensor_tensor(p.rbuf[:, 0:wid], p.rbuf[:, 0:wid], stab, ALU.mult))
                    ve.wait_ge(ev.h, ev.n)
                    if qk == 0:
                        dst = p.qbuf[:, oc * 512: oc * 512 + wid]
                    else:
                        ring0 = (t0 // 128) % 8
                        dst = p.kbuf[:, oc * 1024 + ring0 * 128: oc * 1024 + ring0 * 128 + wid]
                    ev.inc(ve.tensor_add(dst, p.ubuf[:, 0:wid], p.rbuf[:, 0:wid]))
                vec_(p, v_rot)
                M[("facfree", pp)] = ev.n
                M[("scrfree",)] = ev.n
                M[("tabfree",)] = ev.n

        for j in range(nt):
            for half in range(2):
                pp = (j * 2 + half) % 2
                bank = p.ps0 if pp == 0 else p.ps1

                def t_v(te, j=j, half=half, bank=bank, x_rdy=x_rdy, pp=pp):
                    te.wait_ge(wx.h, x_rdy)
                    g = M.get(("vfree", pp))
                    if g is not None:
                        te.wait_ge(ev.h, g)
                    g2 = M.get(("projfree", pp))
                    if g2 is not None:
                        te.wait_ge(ac.h, g2)
                    for dc in range(OC):
                        ins = te.matmul(
                            bank[:, 0:512],
                            p.xb[:, dc * 512 + j * 128: dc * 512 + j * 128 + 128],
                            p.WSLOT[:, dc * 3 * D + 2 * D + half * 512: dc * 3 * D + 2 * D + half * 512 + 512],
                            start=(dc == 0), stop=(dc == OC - 1))
                    ins.then_inc(mm.h); mm.n += 1
                blk.tensor(t_v)
                mmv = mm.n

                def v_v(ve, j=j, half=half, t0=t0, mmv=mmv, bank=bank):
                    ve.wait_ge(mm.h, mmv)
                    ring = ((t0 // 128) + j) % 8
                    ev.inc(ve.tensor_copy(
                        p.vbuf[:, ring * 1024 + half * 512: ring * 1024 + half * 512 + 512],
                        bank[:, 0:512]))
                vec_(p, v_v)
                M[("vfree", pp)] = ev.n
        proj_ev = ev.n
        M[("xbfree", sti + 1)] = mm.n

        ready = [j for j in range(17) if t0 <= (j + 2) * 128 < t0 + wid]
        pend = []
        for j in ready:
            qcol = (j + 2) * 128 - t0
            group = next(gi for gi, g_ in enumerate(QT_GROUPS) if j in g_)
            gslot = group % 2
            afm_t = p.afm if gslot == 0 else p.afm2
            jj = j - QT_GROUPS[group][0]
            for h in range(H):
                par = h % 2
                oc = h // 2
                slot = h % 2
                sbank = p.ps2 if h % 2 == 0 else p.ps3
                tbank = p.ps4 if h % 2 == 0 else p.ps5
                pvbank = p.ps6 if h % 2 == 0 else p.ps7

                def t_s(te, j=j, h=h, par=par, oc=oc, qcol=qcol, sbank=sbank,
                        proj_ev=proj_ev, first=(j == ready[0] and h == 0), group=group, gslot=gslot):
                    if first:
                        te.wait_ge(ev.h, proj_ev)
                        g0_ = M["afm_free"][gslot]
                        if g0_ is not None:
                            te.wait_ge(ev.h, g0_)
                    g = M.get(("sfree", id(sbank)))
                    if g is not None:
                        te.wait_ge(ev.h, g)
                    kt = 0
                    while kt < 3:
                        r0 = (j + kt) % 8
                        run = min(3 - kt, 8 - r0)
                        ins = te.matmul(
                            sbank[:, kt * 128: (kt + run) * 128],
                            p.qbuf[par * 64:(par + 1) * 64, oc * 512 + qcol: oc * 512 + qcol + 128],
                            p.kbuf[par * 64:(par + 1) * 64, oc * 1024 + r0 * 128: oc * 1024 + (r0 + run) * 128],
                            start=True, stop=True)
                        kt += run
                    ins.then_inc(mm.h); mm.n += 1
                blk.tensor(t_s)
                smm = mm.n

                def v_cpm(ve, j=j, smm=smm, sbank=sbank, slot=slot):
                    ve.wait_ge(mm.h, smm)
                    g = M.get(("scrAfree", slot))
                    if g is not None:
                        ve.wait_ge(ac.h, g)
                    g2 = M.get(("mrowfree", slot))
                    if g2 is not None:
                        ve.wait_ge(ac.h, g2)
                    mk = p.msk[:, (j + 1) * 384:(j + 2) * 384] if j < 3 else p.msk[:, 0:384]
                    ev.inc(ve.tensor_add(p.scr[:, slot * 512: slot * 512 + 384],
                                         sbank[:, 0:384], mk))
                    ve.wait_ge(ev.h, ev.n)
                    ev.inc(ve.reduce_max(p.mrow[:, slot * 4: slot * 4 + 1],
                                         p.scr[:, slot * 512: slot * 512 + 384],
                                         axis=AX, negate=True))
                vec_(p, v_cpm)
                M[("sfree", id(sbank))] = ev.n
                evm = ev.n

                def a_ex(se, slot=slot, evm=evm):
                    se.wait_ge(ev.h, evm)
                    g = M.get(("pbfree", slot))
                    if g is not None:
                        se.wait_ge(ev.h, g)
                    ac.inc(se.activation(p.pbuf[:, slot * 384:(slot + 1) * 384],
                                         p.scr[:, slot * 512: slot * 512 + 384],
                                         AF.Exp, bias=p.mrow[:, slot * 4: slot * 4 + 1],
                                         accum_out=p.mrow[:, slot * 4 + 1: slot * 4 + 2]))
                scl_(p, a_ex)
                M[("scrAfree", slot)] = ac.n
                M[("mrowfree", slot)] = ac.n
                acex = ac.n

                def v_sc(ve, slot=slot, acex=acex):
                    ve.wait_ge(ac.h, acex)
                    ev.inc(ve.reciprocal(p.mrow[:, slot * 4 + 2: slot * 4 + 3],
                                         p.mrow[:, slot * 4 + 1: slot * 4 + 2]))
                    ve.wait_ge(ev.h, ev.n)
                    ev.inc(ve.tensor_scalar(p.pbuf[:, slot * 384:(slot + 1) * 384],
                                            p.pbuf[:, slot * 384:(slot + 1) * 384],
                                            p.mrow[:, slot * 4 + 2: slot * 4 + 3],
                                            None, ALU.mult))
                vec_(p, v_sc)
                evp = ev.n

                if pend:
                    pend.pop(0)()

                def emit_trpv(j=j, h=h, slot=slot, evp=evp, tbank=tbank,
                              pvbank=pvbank, par=par, oc=oc, jj=jj, afm_t=afm_t):
                    def t_tr(te, slot=slot, evp=evp, tbank=tbank):
                        te.wait_ge(ev.h, evp)
                        g = M.get(("trfree", id(tbank)))
                        if g is not None:
                            te.wait_ge(ev.h, g)
                        for kt in range(3):
                            ins = te.matmul(
                                tbank[:, kt * 128: kt * 128 + 128],
                                p.pbuf[:, slot * 384 + kt * 128: slot * 384 + kt * 128 + 128],
                                p.mident[:, :], start=True, stop=True)
                        ins.then_inc(mm.h); mm.n += 1
                    blk.tensor(t_tr)
                    trmm = mm.n

                    def v_tr(ve, slot=slot, trmm=trmm, tbank=tbank):
                        ve.wait_ge(mm.h, trmm)
                        g = M.get(("ptfree", slot))
                        if g is not None:
                            ve.wait_ge(mm.h, g)
                        ev.inc(ve.tensor_copy(p.ptb[:, slot * 384:(slot + 1) * 384],
                                              tbank[:, 0:384]))
                    vec_(p, v_tr)
                    M[("trfree", id(tbank))] = ev.n
                    M[("pbfree", slot)] = ev.n
                    evt = ev.n

                    def t_pv(te, j=j, h=h, slot=slot, evt=evt, pvbank=pvbank):
                        te.wait_ge(ev.h, evt)
                        g = M.get(("pvfree", id(pvbank)))
                        if g is not None:
                            te.wait_ge(ev.h, g)
                        for kt in range(3):
                            ring = (j + kt) % 8
                            ins = te.matmul(
                                pvbank[0:64, 0:128],
                                p.vbuf[:, ring * 1024 + h * HD: ring * 1024 + h * HD + 64],
                                p.ptb[:, slot * 384 + kt * 128: slot * 384 + kt * 128 + 128],
                                start=(kt == 0), stop=(kt == 2))
                        ins.then_inc(mm.h); mm.n += 1
                    blk.tensor(t_pv)
                    M[("ptfree", slot)] = mm.n
                    pvmm = mm.n

                    def v_pv(ve, h=h, par=par, oc=oc, jj=jj, pvmm=pvmm, pvbank=pvbank, afm_t=afm_t):
                        ve.wait_ge(mm.h, pvmm)
                        ev.inc(ve.tensor_copy(
                            afm_t[par * 64:(par + 1) * 64, oc * 512 + jj * 128: oc * 512 + jj * 128 + 128],
                            pvbank[0:64, 0:128]))
                    vec_(p, v_pv)
                    M[("pvfree", id(pvbank))] = ev.n
                pend.append(emit_trpv)

            if j == QT_GROUPS[group][-1]:
                while pend:
                    pend.pop(0)()
                emit_p1_wo(p, i, group, M)
        qdone[sti] = ev.n

    p._p1_prev_done = p.sems["ev"].n
    p._p1_last_mm = mm.n
    return M


def emit_p1_wo(p, i, group, M):
    blk = p.block
    ld, wx, mm, ev, pl, ac, st, xq, dp = (p.sems[k] for k in
        ["ld", "wx", "mm", "ev", "pl", "ac", "st", "xr", "dp"])
    js = QT_GROUPS[group]
    glen = len(js) * 128
    e0 = 256 + js[0] * 128
    gslot = group % 2
    afm_t = p.afm if gslot == 0 else p.afm2
    afm_rdy = p.sems["ev"].n

    def s_x(se, group=group, e0=e0, glen=glen):
        g = M.get(("xresfree",))
        if g is not None:
            se.wait_ge(ev.h, g)
        g2 = M["x2st"].get(group - 1)
        if g2 is not None:
            se.wait_ge(st.h, g2)
        if group == 0:
            se.wait_ge(dp.h, p._halo_dp)
        se.dma_start(out=sb3(p.xres[:, 0:8 * 512], 512)[:, :, 0:glen],
                     in_=fm(p.Xs, e0, glen)).then_inc(xq.h, 16)
        xq.n += 16
    blk.sync(s_x)
    xr_rdy = xq.n

    woev = {}
    for oco in range(OC):
        bank = p.ps0 if oco % 2 == 0 else p.ps1

        def t_w(te, oco=oco, bank=bank, glen=glen, afm_rdy=afm_rdy, group=group, afm_t=afm_t):
            te.wait_ge(ev.h, afm_rdy)
            g = woev.get(oco - 2)
            if g is not None:
                te.wait_ge(ev.h, g)
            g2 = M.get(("wo_projpend", oco % 2))
            if g2 is not None:
                te.wait_ge(ac.h, g2)
            for ic in range(OC):
                ins = te.matmul(
                    bank[:, 0:glen],
                    p.wbuf[:, ic * D + oco * 128: ic * D + oco * 128 + 128],
                    afm_t[:, ic * 512: ic * 512 + glen],
                    start=(ic == 0), stop=(ic == OC - 1))
            ins.then_inc(mm.h); mm.n += 1
        blk.tensor(t_w)
        wmm = mm.n

        def v_w(ve, oco=oco, bank=bank, wmm=wmm, xr_rdy=xr_rdy, glen=glen):
            ve.wait_ge(mm.h, wmm)
            ve.wait_ge(xq.h, xr_rdy)
            ev.inc(ve.tensor_add(
                p.xres[:, oco * 512: oco * 512 + glen],
                bank[:, 0:glen],
                p.xres[:, oco * 512: oco * 512 + glen]))
        vec_(p, v_w)
        woev[oco] = ev.n
    M["afm_free"][gslot] = ev.n
    M[("xresfree",)] = ev.n
    x2_ev = ev.n

    def s_o(se, group=group, e0=e0, glen=glen, x2_ev=x2_ev):
        se.wait_ge(ev.h, x2_ev)
        se.dma_start(out=fm(p.X2s, e0, glen),
                     in_=sb3(p.xres[:, 0:8 * 512], 512)[:, :, 0:glen]).then_inc(st.h, 16)
        st.n += 16
    blk.sync(s_o)
    M["x2st"][group] = st.n


def emit_p2(p, i):
    blk = p.block
    ld, wx, mm, ev, pl, ac, st, dp, xq = (p.sems[k] for k in
        ["ld", "wx", "mm", "ev", "pl", "ac", "st", "dp", "xr"])

    lv = p.sems["lv"]

    def g_lnv(gp, i=i):
        gp.wait_ge(st.h, p._p1_x2st_all)
        gp.dma_start(out=p.lncol[:, 0:OC], in_=p.ln_g[i]).then_inc(lv.h, 16); lv.n += 16
        gp.dma_start(out=p.lncol[:, OC:2 * OC], in_=p.ln_b[i]).then_inc(lv.h, 16); lv.n += 16
        gp.dma_start(out=p.lncol[:, 2 * OC:3 * OC], in_=p.b2c[i]).then_inc(lv.h, 16); lv.n += 16
        gp.dma_start(out=p.b1col[:, :], in_=p.b1c[i]).then_inc(lv.h, 16); lv.n += 16
    blk.gpsimd(g_lnv)
    lnv_rdy = lv.n

    last_store = None
    for tt in range(4):
        e0 = HALO + tt * 512

        def s_x2(se, tt=tt, e0=e0, i=i):
            g = getattr(p, "_p2_xres_free", None)
            if g is not None:
                se.wait_ge(ev.h, g)
            se.wait_ge(st.h, p._p1_x2st_all)
            se.dma_start(out=sb3(p.xres[:, 0:8 * 512], 512)[:, 0:4, :],
                         in_=fm(p.X2s, e0, 512)[:, 0:4, :]).then_inc(xq.h, 16)
            xq.n += 16
            se.dma_start(out=sb3(p.xres[:, 0:8 * 512], 512)[:, 4:8, :],
                         in_=fm(p.X2s, e0 - 1, 512)[:, 4:8, :]).then_inc(xq.h, 16)
            xq.n += 16
        blk.sync(s_x2)
        x2_rdy = xq.n

        emit_ln_fm(p, p.xres, None, gcol=p.lncol[:, 0:OC], bcol=p.lncol[:, OC:2 * OC],
                   out_sb=p.xnb[:, tt * 4096:(tt + 1) * 4096],
                   waits=[(xq.h, x2_rdy), (lv.h, lnv_rdy)])
        p._p2_xres_free = p._a2_x1_done

    xn_rdy = p._a2_x1_done

    w2s = p.sems["w2"]

    def g_w2(gp, i=i):
        gp.wait_ge(mm.h, p._p1_last_mm)
        gp.dma_start(out=sb3(p.WSLOT[:, 0:22 * 1024], 1024),
                     in_=p.w2[i].rearrange("(j p) o -> p j o", p=128)).then_inc(w2s.h, 16)
        w2s.n += 16
    blk.gpsimd(g_w2)
    w2_rdy = w2s.n

    ghst = {}
    for jp in range(22):
        wslot2 = (jp % 2) * 2048

        def g_w1(gp, jp=jp, i=i, wslot2=wslot2):
            g = getattr(p, "_p2_w1_free", {}).get(jp - 2)
            if g is not None:
                gp.wait_ge(mm.h, g)
            if jp < 2:
                gp.wait_ge(mm.h, p._p1_last_mm)
            g2 = M2get(p, ("w1cons", jp - 1 if jp >= 1 else 21))
            if g2 is not None:
                gp.wait_ge(mm.h, g2)
            gp.dma_start(out=sb3(p.w1slot[:, wslot2:wslot2 + 1024], 128),
                         in_=p.w1[i].rearrange("(c p) o -> p c o", p=128)[:, :, jp * 128:(jp + 1) * 128]
                         ).then_inc(ld.h, 16)
            ld.n += 16
            gp.dma_start(out=sb3(p.w1slot[:, wslot2 + 1024:wslot2 + 2048], 128),
                         in_=p.w1[i].rearrange("(c p) o -> p c o", p=128)[:, :, (jp + 22) * 128:(jp + 23) * 128]
                         ).then_inc(ld.h, 16)
            ld.n += 16
        blk.gpsimd(g_w1)
        w1_rdy = ld.n

        mb_ev = {}
        for tt in range(4):
            pa = p.ps0 if tt % 2 == 0 else p.ps2
            pg = p.ps1 if tt % 2 == 0 else p.ps3

            def t_h(te, jp=jp, tt=tt, pa=pa, pg=pg, w1_rdy=w1_rdy, xn_rdy=xn_rdy, wslot2=wslot2):
                if tt == 0:
                    te.wait_ge(ld.h, w1_rdy)
                    if jp == 0:
                        te.wait_ge(ev.h, xn_rdy)
                g = M2get(p, ("hfree", tt % 2))
                if g is not None:
                    te.wait_ge(ev.h, g)
                for dc in range(OC):
                    te.matmul(pa[:, 0:512],
                              p.w1slot[:, wslot2 + dc * 128: wslot2 + dc * 128 + 128],
                              p.xnb[:, tt * 4096 + dc * 512: tt * 4096 + (dc + 1) * 512],
                              start=(dc == 0), stop=(dc == OC - 1))
                for dc in range(OC):
                    ins = te.matmul(pg[:, 0:512],
                                    p.w1slot[:, wslot2 + 1024 + dc * 128: wslot2 + 1024 + dc * 128 + 128],
                                    p.xnb[:, tt * 4096 + dc * 512: tt * 4096 + (dc + 1) * 512],
                                    start=(dc == 0), stop=(dc == OC - 1))
                ins.then_inc(mm.h); mm.n += 1
            blk.tensor(t_h)
            hmm = mm.n
            if tt == 0:
                M2set(p, ("w1cons", jp), hmm)
            if not hasattr(p, "_p2_w1_free"):
                p._p2_w1_free = {}
            p._p2_w1_free[jp] = hmm

            def a_g(se, jp=jp, tt=tt, pg=pg, hmm=hmm):
                se.wait_ge(mm.h, hmm)
                g = M2get(p, ("gactfree", tt % 2))
                if g is not None:
                    se.wait_ge(ev.h, g)
                g3 = M2get(p, ("xsqfree",))
                if g3 is not None:
                    se.wait_ge(ev.h, g3)
                ac.inc(se.activation(p.xsq[:, 0:512], pg[:, 0:512], AF.Gelu,
                                     bias=p.b1col[:, jp + 22: jp + 23]))
            scl_(p, a_g)
            agm = ac.n

            def v_m(ve, jp=jp, tt=tt, pa=pa, agm=agm):
                ve.wait_ge(ac.h, agm)
                g = M2get(p, ("mbfree",))
                if g is not None:
                    ve.wait_ge(dp.h, g)
                ev.inc(ve.scalar_tensor_tensor(
                    p.mb2[:, tt * 512:(tt + 1) * 512],
                    pa[:, 0:512], p.b1col[:, jp:jp + 1],
                    p.xsq[:, 0:512], ALU.add, ALU.mult))
            vec_(p, v_m)
            M2set(p, ("hfree", tt % 2), ev.n)
            M2set(p, ("gactfree", tt % 2), ev.n)
            M2set(p, ("xsqfree",), ev.n)
            mb_ev[tt] = ev.n

        def g_mb(gp, jp=jp, mb_ev=dict(mb_ev)):
            gp.wait_ge(ev.h, mb_ev[3])
            gp.dma_start(out=p.gh[jp * 128:(jp + 1) * 128, :], in_=p.mb2[:, :]).then_inc(dp.h, 16)
            dp.n += 16
        blk.gpsimd(g_mb)
        ghst[jp] = dp.n
        M2set(p, ("mbfree",), dp.n)
    gh_all = dp.n

    def mchunk(slot, jc):
        if slot == 0:
            return p.xnb[:, jc * 512:(jc + 1) * 512]
        if jc < 10:
            return p.xnb[:, 11264 + jc * 512: 11264 + (jc + 1) * 512]
        return p.kbuf[:, (jc - 10) * 512:(jc - 9) * 512]

    ghv = p.gh.rearrange("(j p) t -> p j t", p=128)
    x3st = 0
    for tt in range(4):
        e0 = HALO + tt * 512
        slot = tt % 2

        def g_m(gp, tt=tt, slot=slot, gh_all=gh_all):
            gp.wait_ge(dp.h, gh_all)
            g = M2get(p, ("xnfree", slot))
            if g is not None:
                gp.wait_ge(mm.h, g)
            g2 = M2get(p, ("mcons", tt - 1 if tt >= 1 else 3))
            if g2 is not None:
                gp.wait_ge(mm.h, g2)
            ts = slice(tt * 512, (tt + 1) * 512)
            if slot == 0:
                gp.dma_start(out=sb3(p.xnb[:, 0:22 * 512], 512),
                             in_=ghv[:, :, ts]).then_inc(wx.h, 16)
                wx.n += 16
            else:
                gp.dma_start(out=sb3(p.xnb[:, 11264:11264 + 10 * 512], 512),
                             in_=ghv[:, 0:10, ts]).then_inc(wx.h, 16)
                wx.n += 16
                gp.dma_start(out=sb3(p.kbuf[:, 0:12 * 512], 512),
                             in_=ghv[:, 10:22, ts]).then_inc(wx.h, 16)
                wx.n += 16
        blk.gpsimd(g_m)
        m_rdy = wx.n

        def s_x2b(se, tt=tt, e0=e0, xn_rdy=xn_rdy):
            g = M2get(p, ("x2bfree",))
            if g is not None:
                se.wait_ge(ev.h, g)
            if tt == 0:
                se.wait_ge(ev.h, xn_rdy)
            g2 = M2get(p, ("x3st",))
            if g2 is not None:
                se.wait_ge(st.h, g2)
            se.dma_start(out=sb3(p.xres[:, 0:8 * 512], 512),
                         in_=fm(p.X2s, e0, 512)).then_inc(xq.h, 16)
            xq.n += 16
        blk.sync(s_x2b)
        x2b_rdy = xq.n

        for oco in range(OC):
            bank = p.ps4 if oco % 2 == 0 else p.ps5

            def t_o(te, tt=tt, slot=slot, oco=oco, bank=bank, m_rdy=m_rdy):
                te.wait_ge(wx.h, m_rdy)
                if tt == 0 and oco == 0:
                    te.wait_ge(w2s.h, w2_rdy)
                g = M2get(p, ("obfree", oco % 2))
                if g is not None:
                    te.wait_ge(ev.h, g)
                for jc in range(22):
                    ins = te.matmul(
                        bank[:, 0:512],
                        p.WSLOT[:, jc * 1024 + oco * 128: jc * 1024 + oco * 128 + 128],
                        mchunk(slot, jc),
                        start=(jc == 0), stop=(jc == 21))
                ins.then_inc(mm.h); mm.n += 1
            blk.tensor(t_o)
            omm = mm.n
            if oco == 0:
                M2set(p, ("mcons", tt), omm)

            def v_o(ve, tt=tt, oco=oco, bank=bank, omm=omm, x2b_rdy=x2b_rdy):
                ve.wait_ge(mm.h, omm)
                ve.wait_ge(xq.h, x2b_rdy)
                ev.inc(ve.scalar_tensor_tensor(
                    p.xres[:, oco * 512:(oco + 1) * 512],
                    bank[:, 0:512], p.lncol[:, 2 * OC + oco: 2 * OC + oco + 1],
                    p.xres[:, oco * 512:(oco + 1) * 512],
                    ALU.add, ALU.add))
            vec_(p, v_o)
            M2set(p, ("obfree", oco % 2), ev.n)
        M2set(p, ("xnfree", slot), mm.n)
        M2set(p, ("x2bfree",), ev.n)
        x3_ev = ev.n

        def s_x3(se, tt=tt, e0=e0, x3_ev=x3_ev):
            se.wait_ge(ev.h, x3_ev)
            se.dma_start(out=fm(p.Xs, e0, 512),
                         in_=sb3(p.xres[:, 0:8 * 512], 512)).then_inc(st.h, 16)
            st.n += 16
        blk.sync(s_x3)
        x3st = st.n
        M2set(p, ("x3st",), st.n)
    p._wslot_free = mm.n
    return x3st


def M2get(p, k):
    return getattr(p, "_M2", {}).get(k)


def M2set(p, k, v):
    if not hasattr(p, "_M2"):
        p._M2 = {}
    p._M2[k] = v


def emit_final(p):
    blk = p.block
    ev, st, xq, dp = (p.sems[k] for k in ["ev", "st", "xr", "dp"])
    for gidx in range(4):
        def s_l(se, gidx=gidx):
            g = getattr(p, "_fin_free", None)
            if g is not None:
                se.wait_ge(ev.h, g)
            g2 = getattr(p, "_fin_stp", None)
            if g2 is not None:
                se.wait_ge(st.h, g2)
            se.wait_ge(st.h, p._x6_st)
            se.dma_start(out=sb3(p.xres[:, 0:8 * 512], 512),
                         in_=fm(p.Xs, HALO + gidx * 512, 512)).then_inc(xq.h, 16)
            xq.n += 16
            se.dma_start(out=sb3(p.xnb.bitcast(F32)[:, 0:4096], 512),
                         in_=fm(p.x1s, gidx * 512, 512)).then_inc(xq.h, 16)
            xq.n += 16
        blk.sync(s_l)
        l_rdy = xq.n

        def v_add(ve, l_rdy=l_rdy):
            ve.wait_ge(xq.h, l_rdy)
            ev.inc(ve.tensor_add(p.xres[:, 0:4096], p.xres[:, 0:4096],
                                 p.xnb.bitcast(F32)[:, 0:4096]))
        vec_(p, v_add)
        emit_ln_fm(p, p.xres, None, waits=[(ev.h, ev.n)])
        app = p._a2_x1_done
        p._fin_free = app

        def s_out(se, gidx=gidx, app=app):
            se.wait_ge(ev.h, app)
            se.dma_start(out=fm(p.out_fm, gidx * 512, 512),
                         in_=sb3(p.xres[:, 0:8 * 512], 512)).then_inc(st.h, 16)
            st.n += 16
        blk.sync(s_out)
        p._fin_stp = st.n
    fin_st = st.n

    def g_end(gp, fin_st=fin_st):
        gp.wait_ge(st.h, fin_st)
    blk.gpsimd(g_end)


def kernel(**inputs):
    per_core = prep_inputs(inputs)
    p = build(mode="full")
    emit_stage_a(p)
    p.actx.close()
    alloc_stage_b(p)
    p._wslot_free = 0
    p._wbuf_free = p.sems["mm"].n
    p._p1_x2st_all = 0
    ready_st = p._a2_x1_store
    for i in range(DEPTH):
        halo_dp = emit_cc_x(p, i, ready_st)
        M = emit_p1(p, i, halo_dp)
        p._p1_x2st_all = p.sems["st"].n
        ready_st = emit_p2(p, i)
        p._wbuf_free = p.sems["mm"].n
    p._x6_st = ready_st
    emit_final(p)
    p.ctx.close()

    keys = ["x_fm", "wqT", "wkT", "wvT", "woT", "bq128", "bk128", "bo128",
            "bkrow", "bvrow", "sel", "invrow",
            "wqkv", "wo_b", "w1", "w2", "ln_g", "ln_b", "b2c", "b1c",
            "qsp", "ksp", "tab_cq", "tab_sq", "tab_ck", "tab_sk",
            "maskstd", "maskfirst", "selh"]
    in_maps = [{k: pc[k] for k in keys} for pc in per_core]
    res = run_bass_kernel_spmd(p.nc, in_maps, core_ids=list(range(NCORES)))
    outs = [res.results[c]["out_fm"] for c in range(NCORES)]
    full = np.concatenate([o.T for o in outs], axis=0)
    return np.ascontiguousarray(full.reshape(B, TSEQ, D))

